# version 02
# baseline (speedup 1.0000x reference)
import numpy as np
import ml_dtypes
from contextlib import ExitStack
import concourse.bass as bass
import concourse.mybir as mybir
from concourse.bass_utils import run_bass_kernel_spmd

F32 = mybir.dt.float32
BF16 = mybir.dt.bfloat16
I32 = mybir.dt.int32
AF = mybir.ActivationFunctionType
ALU = mybir.AluOpType
AXX = mybir.AxisListType.X

NCORES = 8
D = 1024
DCH = 8
TSEQ = 8192
NB = 2
JB = 8
CW = 130
BLK = 2 * CW
NCOL = JB * BLK
QW = JB * CW
ST = 416
NT = NCOL // ST
QT = 520
HS = 64
D1 = 2048
DH = 4096
NKC = TSEQ // 128
KC_LIM = None
P4DBG = ()
P2DBG = ()
SKIP_SV = False
N_LIM = None

_UID = [0]


def _u():
    _UID[0] += 1
    return f"_{_UID[0]}"


class K:
    ENGINES = ("sync", "tensor", "vector", "scalar", "gpsimd")

    def __init__(self, nc, stack):
        self.nc = nc
        self.root = stack
        self.scopes = [stack]
        self.ops = {e: [] for e in self.ENGINES}
        self.sems = {}
        self.uid = _u()
        self.t = {}

    def push_scope(self):
        s = ExitStack()
        self.scopes.append(s)
        return s

    def pop_scope(self):
        self.scopes.pop().close()

    def sb(self, name, shape, dtype):
        return self.scopes[-1].enter_context(self.nc.sbuf_tensor(name + self.uid, shape, dtype))

    def ps(self, name, shape, dtype=F32):
        return self.scopes[-1].enter_context(self.nc.psum_tensor(name + self.uid, shape, dtype))

    def _sem(self, name):
        if name not in self.sems:
            h = self.root.enter_context(self.nc.semaphore(name + self.uid))
            self.sems[name] = [h, 0]
        return self.sems[name]

    def val(self, name):
        return self._sem(name)[1]

    def _post(self, eng, fn, inc, inck):
        if inc is None:
            self.ops[eng].append(fn)
            return None
        s = self._sem(inc)
        s[1] += inck
        h = s[0]
        self.ops[eng].append(lambda e, fn=fn, h=h, k=inck: fn(e).then_inc(h, k))
        return s[1]

    def wait(self, eng, sem, v):
        if not v:
            return
        h = self._sem(sem)[0]
        assert v <= self._sem(sem)[1], f"wait beyond emitted incs: {sem} {v}"
        self.ops[eng].append(lambda e, h=h, v=v: e.wait_ge(h, v))

    def dma(self, q, out, in_, inc, **kw):
        return self._post(q, lambda e, o=out, i=in_, kw=kw: e.dma_start(o, i, **kw), inc, 16)

    def mm(self, out, lhsT, rhs, start, stop, inc=None):
        return self._post("tensor", lambda e, o=out, l=lhsT, r=rhs, s=start, p=stop:
                          e.matmul(o, l, r, start=s, stop=p), inc, 1)

    def tr(self, out, in_, iden, inc=None):
        return self._post("tensor", lambda e, o=out, i=in_, d=iden: e.transpose(o, i, d), inc, 1)

    def cp(self, eng, out, in_, inc=None):
        if eng == "scalar":
            return self._post(eng, lambda e, o=out, i=in_: e.copy(o, i), inc, 1)
        return self._post(eng, lambda e, o=out, i=in_: e.tensor_copy(o, i), inc, 1)

    def act(self, out, in_, func, bias=0.0, scale=1.0, inc=None):
        return self._post("scalar", lambda e, o=out, i=in_, f=func, b=bias, s=scale:
                          e.activation(o, i, f, bias=b, scale=s), inc, 1)

    def tt(self, eng, out, in0, in1, alu, inc=None):
        return self._post(eng, lambda e, o=out, a=in0, b=in1, p=alu:
                          e.tensor_tensor(o, a, b, p), inc, 1)

    def tsc(self, eng, out, in0, s1, s2, op0, op1=ALU.bypass, inc=None):
        return self._post(eng, lambda e, o=out, i=in0, s1=s1, s2=s2, p0=op0, p1=op1:
                          e.tensor_scalar(o, i, s1, s2, p0, p1), inc, 1)

    def stt(self, eng, out, in0, scalar, in1, op0, op1, inc=None):
        return self._post(eng, lambda e, o=out, a=in0, s=scalar, b=in1, p0=op0, p1=op1:
                          e.scalar_tensor_tensor(o, a, s, b, p0, p1), inc, 1)

    def red(self, eng, out, in_, alu, absval=False, inc=None):
        return self._post(eng, lambda e, o=out, i=in_, p=alu, a=absval:
                          e.tensor_reduce(o, i, AXX, p, apply_absolute_value=a), inc, 1)

    def recip(self, out, in_, inc=None):
        return self._post("vector", lambda e, o=out, i=in_: e.reciprocal(o, i), inc, 1)

    def memset(self, eng, ap, v, inc=None):
        return self._post(eng, lambda e, a=ap, v=v: e.memset(a, v), inc, 1)

    def nopi(self, eng, inc):
        return self._post(eng, lambda e: e.nop(), inc, 1)

    def barrier(self, tag):
        s = self._sem("bar_" + tag)
        h = s[0]
        s[1] += len(self.ENGINES)
        tgt = s[1]
        for eng in self.ENGINES:
            self.ops[eng].append(lambda e, h=h: e.nop().then_inc(h, 1))
        for eng in self.ENGINES:
            self.ops[eng].append(lambda e, h=h, t=tgt: e.wait_ge(h, t))

    def finalize(self):
        while len(self.scopes) > 1:
            self.pop_scope()
        with self.nc.Block() as block:
            for eng in self.ENGINES:
                getattr(block, eng)(lambda e, eng=eng: [f(e) for f in self.ops[eng]])



def _f32c(x):
    return np.ascontiguousarray(np.asarray(x), dtype=np.float32)


def host_prep(inputs):
    P = {}
    x = np.asarray(inputs["L0_attn_ids"])
    P["xT"] = _f32c(x.transpose(2, 0, 1))
    P["qkvT"] = _f32c(np.asarray(inputs["L0_attn_qkv_w"]).T)
    qb = _f32c(inputs["L0_attn_qkv_b"])
    P["bq"] = qb[0:64].reshape(64, 1).copy()
    P["bkv"] = qb[64:192].reshape(128, 1).copy()
    P["projT"] = _f32c(np.asarray(inputs["L0_attn_proj_w"]).T)
    P["proj_b"] = _f32c(inputs["L0_attn_proj_b"]).reshape(8, 128).T.copy()
    E = np.zeros((1024, 64), dtype=np.float32)
    E[np.arange(1024), np.arange(1024) % 64] = 1.0
    P["efold"] = E
    P["ones1"] = np.ones((1, 128), dtype=np.float32)
    P["onesn"] = np.full((128, 1), 1.0 / 1024.0, dtype=np.float32)
    P["iden"] = np.eye(128, dtype=np.float32)
    P["qTw"] = np.ascontiguousarray(np.asarray(inputs["L0_ffn_qweights"]).transpose(0, 2, 1))
    P["sTe"] = _f32c(np.repeat(np.asarray(inputs["L0_ffn_scales"]), 16, axis=2).transpose(0, 2, 1))
    P["biasA"] = np.ascontiguousarray(
        _f32c(inputs["L0_ffn_biases"]).reshape(6, 8, 128).transpose(2, 0, 1).reshape(128, 48))
    P["laT"] = _f32c(np.asarray(inputs["L0_ffn_lora_a"]).transpose(0, 2, 1))
    P["lbT"] = _f32c(np.asarray(inputs["L0_ffn_lora_b"]).transpose(0, 2, 1))
    P["lbT2"] = np.ascontiguousarray(P["lbT"].transpose(1, 0, 2).reshape(32, 6 * 1024))
    P["gamma"] = np.ascontiguousarray(
        _f32c(inputs["L0_ffn_ln_gamma"]).reshape(3, 8, 128).transpose(2, 0, 1).reshape(128, 24))
    P["beta"] = np.ascontiguousarray(
        _f32c(inputs["L0_ffn_ln_beta"]).reshape(3, 8, 128).transpose(2, 0, 1).reshape(128, 24))
    P["WuT"] = _f32c(np.asarray(inputs["L1_attn_Wu"]).T)
    P["WvT"] = _f32c(np.asarray(inputs["L1_attn_Wv"]).T)
    P["WoT"] = _f32c(np.asarray(inputs["L1_attn_Wo"]).T)
    P["bu"] = _f32c(inputs["L1_attn_bu"]).reshape(16, 128).T.copy()
    P["bv"] = _f32c(inputs["L1_attn_bv"]).reshape(16, 128).T.copy()
    P["bo"] = _f32c(inputs["L1_attn_bo"]).reshape(8, 128).T.copy()
    P["ze"] = _f32c(np.repeat(np.asarray(inputs["L1_attn_zero"])[:, 0], 256)).reshape(16, 128).T.copy()
    P["pe"] = _f32c(np.repeat(np.asarray(inputs["L1_attn_pos"])[:, 0], 256)).reshape(16, 128).T.copy()
    P["w1T"] = _f32c(np.asarray(inputs["L1_ffn_w1"]).T)
    P["w2T"] = _f32c(np.asarray(inputs["L1_ffn_w2"]).T)
    cw = _f32c(np.asarray(inputs["L1_ffn_conv_w"])[:, 0, :])
    P["cw"] = cw.reshape(32, 128, 3).transpose(1, 0, 2).copy()
    P["cb"] = _f32c(inputs["L1_ffn_conv_b"]).reshape(32, 128).T.copy()
    return P


def host_core_inputs(P, c):
    Q = {}
    xT = P["xT"]
    pos = np.empty((JB, CW), dtype=np.int64)
    for j in range(JB):
        pos[j] = 1024 * j + 128 * c + np.arange(CW) - 1
    posc = np.clip(pos, 0, TSEQ - 1)
    g = xT[:, :, posc.reshape(-1)].reshape(1024, 2, JB, CW)
    xin = np.zeros((1024, JB, CW, 2), dtype=np.float32)
    xin[:, :, :, 0] = g[:, 0]
    xin[:, :, :, 1] = g[:, 1]
    bad = (pos < 0) | (pos >= TSEQ)
    xin[:, bad, :] = 0.0
    Q["xin"] = np.ascontiguousarray(xin.reshape(1024, NCOL))
    m = np.arange(8)[:, None, None] * 128 + np.arange(128)[None, :, None]
    qq = 128 * c + np.arange(CW)[None, None, :] - 1
    Q["mask"] = np.ascontiguousarray((m <= qq).astype(np.float32).transpose(1, 0, 2).reshape(128, 8 * CW))
    cm = np.ones((1, NCOL), dtype=np.float32)
    for j in range(JB):
        for u in (0, CW - 1):
            for n in range(2):
                if pos[j, u] < 0 or pos[j, u] >= TSEQ:
                    cm[0, BLK * j + 2 * u + n] = 0.0
    Q["cmrow"] = cm
    return Q


def host_assemble(outs):
    y = np.empty((1024, 2, TSEQ), dtype=np.float32)
    for c in range(NCORES):
        o = np.asarray(outs[c]).reshape(1024, 2, JB, 128)
        for j in range(JB):
            y[:, :, 1024 * j + 128 * c:1024 * j + 128 * c + 128] = o[:, :, j, :]
    return np.ascontiguousarray(y.transpose(1, 2, 0))



def alloc_common(k, io, stream=True):
    if stream:
        k.t["X"] = k.sb("X", [128, DCH * NCOL], F32)
        k.t["XB"] = k.sb("XB", [128, DCH * NCOL], BF16)
    k.t["ones1_f"] = k.sb("ones1f", [1, 128], F32)
    k.t["onescol_f"] = k.sb("onescolf", [128, 1], F32)
    k.t["onesn_bf"] = k.sb("onesnb", [128, 1], BF16)
    k.t["iden_f"] = k.sb("idenf", [128, 128], F32)
    k.t["iden_b"] = k.sb("idenb", [128, 128], BF16)
    k.t["eps_ln"] = k.sb("epsln", [128, 1], F32)
    k.t["neg1"] = k.sb("neg1", [128, 1], F32)
    k.t["ones1_b"] = k.sb("ones1b", [1, 128], BF16)
    k.dma("sync", k.t["ones1_f"][:], io["ones1"], "cst")
    k.dma("gpsimd", k.t["onesn_bf"][:], io["onesn"], "cstg")
    k.dma("gpsimd", k.t["ones1_b"][:], io["ones1"], "cstg")
    k.dma("sync", k.t["iden_f"][:], io["iden"], "cst")
    k.dma("gpsimd", k.t["iden_b"][:], io["iden"], "cstg")
    k.memset("vector", k.t["eps_ln"][:], 1e-5, inc="cstv")
    k.memset("vector", k.t["neg1"][:], -1.0, inc="cstv")
    k.memset("vector", k.t["onescol_f"][:], 1.0, inc="cstv")
    for eng in ("tensor", "vector", "scalar"):
        k.wait(eng, "cst", k.val("cst"))
        k.wait(eng, "cstg", k.val("cstg"))
        k.wait(eng, "cstv", k.val("cstv"))


def Xc(k, ch, a, b, step=1):
    return k.t["X"][:, ch * NCOL + a: ch * NCOL + b: step] if step != 1 else \
        k.t["X"][:, ch * NCOL + a: ch * NCOL + b]


def XBc(k, ch, a, b):
    return k.t["XB"][:, ch * NCOL + a: ch * NCOL + b]


def emit_load_stream(k, io):
    X, XB = k.t["X"], k.t["XB"]
    src = io["xin"].rearrange("(c p) n -> p c n", p=128)
    k.dma("sync", X[:].rearrange("p (c n) -> p c n", c=DCH), src, "ldx")
    k.dma("gpsimd", XB[:].rearrange("p (c n) -> p c n", c=DCH), src, "ldxb")
    for eng in ("tensor", "vector", "scalar", "gpsimd"):
        k.wait(eng, "ldx", k.val("ldx"))
        k.wait(eng, "ldxb", k.val("ldxb"))


def emit_norm_stats(k, tag, src_bf, ncol, nt, tw, nch=DCH, rms=False, eps=1e-5):
    onesn = k.t["onesn_bf"]
    iden = k.t["iden_f"]
    ntr = (ncol + 127) // 128
    sq_scr = k.sb(f"sqs_{tag}", [128, 2 * tw], BF16)
    ps_s = k.ps(f"pss_{tag}", [1, 512], F32)
    ps_m = k.ps(f"psm_{tag}", [1, 512], F32) if not rms else None
    srow = k.sb(f"srow_{tag}", [1, ncol], F32)
    mrow = k.sb(f"mrow_{tag}", [1, ncol], F32) if not rms else None
    ps_tm = k.ps(f"pstm_{tag}", [128, 2 * ntr], F32)
    tmm = k.sb(f"tmm_{tag}", [128, 4 * ntr], F32)
    ps_rb = ps_s
    ps_rb2 = ps_m if not rms else ps_s
    arow = k.sb(f"arow_{tag}", [1, ncol], F32)
    brow = k.sb(f"brow_{tag}", [1, ncol], F32) if not rms else None

    qs0 = k.val("ns_qs")
    sv0 = k.val("ns_sqv")
    k.wait("scalar", "ns_qs", qs0)
    k.wait("vector", "ns_rb", k.val("ns_rb"))
    k.wait("vector", "ns_tm", k.val("ns_tm"))
    isq = 0
    for t in range(nt):
        a = t * tw
        for ch in range(nch):
            half = isq % 2
            scr = sq_scr[:, half * tw:(half + 1) * tw]
            if isq >= 2:
                k.wait("scalar", "ns_qs", qs0 + isq - 1)
            k.act(scr, src_bf(ch, a, a + tw), AF.Square, inc="ns_sqv")
            if not rms:
                k.mm(ps_m[:, 0:tw], onesn[:], src_bf(ch, a, a + tw),
                     start=(ch == 0), stop=(ch == nch - 1))
            k.wait("tensor", "ns_sqv", sv0 + isq + 1)
            k.mm(ps_s[:, 0:tw], onesn[:], scr, start=(ch == 0), stop=(ch == nch - 1),
                 inc="ns_qs")
            isq += 1
        k.wait("vector", "ns_qs", k.val("ns_qs"))
        k.cp("vector", srow[0:1, a:a + tw], ps_s[:, 0:tw], inc="ns_rows")
        if not rms:
            k.cp("vector", mrow[0:1, a:a + tw], ps_m[:, 0:tw], inc="ns_rows")
        k.wait("tensor", "ns_rows", k.val("ns_rows"))
    k.memset("vector", ps_tm[:], 0.0, inc="ns_tmz")
    k.wait("tensor", "ns_tmz", k.val("ns_tmz"))
    for kk in range(ntr):
        w = min(128, ncol - kk * 128)
        k.tr(ps_tm[0:w, kk:kk + 1], srow[0:1, kk * 128:kk * 128 + w], iden[0:1, 0:1],
             inc="ns_tm")
        if not rms:
            k.tr(ps_tm[0:w, ntr + kk:ntr + kk + 1], mrow[0:1, kk * 128:kk * 128 + w],
                 iden[0:1, 0:1], inc="ns_tm")
    k.wait("vector", "ns_tm", k.val("ns_tm"))
    k.cp("vector", tmm[:, 0:ntr], ps_tm[:, 0:ntr], inc="ns_tmc")
    if not rms:
        k.cp("vector", tmm[:, ntr:2 * ntr], ps_tm[:, ntr:2 * ntr], inc="ns_tmc")
        k.wait("vector", "ns_tmc", k.val("ns_tmc"))
        k.tt("vector", tmm[:, 3 * ntr:4 * ntr], tmm[:, ntr:2 * ntr], tmm[:, ntr:2 * ntr],
             ALU.mult, inc="ns_tmc")
        k.wait("vector", "ns_tmc", k.val("ns_tmc"))
        k.tt("vector", tmm[:, 0:ntr], tmm[:, 0:ntr], tmm[:, 3 * ntr:4 * ntr], ALU.subtract,
             inc="ns_tmc")
        k.wait("scalar", "ns_tmc", k.val("ns_tmc"))
        k.act(tmm[:, 2 * ntr:3 * ntr], tmm[:, 0:ntr], AF.Sqrt, bias=k.t["eps_ln"][:],
              inc="ns_sq2")
        k.wait("vector", "ns_sq2", k.val("ns_sq2"))
        k.recip(tmm[:, 2 * ntr:3 * ntr], tmm[:, 2 * ntr:3 * ntr], inc="ns_rc")
        k.wait("vector", "ns_rc", k.val("ns_rc"))
        k.stt("vector", tmm[:, ntr:2 * ntr], tmm[:, ntr:2 * ntr], -1.0,
              tmm[:, 2 * ntr:3 * ntr], ALU.mult, ALU.mult, inc="ns_rc")
    else:
        k.wait("scalar", "ns_tmc", k.val("ns_tmc"))
        k.act(tmm[:, 2 * ntr:3 * ntr], tmm[:, 0:ntr], AF.Sqrt, inc="ns_sq2")
        k.wait("vector", "ns_sq2", k.val("ns_sq2"))
        k.tsc("vector", tmm[:, 2 * ntr:3 * ntr], tmm[:, 2 * ntr:3 * ntr], eps, None, ALU.add,
              inc="ns_sq2e")
        k.wait("vector", "ns_sq2e", k.val("ns_sq2e"))
        k.recip(tmm[:, 2 * ntr:3 * ntr], tmm[:, 2 * ntr:3 * ntr], inc="ns_rc")
    k.wait("tensor", "ns_rc", k.val("ns_rc"))
    for kk0 in range(0, ntr, 4):
        kks = list(range(kk0, min(kk0 + 4, ntr)))
        for (pp, colblk, dst) in ((ps_rb, 2, arow),) + ((() if rms else ((ps_rb2, 1, brow),))):
            for kk in kks:
                w = min(128, ncol - kk * 128)
                k.tr(pp[0:1, (kk - kk0) * 128:(kk - kk0) * 128 + w],
                     tmm[0:w, colblk * ntr + kk:colblk * ntr + kk + 1], iden[0:w, 0:w],
                     inc="ns_rb")
            k.wait("scalar", "ns_rb", k.val("ns_rb"))
            w4 = min(512, ncol - kk0 * 128)
            k.cp("scalar", dst[0:1, kk0 * 128:kk0 * 128 + w4], pp[0:1, 0:w4],
                 inc="ns_rbe")
        k.wait("tensor", "ns_rbe", k.val("ns_rbe"))
    return arow, brow, "ns_rbe", k.val("ns_rbe")


def emit_postln(k, tag, pre):
    X, XB = k.t["X"], k.t["XB"]
    ones1 = k.t["ones1_f"]
    k.push_scope()
    for eng in ("scalar", "vector", "gpsimd", "tensor"):
        for (ps, pv) in pre:
            k.wait(eng, ps, pv)
    engs = ("scalar", "vector", "gpsimd")
    for t in range(NT):
        a = t * ST
        for ch in range(DCH):
            k.cp(engs[(t * DCH + ch) % 3], XBc(k, ch, a, a + ST), Xc(k, ch, a, a + ST),
                 inc="pl_cst")
    vc = k.val("pl_cst")
    for eng in ("scalar", "tensor"):
        k.wait(eng, "pl_cst", vc)
    arow, brow, ssem, sval = emit_norm_stats(
        k, tag, lambda ch, a, b: XBc(k, ch, a, b), NCOL, NT, ST)
    ps_a = k.ps(f"psa_{tag}", [128, ST], F32)
    ps_b = k.ps(f"psb_{tag}", [128, ST], F32)
    a_sb = k.sb(f"asb_{tag}", [128, ST], F32)
    b_sb = k.sb(f"bsb_{tag}", [128, ST], F32)
    k.wait("tensor", ssem, sval)
    for t in range(NT):
        a = t * ST
        if t > 0:
            k.wait("tensor", "pl_abe", k.val("pl_abe"))
            k.wait("scalar", "pl_nm", k.val("pl_nm"))
            k.wait("scalar", "pl_na", k.val("pl_na"))
        k.mm(ps_a[:], ones1[:], arow[0:1, a:a + ST], start=True, stop=True)
        k.mm(ps_b[:], ones1[:], brow[0:1, a:a + ST], start=True, stop=True, inc="pl_ab")
        k.wait("scalar", "pl_ab", k.val("pl_ab"))
        k.cp("scalar", a_sb[:], ps_a[:], inc="pl_abe")
        k.cp("scalar", b_sb[:], ps_b[:], inc="pl_abe")
        vab = k.val("pl_abe")
        k.wait("gpsimd", "pl_abe", vab)
        k.wait("vector", "pl_abe", vab)
        for ch in range(DCH):
            xa = Xc(k, ch, a, a + ST)
            xba = XBc(k, ch, a, a + ST)
            vm = k.tt("gpsimd", xa, xa, a_sb[:], ALU.mult, inc="pl_nm")
            k.wait("vector", "pl_nm", vm)
            va = k.tt("vector", xa, xa, b_sb[:], ALU.add, inc="pl_na")
            k.wait("scalar", "pl_na", va)
            k.cp("scalar", xba, xa, inc="pl_nc")
    k.wait("tensor", "pl_abe", k.val("pl_abe"))
    out = ("pl_nc", k.val("pl_nc"))
    k.pop_scope()
    return out



def emit_phase1(k, io, upto="pl"):
    X, XB = k.t["X"], k.t["XB"]
    iden_b = k.t["iden_b"]
    k.push_scope()
    qkvT = k.sb("qkvT", [128, DCH * 192], BF16)
    bq = k.sb("bq", [64, 1], F32)
    bkv = k.sb("bkv", [128, 1], F32)
    projT = k.sb("projT", [128, DCH * 1024], BF16)
    pbias = k.sb("pbias", [128, 8], F32)
    efold = k.sb("efold", [128, DCH * 64], BF16)
    masks = k.sb("masks", [128, 8 * CW], BF16)
    kv_sb = k.sb("kv_sb", [128, NB * TSEQ], BF16)
    vT_sb = k.sb("vT_sb", [128, NB * NKC * 64], BF16)
    q_sb = k.sb("q_sb", [64, NB * QW], BF16)
    outh = k.sb("outh", [64, NB * QW], BF16)
    stsb = k.sb("stsb", [128, 2 * QW], BF16)
    pweff = k.sb("pweff", [64, 1024], BF16)

    load_wT_bf(k, io["qkvT"], qkvT, DCH, 192, "p1w")
    load_wT_bf(k, io["projT"], projT, DCH, 1024, "p1w")
    load_wT_bf(k, io["efold"], efold, DCH, 64, "p1w")
    k.dma("gpsimd", masks[:], io["mask"], "p1w")
    k.dma("sync", bq[:], io["bq"], "p1b")
    k.dma("sync", bkv[:], io["bkv"], "p1b")
    k.dma("sync", pbias[:], io["proj_b"], "p1b")
    vw = k.val("p1w")
    vb = k.val("p1b")
    for eng in ("tensor", "vector", "scalar"):
        k.wait(eng, "p1w", vw)
        k.wait(eng, "p1b", vb)

    KVT = 512
    nkv = TSEQ // KVT
    xkv = k.sb("xkv", [128, 2 * DCH * KVT], BF16)
    k.push_scope()
    ps_kv = [k.ps(f"pskv{i}", [128, KVT], F32) for i in range(2)]
    ps_vt = k.ps("psvt", [128, 256], BF16)
    for n in range(NB):
        for tks in range(nkv):
            ti = n * nkv + tks
            buf = ti % 2
            pkv = ps_kv[ti % 2]
            s0 = tks * KVT
            src = io["xT"].rearrange("(c p) n t -> p c n t", p=128)[:, :, n, s0:s0 + KVT]
            dst = xkv[:, buf * DCH * KVT:(buf + 1) * DCH * KVT].rearrange(
                "p (c w) -> p c w", c=DCH)
            if ti >= 2:
                k.wait("gpsimd", "p1kvmm", k.val("p1kvmm") - 1)
            vld = k.dma("gpsimd", dst, src, f"p1kv{buf}")
            k.wait("tensor", f"p1kv{buf}", vld)
            if ti >= 2:
                k.wait("tensor", "p1kve", k.val("p1kve") - 1)
            for ch in range(DCH):
                k.mm(pkv[:], qkvT[:, ch * 192 + 64:(ch + 1) * 192],
                     xkv[:, buf * DCH * KVT + ch * KVT: buf * DCH * KVT + (ch + 1) * KVT],
                     start=(ch == 0), stop=(ch == DCH - 1),
                     inc=("p1kvmm" if ch == DCH - 1 else None))
            k.wait("vector", "p1kvmm", k.val("p1kvmm"))
            vev = k.tsc("vector", kv_sb[:, n * TSEQ + s0:n * TSEQ + s0 + KVT], pkv[:],
                        bkv[:], None, ALU.add, inc="p1kve")
            k.wait("tensor", "p1kve", vev)
            if ti >= 1:
                k.wait("tensor", "p1vte", k.val("p1vte"))
            for pz in range(KVT // 128):
                k.tr(ps_vt[:, pz * 64:(pz + 1) * 64],
                     kv_sb[64:128, n * TSEQ + s0 + pz * 128:n * TSEQ + s0 + (pz + 1) * 128],
                     iden_b[64:128, 64:128], inc=("p1vt" if pz == 3 else None))
            k.wait("scalar", "p1vt", k.val("p1vt"))
            base = n * NKC * 64 + (s0 // 128) * 64
            k.cp("scalar", vT_sb[:, base:base + 256], ps_vt[:], inc="p1vte")

    k.wait("tensor", "p1vte", k.val("p1vte"))
    k.wait("tensor", "p1kve", k.val("p1kve"))
    k.pop_scope()

    if upto == "a":
        k.t["dbg"] = (kv_sb, "p1vte", k.val("p1vte"))
        return None
    k.push_scope()
    ps_q = k.ps("psq", [64, QT], F32)
    for n in range(NB):
        for t in range(QW // QT):
            a = t * QT
            if n + t > 0:
                k.wait("tensor", "p1qe", k.val("p1qe"))
            for (lo, hi) in ((0, 512), (512, QT)):
                for ch in range(DCH):
                    k.mm(ps_q[:, lo:hi], qkvT[:, ch * 192:ch * 192 + 64],
                         XB[:, ch * NCOL + 2 * (a + lo) + n: ch * NCOL + 2 * (a + hi) + n - 1: 2],
                         start=(ch == 0), stop=(ch == DCH - 1),
                         inc=("p1q" if (ch == DCH - 1 and hi == QT) else None))
            k.wait("vector", "p1q", k.val("p1q"))
            k.tsc("vector", q_sb[:, n * QW + a:n * QW + a + QT], ps_q[:], bq[:], None,
                  ALU.add, inc="p1qe")
    k.wait("tensor", "p1qe", k.val("p1qe"))
    k.pop_scope()

    if upto == "b":
        k.t["dbg"] = (q_sb, "p1qe", k.val("p1qe"))
        return None
    k.push_scope()
    ps_o1 = k.ps("pso1", [64, 512], F32)
    ps_o2 = k.ps("pso2", [64, 512], F32)
    ps_o3 = k.ps("pso3", [64, QW - 1024], F32)
    ps_s1 = [k.ps(f"pss1{i}", [128, 512], F32) for i in range(2)]
    ps_s2 = [k.ps(f"pss2{i}", [128, 512], F32) for i in range(2)]
    ps_s3 = [k.ps("pss3", [128, QW - 1024], F32)] * 2
    OP = ((0, 512), (512, 1024), (1024, QW))

    def stp(dbuf, lo, hi):
        for i, (p0, p1) in enumerate(OP):
            if lo >= p0 and hi <= p1:
                return (ps_s1, ps_s2, ps_s3)[i][dbuf][:, lo - p0:hi - p0]
        raise AssertionError((lo, hi))

    nkc_l = KC_LIM or NKC
    for n in range(N_LIM or NB):
        if n == 1:
            k.wait("tensor", "p1oe", k.val("p1oe"))
        for kc in range(nkc_l):
            jmin = kc // 8
            mloc = kc % 8
            qstart = jmin * CW
            dbuf = kc % 2
            pieces = [(max(p0, qstart), p1) for (p0, p1) in OP if max(p0, qstart) < p1]
            idx = n * NKC + kc
            if idx >= 2:
                k.wait("tensor", "p1sv", k.val("p1sv") - 1)
            if idx >= 1:
                k.wait("tensor", "p1se3", k.val("p1se3"))
            for pj, (lo, hi) in enumerate(pieces):
                k.mm(stp(dbuf, lo, hi), kv_sb[0:64, n * TSEQ + kc * 128:n * TSEQ + (kc + 1) * 128],
                     q_sb[0:64, n * QW + lo:n * QW + hi], start=True, stop=True,
                     inc=("p1st" if pj == len(pieces) - 1 else None))
            vst = k.val("p1st")
            k.wait("vector", "p1st", vst)
            k.wait("scalar", "p1st", vst)
            for pi, (lo, hi) in enumerate(pieces):
                dhi = min(hi, qstart + CW)
                plo = max(lo, dhi)
                if lo < dhi:
                    k.tt("vector", stsb[:, dbuf * QW + lo:dbuf * QW + dhi],
                         stp(dbuf, lo, dhi),
                         masks[:, mloc * CW + (lo - qstart):mloc * CW + (dhi - qstart)],
                         ALU.mult, inc=("p1se3" if lo >= 1024 else "p1sev"))
                if plo < hi:
                    s3b = plo >= 1024
                    k.cp("vector", stsb[:, dbuf * QW + plo:dbuf * QW + hi],
                         stp(dbuf, plo, hi), inc=("p1se3" if s3b else "p1sev"))
            k.wait("tensor", "p1sev", k.val("p1sev"))
            k.wait("tensor", "p1ses", k.val("p1ses"))
            k.wait("tensor", "p1se3", k.val("p1se3"))
            subs = []
            for (lo, hi) in (() if SKIP_SV else pieces):
                for i, (p0, p1) in enumerate(OP):
                    l2, h2 = max(lo, p0), min(hi, p1)
                    if l2 < h2:
                        subs.append((i, l2, h2))
            for sj, (i, l2, h2) in enumerate(subs):
                po = (ps_o1, ps_o2, ps_o3)[i]
                p0 = OP[i][0]
                last = min(nkc_l - 1, NKC - 1 if p0 >= 512 else (4 * 8 - 1))
                k.mm(po[:, l2 - p0:h2 - p0],
                     vT_sb[:, n * NKC * 64 + kc * 64:n * NKC * 64 + (kc + 1) * 64],
                     stsb[:, dbuf * QW + l2:dbuf * QW + h2],
                     start=(kc == 0), stop=(kc == last),
                     inc=("p1sv" if sj == len(subs) - 1 else None))
            if SKIP_SV:
                k.nopi("tensor", "p1sv")
        k.wait("scalar", "p1sv", k.val("p1sv"))
        if not SKIP_SV:
            k.cp("scalar", outh[:, n * QW:n * QW + 512], ps_o1[:], inc="p1oe")
            k.cp("scalar", outh[:, n * QW + 512:n * QW + 1024], ps_o2[:], inc="p1oe")
            k.cp("scalar", outh[:, n * QW + 1024:(n + 1) * QW], ps_o3[:], inc="p1oe")
        else:
            k.nopi("scalar", "p1oe")
    k.wait("tensor", "p1oe", k.val("p1oe"))
    k.wait("tensor", "p1sev", k.val("p1sev"))
    k.wait("tensor", "p1ses", k.val("p1ses"))
    k.wait("tensor", "p1se3", k.val("p1se3"))
    k.pop_scope()

    if upto == "c1":
        k.t["dbg"] = (stsb, "p1se3", k.val("p1se3"))
        return None
    if upto == "c":
        k.t["dbg"] = (outh, "p1oe", k.val("p1oe"))
        return None
    k.push_scope()
    ps_pw = k.ps("pspw", [64, 1024], F32)
    for ch in range(DCH):
        for half in range(2):
            k.mm(ps_pw[:, half * 512:(half + 1) * 512], efold[:, ch * 64:(ch + 1) * 64],
                 projT[:, ch * 1024 + half * 512:ch * 1024 + (half + 1) * 512],
                 start=(ch == 0), stop=(ch == DCH - 1),
                 inc=("p1pw" if (ch == DCH - 1 and half == 1) else None))
    k.wait("vector", "p1pw", k.val("p1pw"))
    k.cp("vector", pweff[:], ps_pw[:], inc="p1pwe")
    k.wait("tensor", "p1pwe", k.val("p1pwe"))

    ps_pr = [k.ps(f"pspr{i}", [128, QT], F32) for i in range(2)]
    ti = 0
    for n in range(NB):
        for t in range(QW // QT):
            a = t * QT
            pp = ps_pr[ti % 2]
            for oc in range(DCH):
                for (lo, hi) in ((0, 512), (512, QT)):
                    k.mm(pp[:, lo:hi], pweff[:, oc * 128:(oc + 1) * 128],
                         outh[:, n * QW + a + lo:n * QW + a + hi], start=True, stop=True,
                         inc=("p1pr" if hi == QT else None))
                k.wait("vector", "p1pr", k.val("p1pr"))
                k.stt("vector",
                      X[:, oc * NCOL + 2 * a + n: oc * NCOL + 2 * (a + QT) + n - 1: 2],
                      pp[:, 0:QT], pbias[:, oc:oc + 1],
                      X[:, oc * NCOL + 2 * a + n: oc * NCOL + 2 * (a + QT) + n - 1: 2],
                      ALU.add, ALU.add, inc="p1pre")
                k.wait("tensor", "p1pre", k.val("p1pre"))
            ti += 1
    pre = ("p1pre", k.val("p1pre"))
    k.pop_scope()
    k.pop_scope()
    if upto == "e":
        return pre
    return emit_postln(k, "pl1", [pre])


def load_wT_bf(k, hbm, sbuf, nch, width, sem):
    src = hbm.rearrange("(c p) m -> p c m", p=128)
    dst = sbuf[:].rearrange("p (c w) -> p c w", c=nch)
    return k.dma("gpsimd", dst, src, sem)



def declare_io_phase1(nc, standalone):
    io = {}
    add = lambda n, s, d, kind: io.__setitem__(n, nc.dram_tensor(n + _u(), s, d, kind=kind).ap())
    add("xin", [1024, NCOL], F32, "ExternalInput")
    add("xT", [1024, NB, TSEQ], F32, "ExternalInput")
    add("qkvT", [1024, 192], F32, "ExternalInput")
    add("bq", [64, 1], F32, "ExternalInput")
    add("bkv", [128, 1], F32, "ExternalInput")
    add("projT", [1024, 1024], F32, "ExternalInput")
    add("proj_b", [128, 8], F32, "ExternalInput")
    add("efold", [1024, 64], F32, "ExternalInput")
    add("mask", [128, 8 * CW], F32, "ExternalInput")
    add("ones1", [1, 128], F32, "ExternalInput")
    add("onesn", [128, 1], F32, "ExternalInput")
    add("iden", [128, 128], F32, "ExternalInput")
    if standalone:
        add("xout", [128, DCH * NCOL], F32, "ExternalOutput")
    return io


def io_names(io):
    return {kk: ap.tensor.name for kk, ap in io.items()}


def build_phase1_standalone(upto="pl"):
    stack = ExitStack()
    nc = bass.Bass("TRN2", target_bir_lowering=False)
    io = declare_io_phase1(nc, True)
    dbg_shape = [None]
    with stack:
        k = K(nc, stack)
        alloc_common(k, io)
        emit_load_stream(k, io)
        r = emit_phase1(k, io, upto=upto)
        if r is None:
            tile, sem, val = k.t["dbg"]
            shp = [tile[:].shape[0], tile[:].shape[1]]
            dbg_shape[0] = shp
            dbgout = nc.dram_tensor("dbgout" + _u(), shp, tile[:].tensor.dtype,
                                    kind="ExternalOutput")
            io["dbgout"] = dbgout.ap()
            k.wait("sync", sem, val)
            k.dma("sync", io["dbgout"], tile[:], "out")
        elif upto == "e":
            sem, val = r
            k.wait("sync", sem, val)
            k.dma("sync", io["xout"], k.t["X"][:], "out")
        else:
            sem, val = r
            k.wait("sync", sem, val)
            k.dma("sync", io["xout"], k.t["X"][:], "out")
        k.wait("sync", "out", k.val("out"))
        k.finalize()
    return nc, io_names(io)


def phase1_in_map(P, Q):
    return {
        "xin": Q["xin"], "xT": P["xT"], "qkvT": P["qkvT"], "bq": P["bq"], "bkv": P["bkv"],
        "projT": P["projT"], "proj_b": P["proj_b"], "efold": P["efold"], "mask": Q["mask"],
        "ones1": P["ones1"], "onesn": P["onesn"], "iden": P["iden"],
    }





def emit_dequant(k, io, m, wbuf, qbufs, sbufs, waitsem=None, waitval=0):
    neg1 = k.t["neg1"]
    if waitsem:
        k.wait("gpsimd", waitsem, waitval)
    k.wait("sync", "dq_m", k.val("dq_m"))
    for hc in range(2 * DCH):
        ch, hh = hc // 2, hc % 2
        b = hc % 2
        if hc >= 2:
            k.wait("sync", "dq_m", k.val("dq_m") - 1)
        qsrc = io["qTw"][m].rearrange("(c p) w -> p c w", p=128)[:, ch, hh * 512:(hh + 1) * 512]
        ssrc = io["sTe"][m].rearrange("(c p) w -> p c w", p=128)[:, ch, hh * 512:(hh + 1) * 512]
        vqd = k.dma("sync", qbufs[b][:], qsrc, f"dq_q{b}")
        vsd = k.dma("sync", sbufs[b][:], ssrc, f"dq_s{b}")
        k.wait("scalar", f"dq_q{b}", vqd)
        if hc >= 1:
            k.wait("scalar", "dq_m", k.val("dq_m"))
        vc = k.act(k.t["qf_scr"][:], qbufs[b][:], AF.Copy,
                   scale=1.0 / 7.5, inc=f"dq_c{b}")
        k.wait("vector", f"dq_c{b}", vc)
        k.wait("vector", f"dq_s{b}", vsd)
        k.stt("vector", wbuf[:, ch * 1024 + hh * 512: ch * 1024 + (hh + 1) * 512],
              k.t["qf_scr"][:], -1.0, sbufs[b][:], ALU.add, ALU.mult, inc="dq_m")
    return "dq_m", k.val("dq_m")


def emit_phase2(k, io):
    X, XB = k.t["X"], k.t["XB"]
    k.push_scope()
    wbufs = [k.sb(f"w2b{i}", [128, DCH * 1024], BF16) for i in range(3)]
    k.t["qf_scr"] = k.sb("qfscr", [128, 512], F32)
    qbufs = [k.sb(f"qbuf{i}", [128, 512], I32) for i in range(2)]
    sbufs = [k.sb(f"sbuf{i}", [128, 512], F32) for i in range(2)]
    laT = k.sb("laT", [128, 6 * DCH * 32], BF16)
    lbT = k.sb("lbT", [32, 6 * 1024], BF16)
    biasA = k.sb("biasA", [128, 6 * 8], F32)
    gam = k.sb("gam", [128, 3 * 8], F32)
    bet = k.sb("bet", [128, 3 * 8], F32)
    Y1 = k.sb("Y1", [128, DCH * ST], F32)
    VB = k.sb("VB", [128, DCH * ST], BF16)
    G = VB
    t1s = k.sb("t1s", [32, 2 * ST], BF16)
    a_sb = k.sb("p2asb", [128, ST], F32)
    b_sb = k.sb("p2bsb", [128, ST], F32)

    k.dma("gpsimd", laT[:].rearrange("p (m c w) -> p (m c) w", m=6, w=32),
          io["laT"].rearrange("m (c p) w -> p (m c) w", p=128), "p2wg")
    k.dma("gpsimd", lbT[:].rearrange("p (m w) -> p m w", m=6), io["lbT"], "p2wg")
    k.dma("sync", biasA[:], io["biasA"], "p2ws")
    k.dma("sync", gam[:], io["gamma"], "p2ws")
    k.dma("sync", bet[:], io["beta"], "p2ws")
    for eng in ("tensor", "vector", "scalar", "gpsimd"):
        k.wait(eng, "p2wg", k.val("p2wg"))
        k.wait(eng, "p2ws", k.val("p2ws"))

    if "gam" in P2DBG:
        k.wait("sync", "p2ws", k.val("p2ws"))
        vdg = k.dma("sync", io["dbg_gam"], gam[:], "p2dbgc")
        k.wait("sync", "p2dbgc", vdg)
    emit_dequant(k, io, 0, wbufs[0], qbufs, sbufs)
    emit_dequant(k, io, 1, wbufs[1], qbufs, sbufs)

    for i in range(3):
        i0, i1 = 2 * i, 2 * i + 1
        wA, wB = wbufs[i0 % 3], wbufs[i1 % 3]
        k.wait("tensor", "dq_m", k.val("dq_m"))
        k.push_scope()
        ps_l = k.ps(f"psl{i}", [32, ST], F32)
        ps_m = [k.ps(f"psm{i}_{j}", [128, ST], F32) for j in range(2)]
        ps_a = k.ps(f"psa{i}", [128, ST], F32)
        ps_b = k.ps(f"psb{i}", [128, ST], F32)
        for t in range(NT):
            a = t * ST
            k.wait("tensor", "p2t1e", k.val("p2t1e"))
            for ch in range(DCH):
                k.mm(ps_l[:], laT[:, (i0 * DCH + ch) * 32:(i0 * DCH + ch + 1) * 32],
                     XBc(k, ch, a, a + ST), start=(ch == 0), stop=(ch == DCH - 1),
                     inc=("p2t1" if ch == DCH - 1 else None))
            k.wait("scalar", "p2t1", k.val("p2t1"))
            k.cp("scalar", t1s[:, 0:ST], ps_l[:], inc="p2t1e")
            k.wait("tensor", "p2t1e", k.val("p2t1e"))
            for oc in range(DCH):
                p = oc % 2
                pm = ps_m[p]
                k.wait("tensor", f"p2ae{p}", k.val(f"p2ae{p}"))
                k.wait("tensor", f"p2be{p}", k.val(f"p2be{p}"))
                for ch in range(DCH):
                    k.mm(pm[:], wA[:, ch * 1024 + oc * 128: ch * 1024 + (oc + 1) * 128],
                         XBc(k, ch, a, a + ST), start=(ch == 0), stop=False)
                k.mm(pm[:], lbT[:, i0 * 1024 + oc * 128: i0 * 1024 + (oc + 1) * 128],
                     t1s[:, 0:ST], start=False, stop=True, inc="p2a")
                k.wait("vector", "p2a", k.val("p2a"))
                vae = k.tsc("vector", Y1[:, oc * ST:(oc + 1) * ST], pm[:],
                            biasA[:, i0 * 8 + oc: i0 * 8 + oc + 1], None, ALU.add,
                            inc=f"p2ae{p}")
                k.wait("scalar", f"p2ae{p}", vae)
                k.cp("scalar", VB[:, oc * ST:(oc + 1) * ST], Y1[:, oc * ST:(oc + 1) * ST],
                     inc="p2vb")
            vvb = k.val("p2vb")
            k.wait("scalar", "p2vb", vvb)
            k.wait("tensor", "p2vb", vvb)
            if i == 0 and t == 0 and "y1" in P2DBG:
                k.wait("sync", "p2vb", vvb)
                k.dma("sync", io["dbg_y1"], Y1[:], "p2dbga")
                k.wait("gpsimd", "p2dbga", k.val("p2dbga"))
                k.wait("vector", "p2dbga", k.val("p2dbga"))
            k.push_scope()
            arow, brow, ssem, sval = emit_norm_stats(
                k, f"ln{i}_{t}", lambda ch, aa, bb: VB[:, ch * ST + aa: ch * ST + bb],
                ST, 1, ST)
            k.wait("tensor", ssem, sval)
            k.pop_scope()
            k.wait("tensor", "p2abe", k.val("p2abe"))
            k.mm(ps_a[:], k.t["ones1_f"][:], arow[0:1, :], start=True, stop=True)
            k.mm(ps_b[:], k.t["ones1_f"][:], brow[0:1, :], start=True, stop=True, inc="p2ab")
            k.wait("scalar", "p2ab", k.val("p2ab"))
            k.wait("scalar", "p2nm", k.val("p2nm"))
            k.wait("scalar", "p2na", k.val("p2na"))
            k.cp("scalar", a_sb[:], ps_a[:], inc="p2abe")
            k.cp("scalar", b_sb[:], ps_b[:], inc="p2abe")
            vabe = k.val("p2abe")
            k.wait("gpsimd", "p2abe", vabe)
            k.wait("vector", "p2abe", vabe)
            for ch in range(DCH):
                ya = Y1[:, ch * ST:(ch + 1) * ST]
                vm = k.tt("gpsimd", ya, ya, a_sb[:], ALU.mult, inc="p2nm")
                k.wait("vector", "p2nm", vm)
                va = k.tt("vector", ya, ya, b_sb[:], ALU.add, inc="p2na")
                k.wait("vector", "p2na", va)
                vg = k.tsc("vector", ya, ya, gam[:, i * 8 + ch: i * 8 + ch + 1],
                           bet[:, i * 8 + ch: i * 8 + ch + 1], ALU.mult, ALU.add, inc="p2ng")
                k.wait("scalar", "p2ng", vg)
                if i == 0 and t == 0 and "z" in P2DBG:
                    k.wait("sync", "p2ng", vg)
                    vdz = k.dma("sync", io["dbg_z"].rearrange("p (c w) -> p c w", c=DCH)[:, ch, :],
                                ya, f"p2dbgz{ch}")
                    k.wait("vector", f"p2dbgz{ch}", vdz)
                    k.wait("gpsimd", f"p2dbgz{ch}", vdz)
                k.act(G[:, ch * ST:(ch + 1) * ST], ya, AF.Gelu, inc="p2gel")
            vgel = k.val("p2gel")
            k.wait("tensor", "p2gel", vgel)
            if i == 0 and t == 0 and "g" in P2DBG:
                k.wait("sync", "p2gel", vgel)
                k.dma("sync", io["dbg_g"], G[:], "p2dbgb")
                k.wait("gpsimd", "p2dbgb", k.val("p2dbgb"))
                k.wait("tensor", "p2dbgb", k.val("p2dbgb"))
            k.wait("tensor", "p2t1e", k.val("p2t1e"))
            for ch in range(DCH):
                k.mm(ps_l[:], laT[:, (i1 * DCH + ch) * 32:(i1 * DCH + ch + 1) * 32],
                     G[:, ch * ST:(ch + 1) * ST], start=(ch == 0), stop=(ch == DCH - 1),
                     inc=("p2t1b" if ch == DCH - 1 else None))
            k.wait("scalar", "p2t1b", k.val("p2t1b"))
            k.cp("scalar", t1s[:, ST:2 * ST], ps_l[:], inc="p2t1e")
            k.wait("tensor", "p2t1e", k.val("p2t1e"))
            for oc in range(DCH):
                p = oc % 2
                pm = ps_m[p]
                k.wait("tensor", f"p2ae{p}", k.val(f"p2ae{p}"))
                k.wait("tensor", f"p2be{p}", k.val(f"p2be{p}"))
                for ch in range(DCH):
                    k.mm(pm[:], wB[:, ch * 1024 + oc * 128: ch * 1024 + (oc + 1) * 128],
                         G[:, ch * ST:(ch + 1) * ST], start=(ch == 0), stop=False)
                k.mm(pm[:], lbT[:, i1 * 1024 + oc * 128: i1 * 1024 + (oc + 1) * 128],
                     t1s[:, ST:2 * ST], start=False, stop=True, inc="p2b")
                k.wait("vector", "p2b", k.val("p2b"))
                vx = k.stt("vector", Xc(k, oc, a, a + ST), pm[:],
                           biasA[:, i1 * 8 + oc: i1 * 8 + oc + 1], Xc(k, oc, a, a + ST),
                           ALU.add, ALU.add, inc=f"p2be{p}")
                if i < 2:
                    k.wait("scalar", f"p2be{p}", vx)
                    k.act(XBc(k, oc, a, a + ST), Xc(k, oc, a, a + ST), AF.Copy, inc="p2xb")
        for s in ("p2ae0", "p2ae1", "p2be0", "p2be1", "p2t1e"):
            k.wait("tensor", s, k.val(s))
        k.pop_scope()
        if i < 2:
            ws = ("p2be0", k.val("p2be0"))
            emit_dequant(k, io, 2 * i + 2, wbufs[(2 * i + 2) % 3], qbufs, sbufs, ws[0], ws[1])
            emit_dequant(k, io, 2 * i + 3, wbufs[(2 * i + 3) % 3], qbufs, sbufs,
                         "p2be1", k.val("p2be1"))
            k.wait("tensor", "p2xb", k.val("p2xb"))
    for s in ("p2be0", "p2be1"):
        k.wait("vector", s, k.val(s))
        k.wait("gpsimd", s, k.val(s))
    k.pop_scope()
    k.push_scope()
    xrb = [k.sb(f"xrb{j}", [128, DCH * ST], F32) for j in range(2)]
    for s in ("p2be0", "p2be1"):
        k.wait("sync", s, k.val(s))
    for t in range(NT):
        a = t * ST
        b = t % 2
        if t >= 2:
            k.wait("sync", f"p2ra{b}", k.val(f"p2ra{b}"))
        vd = k.dma("sync", xrb[b][:].rearrange("p (c w) -> p c w", c=DCH),
                   io["x1ref"].rearrange("p (c n) -> p c n", c=DCH)[:, :, a:a + ST],
                   f"p2rx{b}")
        for ch in range(DCH):
            eng = ("vector", "gpsimd")[ch % 2]
            k.wait(eng, f"p2rx{b}", vd)
            k.tt(eng, Xc(k, ch, a, a + ST), Xc(k, ch, a, a + ST),
                 xrb[b][:, ch * ST:(ch + 1) * ST], ALU.add, inc=f"p2ra{b}")
    pre = [(f"p2ra0", k.val("p2ra0")), (f"p2ra1", k.val("p2ra1"))]
    k.pop_scope()
    return emit_postln(k, "pl2", pre)


def declare_io_phase2(nc, standalone):
    io = {}
    add = lambda n, s, d, kind: io.__setitem__(n, nc.dram_tensor(n + _u(), s, d, kind=kind).ap())
    if "y1" in P2DBG:
        add("dbg_y1", [128, DCH * ST], F32, "ExternalOutput")
    if "gam" in P2DBG:
        add("dbg_gam", [128, 24], F32, "ExternalOutput")
    if "z" in P2DBG:
        add("dbg_z", [128, DCH * ST], F32, "ExternalOutput")
    if "g" in P2DBG:
        add("dbg_g", [128, DCH * ST], BF16, "ExternalOutput")
    add("qTw", [6, 1024, 1024], I32, "ExternalInput")
    add("sTe", [6, 1024, 1024], F32, "ExternalInput")
    add("biasA", [128, 48], F32, "ExternalInput")
    add("laT", [6, 1024, 32], F32, "ExternalInput")
    add("lbT", [32, 6 * 1024], F32, "ExternalInput")
    add("gamma", [128, 24], F32, "ExternalInput")
    add("beta", [128, 24], F32, "ExternalInput")
    add("ones1", [1, 128], F32, "ExternalInput")
    add("onesn", [128, 1], F32, "ExternalInput")
    add("iden", [128, 128], F32, "ExternalInput")
    if standalone:
        add("xstate", [128, DCH * NCOL], F32, "ExternalInput")
        add("xout", [128, DCH * NCOL], F32, "ExternalOutput")
        io["x1ref"] = io["xstate"]
    return io


def build_phase2_standalone():
    stack = ExitStack()
    nc = bass.Bass("TRN2", target_bir_lowering=False)
    io = declare_io_phase2(nc, True)
    with stack:
        k = K(nc, stack)
        alloc_common(k, io)
        k.dma("sync", k.t["X"][:], io["xstate"], "ldx")
        k.dma("gpsimd", k.t["XB"][:], io["xstate"], "ldxb")
        for eng in ("tensor", "vector", "scalar", "gpsimd"):
            k.wait(eng, "ldx", k.val("ldx"))
            k.wait(eng, "ldxb", k.val("ldxb"))
        sem, val = emit_phase2(k, io)
        k.wait("sync", sem, val)
        k.dma("sync", io["xout"], k.t["X"][:], "out")
        k.wait("sync", "out", k.val("out"))
        k.finalize()
    return nc, io_names(io)


def phase2_in_map(P, xstate):
    return {
        "qTw": P["qTw"], "sTe": P["sTe"], "biasA": P["biasA"], "laT": P["laT"],
        "lbT": P["lbT2"], "gamma": P["gamma"], "beta": P["beta"],
        "ones1": P["ones1"], "onesn": P["onesn"], "iden": P["iden"], "xstate": xstate,
    }



def emit_phase3(k, io):
    X, XB = k.t["X"], k.t["XB"]
    ones1 = k.t["ones1_f"]
    k.push_scope()
    g_all = k.sb("g_all", [128, 16 * NCOL], BF16)
    bu = k.sb("bu3", [128, 16], F32)
    bv = k.sb("bv3", [128, 16], F32)
    bo = k.sb("bo3", [128, 8], F32)
    ze = k.sb("ze3", [128, 16], F32)
    pe = k.sb("pe3", [128, 16], F32)
    arow3 = k.sb("arow3k", [1, NCOL], BF16)
    a_sb = k.sb("a3sb", [128, ST], F32)
    xn = XB
    k.t["p3u"] = k.sb("p3u", [128, 2 * ST], F32)
    k.dma("sync", bu[:], io["bu"], "p3b")
    k.dma("sync", bv[:], io["bv"], "p3b")
    k.dma("sync", bo[:], io["bo"], "p3b")
    k.dma("sync", ze[:], io["ze"], "p3b")
    k.dma("sync", pe[:], io["pe"], "p3b")
    for eng in ("tensor", "vector", "scalar", "gpsimd"):
        k.wait(eng, "p3b", k.val("p3b"))

    k.push_scope()
    ar, _, ssem, sval = emit_norm_stats(k, "rms3", lambda ch, a, b: XBc(k, ch, a, b),
                                        NCOL, NT, ST, rms=True, eps=1e-8)
    k.wait("vector", ssem, sval)
    k.cp("vector", arow3[0:1, :], ar[0:1, :], inc="p3ar")
    k.wait("tensor", "p3ar", k.val("p3ar"))
    for s in ("ns_qs", "ns_tm", "ns_rb", "ns_rbe", "p3ar"):
        k.wait("gpsimd", s, k.val(s))
    k.pop_scope()

    ps_a = k.ps("p3psa", [128, ST], F32)
    ps_v = [k.ps(f"p3psv{j}", [128, ST], F32) for j in range(2)]

    onesb = k.t["ones1_b"]

    def emit_xn(t):
        a = t * ST
        k.wait("tensor", "p3ae", k.val("p3ae"))
        k.mm(ps_a[:], onesb[:], arow3[0:1, a:a + ST], start=True, stop=True, inc="p3abc")
        k.wait("scalar", "p3abc", k.val("p3abc"))
        k.wait("scalar", "p3xnw", k.val("p3xnw"))
        k.cp("scalar", a_sb[:], ps_a[:], inc="p3ae")
        k.wait("gpsimd", "p3ae", k.val("p3ae"))
        k.wait("gpsimd", "p3mm", k.val("p3mm"))
        for ch in range(DCH):
            k.tt("gpsimd", xn[:, ch * ST:(ch + 1) * ST], Xc(k, ch, a, a + ST), a_sb[:],
                 ALU.mult, inc="p3xnw")
        k.wait("tensor", "p3xnw", k.val("p3xnw"))

    k.push_scope()
    ww = k.sb("w3w", [128, 8 * 1024], BF16)
    for half in range(2):
        k.wait("gpsimd", "p3mx", k.val("p3mx"))
        k.wait("gpsimd", "p3mm", k.val("p3mm"))
        vw = k.dma("gpsimd", ww[:].rearrange("p (c w) -> p c w", c=DCH),
                   io["WvT"].rearrange("(c p) w -> p c w", p=128)[:, :, half * 1024:(half + 1) * 1024],
                   f"p3w{half}")
        k.wait("tensor", f"p3w{half}", vw)
        for t in range(NT):
            a = t * ST
            emit_xn(t)
            for oc in range(8):
                pv = ps_v[oc % 2]
                occ = half * 8 + oc
                k.wait("tensor", "p3ve", k.val("p3ve"))
                for ch in range(DCH):
                    k.mm(pv[:], ww[:, ch * 1024 + oc * 128: ch * 1024 + (oc + 1) * 128],
                         xn[:, ch * ST:(ch + 1) * ST], start=(ch == 0), stop=(ch == DCH - 1),
                         inc=("p3mm" if ch == DCH - 1 else None))
                k.wait("scalar", "p3mm", k.val("p3mm"))
                k.wait("scalar", f"p3mx{oc % 2}", k.val(f"p3mx{oc % 2}"))
                vsl = k.act(k.t["p3u"][:, (oc % 2) * ST:(oc % 2 + 1) * ST], pv[:], AF.Silu,
                            bias=bv[:, occ:occ + 1], inc="p3ve")
                vch = k.t["p3u"][:, (oc % 2) * ST:(oc % 2 + 1) * ST]
                gch = g_all[:, occ * NCOL + a: occ * NCOL + a + ST]
                k.wait("vector", "p3ve", vsl)
                vA = k.tsc("vector", gch, vch, ze[:, occ:occ + 1], None, ALU.mult,
                           inc=f"p3mx{oc % 2}")
                k.wait("vector", f"p3mx{oc % 2}", vA)
                k.stt("vector",
                      g_all[:, occ * NCOL + a + 1: occ * NCOL + a + ST: 2],
                      vch[:, 0:ST - 1:2], pe[:, occ:occ + 1],
                      g_all[:, occ * NCOL + a + 1: occ * NCOL + a + ST: 2],
                      ALU.mult, ALU.add, inc=f"p3mx{oc % 2}")
    for half in range(2):
        k.wait("gpsimd", "p3mx0", k.val("p3mx0"))
        k.wait("gpsimd", "p3mx1", k.val("p3mx1"))
        k.wait("gpsimd", "p3mm", k.val("p3mm"))
        vw = k.dma("gpsimd", ww[:].rearrange("p (c w) -> p c w", c=DCH),
                   io["WuT"].rearrange("(c p) w -> p c w", p=128)[:, :, half * 1024:(half + 1) * 1024],
                   f"p3wu{half}")
        k.wait("tensor", f"p3wu{half}", vw)
        for t in range(NT):
            a = t * ST
            emit_xn(t)
            for oc in range(8):
                pv = ps_v[oc % 2]
                occ = half * 8 + oc
                k.wait("tensor", "p3ve", k.val("p3ve"))
                for ch in range(DCH):
                    k.mm(pv[:], ww[:, ch * 1024 + oc * 128: ch * 1024 + (oc + 1) * 128],
                         xn[:, ch * ST:(ch + 1) * ST], start=(ch == 0), stop=(ch == DCH - 1),
                         inc=("p3mm" if ch == DCH - 1 else None))
                k.wait("scalar", "p3mm", k.val("p3mm"))
                k.wait("scalar", f"p3mx{oc % 2}", k.val(f"p3mx{oc % 2}"))
                vsl = k.act(k.t["p3u"][:, (oc % 2) * ST:(oc % 2 + 1) * ST], pv[:], AF.Silu,
                            bias=bu[:, occ:occ + 1], inc="p3ve")
                uch = k.t["p3u"][:, (oc % 2) * ST:(oc % 2 + 1) * ST]
                gch = g_all[:, occ * NCOL + a: occ * NCOL + a + ST]
                k.wait("vector", "p3ve", vsl)
                k.tt("vector", gch, gch, uch, ALU.mult, inc=f"p3mx{oc % 2}")
    k.wait("tensor", "p3mm", k.val("p3mm"))
    k.pop_scope()
    k.wait("gpsimd", "p3mx0", k.val("p3mx0"))
    k.wait("gpsimd", "p3mx1", k.val("p3mx1"))
    k.wait("gpsimd", "p3mm", k.val("p3mm"))
    wo = k.sb("w3o", [128, 16 * 1024], BF16)
    vwo = k.dma("gpsimd", wo[:].rearrange("p (c w) -> p c w", c=16),
                io["WoT"].rearrange("(c p) w -> p c w", p=128), "p3wo")
    k.wait("tensor", "p3wo", vwo)
    k.wait("tensor", "p3ve", k.val("p3ve"))
    tsb = k.t["p3u"][:, 0:ST]
    for t in range(NT):
        a = t * ST
        for oc in range(DCH):
            pv = ps_v[oc % 2]
            k.wait("tensor", "p3oe", k.val("p3oe"))
            for kc in range(16):
                k.mm(pv[:], wo[:, kc * 1024 + oc * 128: kc * 1024 + (oc + 1) * 128],
                     g_all[:, kc * NCOL + a: kc * NCOL + a + ST],
                     start=(kc == 0), stop=(kc == 15),
                     inc=("p3om" if kc == 15 else None))
            k.wait("vector", "p3om", k.val("p3om"))
            vt = k.stt("vector", tsb, pv[:], bo[:, oc:oc + 1], Xc(k, oc, a, a + ST),
                       ALU.add, ALU.add, inc="p3oe")
            k.wait("gpsimd", "p3oe", vt)
            k.tt("gpsimd", Xc(k, oc, a, a + ST), tsb, Xc(k, oc, a, a + ST), ALU.add,
                 inc="p3xo")
            k.wait("vector", "p3xo", k.val("p3xo"))
    for s in ("p3xo",):
        k.wait("tensor", s, k.val(s))
        k.wait("scalar", s, k.val(s))
    k.wait("tensor", "p3oe", k.val("p3oe"))
    pre = [("p3xo", k.val("p3xo"))]
    k.pop_scope()
    return emit_postln(k, "pl3", pre)


def declare_io_phase3(nc, standalone):
    io = {}
    add = lambda n, s, d, kind: io.__setitem__(n, nc.dram_tensor(n + _u(), s, d, kind=kind).ap())
    add("WuT", [1024, 2048], F32, "ExternalInput")
    add("WvT", [1024, 2048], F32, "ExternalInput")
    add("WoT", [2048, 1024], F32, "ExternalInput")
    add("bu", [128, 16], F32, "ExternalInput")
    add("bv", [128, 16], F32, "ExternalInput")
    add("bo", [128, 8], F32, "ExternalInput")
    add("ze", [128, 16], F32, "ExternalInput")
    add("pe", [128, 16], F32, "ExternalInput")
    add("ones1", [1, 128], F32, "ExternalInput")
    add("onesn", [128, 1], F32, "ExternalInput")
    add("iden", [128, 128], F32, "ExternalInput")
    if standalone:
        add("xstate", [128, DCH * NCOL], F32, "ExternalInput")
        add("xout", [128, DCH * NCOL], F32, "ExternalOutput")
    return io


def build_phase3_standalone():
    stack = ExitStack()
    nc = bass.Bass("TRN2", target_bir_lowering=False)
    io = declare_io_phase3(nc, True)
    with stack:
        k = K(nc, stack)
        alloc_common(k, io)
        k.dma("sync", k.t["X"][:], io["xstate"], "ldx")
        k.dma("gpsimd", k.t["XB"][:], io["xstate"], "ldxb")
        for eng in ("tensor", "vector", "scalar", "gpsimd"):
            k.wait(eng, "ldx", k.val("ldx"))
            k.wait(eng, "ldxb", k.val("ldxb"))
        sem, val = emit_phase3(k, io)
        k.wait("sync", sem, val)
        k.dma("sync", io["xout"], k.t["X"][:], "out")
        k.wait("sync", "out", k.val("out"))
        k.finalize()
    return nc, io_names(io)


def phase3_in_map(P, xstate):
    return {
        "WuT": P["WuT"], "WvT": P["WvT"], "WoT": P["WoT"], "bu": P["bu"], "bv": P["bv"],
        "bo": P["bo"], "ze": P["ze"], "pe": P["pe"],
        "ones1": P["ones1"], "onesn": P["onesn"], "iden": P["iden"], "xstate": xstate,
    }



def emit_wquant(k, io, which, wq, nch, width, kout):
    onescol = k.t["onescol_f"]
    ones1 = k.t["ones1_f"]
    iden = k.t["iden_f"]
    k.push_scope()
    wstr = [k.sb(f"wqs{which}{j}", [128, width], F32) for j in range(2)]
    parts = k.sb(f"wqp{which}", [128, nch], F32)
    scr = k.sb(f"wqc{which}", [128, width], F32)
    scri = k.sb(f"wqi{which}", [128, width], F32)[:]
    mgt = k.sb(f"wqg{which}", [128, 1], F32)
    k.memset("gpsimd", mgt[:], 12582912.0, inc="p4mg")
    k.wait("gpsimd", "p4mg", k.val("p4mg"))
    sm = k.sb(f"wqm{which}", [1, 16], F32)
    swpp = k.sb(f"wqw{which}", [128, 2], F32)
    ps_s = k.ps(f"wqps{which}", [128, 64], F32)
    src_h = io[which].rearrange("(c p) m -> p c m", p=128)
    for ch in range(nch):
        b = ch % 2
        if ch >= 2:
            k.wait("sync", "p4wa", k.val("p4wa") - 1)
        vd = k.dma("sync", wstr[b][:], src_h[:, ch, :], f"p4wd{b}")
        k.wait("vector", f"p4wd{b}", vd)
        k.red("vector", parts[:, ch:ch + 1], wstr[b][:], ALU.add, absval=True, inc="p4wa")
    k.wait("tensor", "p4wa", k.val("p4wa"))
    k.mm(ps_s[0:nch, 0:1], parts[:, 0:nch], onescol[:], start=True, stop=True, inc="p4wm")
    k.wait("vector", "p4wm", k.val("p4wm"))
    k.cp("vector", parts[0:nch, 0:1], ps_s[0:nch, 0:1], inc="p4wm2")
    k.wait("tensor", "p4wm2", k.val("p4wm2"))
    k.tr(ps_s[0:1, 4:4 + nch], parts[0:nch, 0:1], iden[0:nch, 0:nch], inc="p4wt")
    k.wait("vector", "p4wt", k.val("p4wt"))
    k.red("vector", sm[0:1, 0:1], ps_s[0:1, 4:4 + nch], ALU.add, inc="p4wr")
    k.wait("vector", "p4wr", k.val("p4wr"))
    k.tsc("vector", sm[0:1, 1:2], sm[0:1, 0:1], 1.0 / (nch * 128 * width), 1e-5,
          ALU.mult, ALU.max, inc="p4wr")
    k.wait("vector", "p4wr", k.val("p4wr"))
    k.recip(sm[0:1, 2:3], sm[0:1, 1:2], inc="p4wr")
    k.wait("tensor", "p4wr", k.val("p4wr"))
    k.mm(ps_s[:, 1:2], ones1[:], sm[0:1, 2:3], start=True, stop=True)
    k.mm(ps_s[:, 2:3], ones1[:], sm[0:1, 1:2], start=True, stop=True, inc="p4wb")
    k.wait("scalar", "p4wb", k.val("p4wb"))
    k.cp("scalar", swpp[:, 0:1], ps_s[:, 1:2], inc="p4we")
    k.cp("scalar", swpp[:, 1:2], ps_s[:, 2:3], inc="p4we")
    vwe = k.val("p4we")
    for eng in ("vector", "gpsimd", "sync"):
        k.wait(eng, "p4we", vwe)
    MAGIC = 12582912.0
    k.wait("sync", "p4wa", k.val("p4wa"))
    for ch in range(nch):
        b = ch % 2
        if ch >= 2:
            k.wait("sync", "p4q1", k.val("p4q1") - 1)
        vd = k.dma("sync", wstr[b][:], src_h[:, ch, :], f"p4we{b}")
        k.wait("vector", f"p4we{b}", vd)
        if ch >= 1:
            k.wait("vector", "p4q2", k.val("p4q2"))
        v1 = k.tsc("vector", scr[:], wstr[b][:], swpp[:, 0:1], MAGIC, ALU.mult, ALU.add,
                   inc="p4q1")
        k.wait("gpsimd", "p4q1", v1)
        if ch >= 1:
            k.wait("gpsimd", "p4q4", k.val("p4q4"))
        v2 = k.tt("gpsimd", scri, scr[:], mgt[:].broadcast_to([128, width]), ALU.subtract,
                  inc="p4q2")
        k.wait("vector", "p4q2", v2)
        v3 = k.tsc("vector", scri, scri, 1.0, -1.0, ALU.min, ALU.max, inc="p4q3")
        k.wait("vector", "p4q3", v3)
        k.tsc("vector", wq[:, ch * width:(ch + 1) * width], scri, swpp[:, 1:2], None,
              ALU.mult, inc="p4q4")
    for eng in ("sync", "gpsimd", "scalar", "tensor"):
        k.wait(eng, "p4q4", k.val("p4q4"))
    k.pop_scope()


def emit_phase4(k, io):
    MAGIC = 12582912.0
    ones1 = k.t["ones1_f"]
    iden = k.t["iden_f"]
    idenb = k.t["iden_b"]
    k.push_scope()
    w1q = k.sb("w1q", [128, DCH * DH], BF16)
    w2q = k.sb("w2q", [128, 32 * 1024], BF16)
    kq1 = k.sb("p4kq1", [128, 2], F32)
    k.memset("vector", kq1[:], 1.0 / 127.0, inc="p4kq")
    k.wait("vector", "p4kq", k.val("p4kq"))
    emit_wquant(k, io, "w1T", w1q, DCH, DH, None)
    emit_wquant(k, io, "w2T", w2q, 32, 1024, None)

    if "w1q" in P4DBG:
        k.wait("sync", "p4q4", k.val("p4q4"))
        k.dma("sync", io["dbg_w1q"], w1q[:], "p4dbga")
        k.wait("gpsimd", "p4dbga", k.val("p4dbga"))
    cw = k.sb("cw4", [128, 32 * 3], F32)
    cb4 = k.sb("cb4", [128, 32], F32)
    k.dma("sync", cw[:], io["cw"].rearrange("p c t -> p (c t)"), "p4b")
    k.dma("sync", cb4[:], io["cb"], "p4b")
    for eng in ("tensor", "vector", "scalar", "gpsimd"):
        k.wait(eng, "p4b", k.val("p4b"))

    TW = BLK
    OW = 256
    QW4 = 8 * OW
    xc = k.sb("p4x", [128, DCH * TW], F32)
    cmr = k.sb("p4cm", [1, TW], F32)
    xq = k.sb("p4xq", [128, DCH * TW], BF16)
    hbuf = k.sb("p4h", [128, 16 * TW], BF16)
    tA = k.sb("p4tA", [128, QW4], F32)
    tB = k.sb("p4tB", [128, QW4], F32)
    cbuf = k.sb("p4c", [128, 32 * OW], BF16)
    y4n = k.sb("p4yn", [128, DCH * OW], F32)
    rowq = k.sb("p4rq", [1, 2 * TW], F32)
    rowq2 = k.sb("p4rq2", [1, 2 * OW], F32)
    cmt = k.sb("p4cmt", [128, 8], F32)
    rb1 = k.sb("p4rb1", [128, 2 * TW], F32)
    rb2 = k.sb("p4rb2", [128, 2 * OW], F32)
    rb3 = k.sb("p4rb3", [128, 2 * OW], F32)
    ps_mm = [k.ps(f"p4pm{j}", [128, TW], F32) for j in range(2)]
    ps_t = k.ps("p4pt", [128, 1024], BF16)
    ps_bc = [k.ps(f"p4pb{j}", [128, TW], F32) for j in range(2)]
    k.memset("vector", cmt[:], 0.0)

    for t in range(JB):
        a = t * TW
        k.wait("sync", "p4y", k.val("p4y"))
        k.wait("sync", "p4qm", k.val("p4qm"))
        k.wait("sync", "p4cst", k.val("p4cst"))
        vx = k.dma("sync", xc[:].rearrange("p (c w) -> p c w", c=DCH),
                   io["xstate"].rearrange("p (c n) -> p c n", c=DCH)[:, :, a:a + TW], "p4xd")
        vcm = k.dma("sync", cmr[:], io["cmrow"][0:1, a:a + TW], "p4cmd")
        engs = ("scalar", "vector", "gpsimd")
        for eng in engs:
            k.wait(eng, "p4xd", vx)
            k.wait(eng, "p4hm", k.val("p4hm"))
            k.wait(eng, "ns_qs", k.val("ns_qs"))
        for ch in range(DCH):
            k.cp(engs[ch % 3], xq[:, ch * TW:(ch + 1) * TW], xc[:, ch * TW:(ch + 1) * TW],
                 inc="p4cst")
        k.wait("tensor", "p4cst", k.val("p4cst"))
        pieces = []
        off, pi = 0, 0
        while off < TW:
            w = min(128, TW - off)
            pieces.append((off, w, pi))
            k.wait("tensor", "p4red", k.val("p4red"))
            for ch in range(DCH):
                k.tr(ps_t[0:w, ch * 128:(ch + 1) * 128],
                     xq[:, ch * TW + off: ch * TW + off + w], idenb[:],
                     inc=("p4tr" if ch == DCH - 1 else None))
            k.wait("vector", "p4tr", k.val("p4tr"))
            k.red("vector", cmt[0:w, pi:pi + 1], ps_t[0:w, :], ALU.max, absval=True,
                  inc="p4red")
            off += w
            pi += 1
        npiece = pi
        k.wait("vector", "p4red", k.val("p4red"))
        k.tsc("vector", cmt[:, 4:4 + npiece], cmt[:, 0:npiece], 1e-5, None, ALU.max,
              inc="p4sc")
        k.wait("vector", "p4sc", k.val("p4sc"))
        k.recip(cmt[:, 0:npiece], cmt[:, 4:4 + npiece], inc="p4sc")
        k.wait("vector", "p4sc", k.val("p4sc"))
        k.tsc("vector", cmt[:, 0:npiece], cmt[:, 0:npiece], 127.0, None, ALU.mult,
              inc="p4sc")
        k.tsc("vector", cmt[:, 4:4 + npiece], cmt[:, 4:4 + npiece], kq1[:, 0:1], None,
              ALU.mult, inc="p4sc")
        k.wait("tensor", "p4sc", k.val("p4sc"))
        k.wait("tensor", "p4rbe", k.val("p4rbe"))
        for (off, w, pi2) in pieces:
            k.tr(ps_bc[0][0:1, off:off + w], cmt[0:w, pi2:pi2 + 1], iden[0:w, 0:w])
            k.tr(ps_bc[1][0:1, off:off + w], cmt[0:w, 4 + pi2:5 + pi2], iden[0:w, 0:w],
                 inc=("p4trr" if off + w >= TW else None))
        k.wait("scalar", "p4trr", k.val("p4trr"))
        k.wait("scalar", "p4bc", k.val("p4bc"))
        k.cp("scalar", rowq[0:1, 0:TW], ps_bc[0][0:1, 0:TW], inc="p4row")
        k.cp("scalar", rowq[0:1, TW:2 * TW], ps_bc[1][0:1, 0:TW], inc="p4row")
        k.wait("vector", "p4row", k.val("p4row"))
        k.wait("vector", "p4cmd", vcm)
        k.tt("vector", rowq[0:1, 0:TW], rowq[0:1, 0:TW], cmr[0:1, 0:TW], ALU.mult,
             inc="p4rowm")
        k.wait("tensor", "p4rowm", k.val("p4rowm"))
        k.mm(ps_bc[0][:], ones1[:], rowq[0:1, 0:TW], start=True, stop=True)
        k.mm(ps_bc[1][:], ones1[:], rowq[0:1, TW:2 * TW], start=True, stop=True, inc="p4bc")
        k.wait("scalar", "p4bc", k.val("p4bc"))
        k.wait("scalar", "p4qm", k.val("p4qm"))
        k.wait("scalar", "p4he", k.val("p4he"))
        k.cp("scalar", rb1[:, 0:TW], ps_bc[0][:], inc="p4rbe")
        k.cp("scalar", rb1[:, TW:2 * TW], ps_bc[1][:], inc="p4rbe")
        vrbe = k.val("p4rbe")
        k.wait("gpsimd", "p4rbe", vrbe)
        k.wait("vector", "p4rbe", vrbe)
        if t == 0 and "rb1" in P4DBG:
            k.wait("sync", "p4rbe", vrbe)
            k.dma("sync", io["dbg_rb1"], rb1[:], "p4dbgg")
            k.wait("gpsimd", "p4dbgg", k.val("p4dbgg"))
        for ch in range(DCH):
            s = ch % 2
            sl = tA[:, s * TW:(s + 1) * TW]
            k.wait("gpsimd", f"p4qa{s}", k.val(f"p4qa{s}"))
            if ch < 2:
                k.wait("gpsimd", "p4cv", k.val("p4cv"))
                k.wait("gpsimd", "p4qm2", k.val("p4qm2"))
            vq1 = k.tt("gpsimd", sl, xc[:, ch * TW:(ch + 1) * TW], rb1[:, 0:TW], ALU.mult,
                       inc="p4qm")
            k.wait("vector", "p4qm", vq1)
            vq2 = k.tsc("vector", sl, sl, MAGIC, None, ALU.add, inc=f"p4qa{s}")
            k.wait("vector", f"p4qa{s}", vq2)
            k.tsc("vector", xq[:, ch * TW:(ch + 1) * TW], sl, MAGIC, None, ALU.subtract,
                  inc=f"p4qa{s}")
        for s in range(2):
            k.wait("tensor", f"p4qa{s}", k.val(f"p4qa{s}"))
        if t == 0 and "xq" in P4DBG:
            k.wait("sync", "p4qa0", k.val("p4qa0"))
            k.wait("sync", "p4qa1", k.val("p4qa1"))
            k.dma("sync", io["dbg_xq"], xq[:], "p4dbgb")
            k.wait("gpsimd", "p4dbgb", k.val("p4dbgb"))
        k.wait("tensor", "p4oe", k.val("p4oe"))
        for hh in range(2):
            for hc16 in range(16):
                hc = hh * 16 + hc16
                pm = ps_mm[hc16 % 2]
                if hc16 >= 2 or hh > 0 or t > 0:
                    k.wait("tensor", "p4he", k.val("p4he") - 1)
                for ch in range(DCH):
                    k.mm(pm[:], w1q[:, ch * DH + hc * 128: ch * DH + (hc + 1) * 128],
                         xq[:, ch * TW:(ch + 1) * TW], start=(ch == 0), stop=(ch == DCH - 1),
                         inc=("p4hm" if ch == DCH - 1 else None))
                k.wait("vector", "p4hm", k.val("p4hm"))
                if hc16 < 2:
                    k.wait("vector", "p4cv", k.val("p4cv"))
                k.tt("vector", hbuf[:, hc16 * TW:(hc16 + 1) * TW], pm[:],
                     rb1[:, TW:2 * TW], ALU.mult, inc="p4he")
            vhe = k.val("p4he")
            if t == 0 and hh == 0 and "h" in P4DBG:
                k.wait("sync", "p4he", vhe)
                k.dma("sync", io["dbg_h"], hbuf[:], "p4dbgc")
                k.wait("gpsimd", "p4dbgc", k.val("p4dbgc"))
                k.wait("vector", "p4dbgc", k.val("p4dbgc"))
            for qq in range(2):
                q0 = qq * 8
                h3 = hbuf[:].rearrange("p (c w) -> p c w", c=16)[:, q0:q0 + 8, :]
                cwr = cw[:].rearrange("p (c t) -> p c t", c=32)
                cw3 = lambda tap: cwr[:, hh * 16 + q0:hh * 16 + q0 + 8, tap:tap + 1] \
                    .broadcast_to([128, 8, OW])
                cb3 = cb4[:, hh * 16 + q0:hh * 16 + q0 + 8].unsqueeze(2) \
                    .broadcast_to([128, 8, OW])
                tA3 = tA[:].rearrange("p (c w) -> p c w", c=8)
                tB3 = tB[:].rearrange("p (c w) -> p c w", c=8)
                for eng in ("vector", "gpsimd"):
                    k.wait(eng, "p4he", vhe)
                    k.wait(eng, "p4cq", k.val("p4cq"))
                    k.wait(eng, "p4qa0", k.val("p4qa0"))
                    k.wait(eng, "p4qa1", k.val("p4qa1"))
                    k.wait(eng, "p4yb", k.val("p4yb"))
                v1 = k.tt("vector", tA3, h3[:, :, 0:OW], cw3(0), ALU.mult, inc="p4cv")
                v2 = k.tt("gpsimd", tB3, h3[:, :, 2:2 + OW], cw3(1), ALU.mult, inc="p4cv")
                k.wait("vector", "p4cv", v2)
                v3 = k.tt("vector", tA3, tA3, tB3, ALU.add, inc="p4cv")
                k.wait("gpsimd", "p4cv", v3)
                v4 = k.tt("gpsimd", tB3, h3[:, :, 4:4 + OW], cw3(2), ALU.mult, inc="p4cv")
                k.wait("vector", "p4cv", v4)
                v5 = k.tt("vector", tA3, tA3, tB3, ALU.add, inc="p4cv")
                k.wait("gpsimd", "p4cv", v5)
                v6 = k.tt("gpsimd", tA3, tA3, cb3, ALU.add, inc="p4cv")
                k.wait("scalar", "p4cv", v6)
                k.wait("scalar", "p4hqw", k.val("p4hqw"))
                k.act(cbuf[:, (hh * 16 + q0) * OW:(hh * 16 + q0 + 8) * OW], tA[:], AF.Silu,
                      inc="p4cq")
        vcq = k.val("p4cq")
        if t == 0 and "c" in P4DBG:
            k.wait("sync", "p4cq", vcq)
            k.dma("sync", io["dbg_c"], cbuf[:], "p4dbgd")
            k.wait("gpsimd", "p4dbgd", k.val("p4dbgd"))
            k.wait("vector", "p4dbgd", k.val("p4dbgd"))
        k.wait("tensor", "p4cq", vcq)
        for cp2 in range(2):
            k.wait("vector", "p4sc", k.val("p4sc"))
            for qq in range(4):
                k.wait("tensor", "p4red", k.val("p4red"))
                for ci in range(8):
                    ch = 8 * qq + ci
                    k.tr(ps_t[:, ci * 128:(ci + 1) * 128],
                         cbuf[:, ch * OW + cp2 * 128: ch * OW + cp2 * 128 + 128], idenb[:],
                         inc=("p4tr" if ci == 7 else None))
                k.wait("vector", "p4tr", k.val("p4tr"))
                k.red("vector", cmt[:, qq:qq + 1], ps_t[:], ALU.max, absval=True,
                      inc="p4red")
            k.wait("vector", "p4red", k.val("p4red"))
            k.tt("vector", cmt[:, 0:1], cmt[:, 0:1], cmt[:, 1:2], ALU.max, inc="p4sc")
            k.tt("vector", cmt[:, 2:3], cmt[:, 2:3], cmt[:, 3:4], ALU.max, inc="p4sc")
            k.wait("vector", "p4sc", k.val("p4sc"))
            k.tt("vector", cmt[:, 4 + cp2:5 + cp2], cmt[:, 0:1], cmt[:, 2:3], ALU.max,
                 inc="p4sc")
        k.wait("vector", "p4sc", k.val("p4sc"))
        k.tsc("vector", cmt[:, 6:8], cmt[:, 4:6], 1e-5, None, ALU.max, inc="p4sc")
        k.wait("vector", "p4sc", k.val("p4sc"))
        k.recip(cmt[:, 4:6], cmt[:, 6:8], inc="p4sc")
        k.wait("vector", "p4sc", k.val("p4sc"))
        k.tsc("vector", cmt[:, 4:6], cmt[:, 4:6], 127.0, None, ALU.mult, inc="p4sc")
        k.tsc("vector", cmt[:, 6:8], cmt[:, 6:8], kq1[:, 1:2], None, ALU.mult, inc="p4sc")
        k.wait("tensor", "p4sc", k.val("p4sc"))
        k.wait("tensor", "p4rbe2", k.val("p4rbe2"))
        for cp2 in range(2):
            k.tr(ps_bc[0][0:1, cp2 * 128:(cp2 + 1) * 128], cmt[:, 4 + cp2:5 + cp2], iden[:])
            k.tr(ps_bc[1][0:1, cp2 * 128:(cp2 + 1) * 128], cmt[:, 6 + cp2:7 + cp2], iden[:],
                 inc=("p4trr2" if cp2 == 1 else None))
        k.wait("scalar", "p4trr2", k.val("p4trr2"))
        k.wait("scalar", "p4bc2", k.val("p4bc2"))
        k.cp("scalar", rowq2[0:1, 0:OW], ps_bc[0][0:1, 0:OW], inc="p4row2")
        k.cp("scalar", rowq2[0:1, OW:2 * OW], ps_bc[1][0:1, 0:OW], inc="p4row2")
        k.wait("tensor", "p4row2", k.val("p4row2"))
        k.mm(ps_bc[0][:, 0:OW], ones1[:], rowq2[0:1, 0:OW], start=True, stop=True)
        k.mm(ps_bc[1][:, 0:OW], ones1[:], rowq2[0:1, OW:2 * OW], start=True, stop=True,
             inc="p4bc2")
        k.wait("scalar", "p4bc2", k.val("p4bc2"))
        k.wait("scalar", "p4qm2", k.val("p4qm2"))
        k.wait("scalar", "p4oe", k.val("p4oe"))
        k.cp("scalar", rb2[:, 0:OW], ps_bc[0][:, 0:OW], inc="p4rbe2")
        k.cp("scalar", rb2[:, OW:2 * OW], ps_bc[1][:, 0:OW], inc="p4rbe2")
        vrbe2 = k.val("p4rbe2")
        k.wait("gpsimd", "p4rbe2", vrbe2)
        k.wait("vector", "p4rbe2", vrbe2)
        for qq in range(4):
            q0 = qq * 8
            r4b = rb2[:, 0:OW].unsqueeze(1).broadcast_to([128, 8, OW])
            c3 = cbuf[:].rearrange("p (c w) -> p c w", c=32)[:, q0:q0 + 8, :]
            tA3 = tA[:].rearrange("p (c w) -> p c w", c=8)
            k.wait("gpsimd", "p4cq", vcq)
            k.wait("gpsimd", "p4hqw", k.val("p4hqw"))
            vq1 = k.tt("gpsimd", tA3, c3, r4b, ALU.mult, inc="p4qm2")
            k.wait("vector", "p4qm2", vq1)
            vq2 = k.tsc("vector", tA[:], tA[:], MAGIC, None, ALU.add, inc="p4hqw")
            k.wait("vector", "p4hqw", vq2)
            k.tsc("vector", c3, tA3, MAGIC, None, ALU.subtract, inc="p4hqw")
        k.wait("tensor", "p4hqw", k.val("p4hqw"))
        if t == 0 and "hq" in P4DBG:
            k.wait("sync", "p4hqw", k.val("p4hqw"))
            k.dma("sync", io["dbg_hq"], cbuf[:], "p4dbge")
            k.wait("gpsimd", "p4dbge", k.val("p4dbge"))
            k.wait("tensor", "p4dbge", k.val("p4dbge"))
        for oc in range(DCH):
            pm = ps_mm[oc % 2]
            if oc >= 2:
                k.wait("tensor", "p4oe", k.val("p4oe") - 1)
            for hc in range(32):
                k.mm(pm[:, 0:OW], w2q[:, hc * 1024 + oc * 128: hc * 1024 + (oc + 1) * 128],
                     cbuf[:, hc * OW:(hc + 1) * OW], start=(hc == 0), stop=(hc == 31),
                     inc=("p4om" if hc == 31 else None))
            k.wait("vector", "p4om", k.val("p4om"))
            k.wait("vector", "p4yb", k.val("p4yb"))
            v1 = k.tt("vector", tB[:, 0:OW], pm[:, 0:OW], rb2[:, OW:2 * OW], ALU.mult,
                      inc="p4oe")
            k.wait("gpsimd", "p4oe", v1)
            k.wait("gpsimd", "p4fa", k.val("p4fa"))
            k.wait("gpsimd", "p4out", k.val("p4out"))
            k.tt("gpsimd",
                 y4n[:].rearrange("p (c n w) -> p c n w", c=DCH, n=2)[:, oc],
                 tB[:, 0:OW].rearrange("p (w n) -> p n w", n=2),
                 xc[:, oc * TW + 2: oc * TW + 2 + OW].rearrange("p (w n) -> p n w", n=2),
                 ALU.add, inc="p4y")
            k.wait("vector", "p4y", k.val("p4y"))
            k.nopi("vector", "p4yb")
        vy = k.val("p4y")
        if t == 0 and "y" in P4DBG:
            k.wait("sync", "p4y", vy)
            k.dma("sync", io["dbg_y"], y4n[:], "p4dbgf")
            k.wait("gpsimd", "p4dbgf", k.val("p4dbgf"))
            k.wait("scalar", "p4dbgf", k.val("p4dbgf"))
            k.wait("vector", "p4dbgf", k.val("p4dbgf"))
        k.wait("tensor", "p4y", vy)
        engs = ("scalar", "vector", "gpsimd")
        for eng in engs:
            k.wait(eng, "p4y", vy)
        for ch in range(DCH):
            k.cp(engs[ch % 3], xq[:, ch * OW:(ch + 1) * OW], y4n[:, ch * OW:(ch + 1) * OW],
                 inc="p4yc")
        vyc = k.val("p4yc")
        k.wait("scalar", "p4yc", vyc)
        k.wait("tensor", "p4yc", vyc)
        k.push_scope()
        arow, brow, ssem, sval = emit_norm_stats(
            k, f"p4s{t}", lambda ch, aa, bb: xq[:, ch * OW + aa: ch * OW + bb], OW, 1, OW)
        k.wait("tensor", ssem, sval)
        k.pop_scope()
        k.wait("tensor", "p4rbe3", k.val("p4rbe3"))
        k.mm(ps_bc[0][:, 0:OW], ones1[:], arow[0:1, :], start=True, stop=True)
        k.mm(ps_bc[1][:, 0:OW], ones1[:], brow[0:1, :], start=True, stop=True, inc="p4bc3")
        k.wait("scalar", "p4bc3", k.val("p4bc3"))
        k.wait("scalar", "p4fm", k.val("p4fm"))
        k.wait("scalar", "p4fa", k.val("p4fa"))
        k.cp("scalar", rb3[:, 0:OW], ps_bc[0][:, 0:OW], inc="p4rbe3")
        k.cp("scalar", rb3[:, OW:2 * OW], ps_bc[1][:, 0:OW], inc="p4rbe3")
        vab = k.val("p4rbe3")
        for eng in ("vector", "gpsimd"):
            k.wait(eng, "p4rbe3", vab)
        for ch in range(DCH):
            ya = y4n[:, ch * OW:(ch + 1) * OW]
            vm = k.tt("gpsimd", ya, ya, rb3[:, 0:OW], ALU.mult, inc="p4fm")
            k.wait("vector", "p4fm", vm)
            k.tt("vector", ya, ya, rb3[:, OW:2 * OW], ALU.add, inc="p4fa")
        vfn = k.val("p4fa")
        k.wait("sync", "p4fa", vfn)
        k.dma("sync",
              io["yout"].rearrange("p (c n j w) -> p (c n) j w", c=DCH, n=2, j=JB, w=128)[:, :, t, :],
              y4n[:].rearrange("p (cn w) -> p cn w", w=128), "p4out")
    k.wait("sync", "p4out", k.val("p4out"))
    k.pop_scope()
    return "p4out", k.val("p4out")


def declare_io_phase4(nc, standalone):
    io = {}
    add = lambda n, s, d, kind: io.__setitem__(n, nc.dram_tensor(n + _u(), s, d, kind=kind).ap())
    for nm, shp, dt in (("dbg_w1q", [128, DCH * DH], BF16), ("dbg_xq", [128, DCH * BLK], BF16),
                        ("dbg_rb1", [128, 2 * BLK], F32),
                        ("dbg_h", [128, 16 * BLK], BF16), ("dbg_c", [128, 32 * 256], BF16),
                        ("dbg_hq", [128, 32 * 256], BF16), ("dbg_y", [128, DCH * 256], F32)):
        if nm[4:] in P4DBG:
            add(nm, shp, dt, "ExternalOutput")
    add("w1T", [1024, DH], F32, "ExternalInput")
    add("w2T", [DH, 1024], F32, "ExternalInput")
    add("cw", [128, 32, 3], F32, "ExternalInput")
    add("cb", [128, 32], F32, "ExternalInput")
    add("cmrow", [1, NCOL], F32, "ExternalInput")
    add("ones1", [1, 128], F32, "ExternalInput")
    add("onesn", [128, 1], F32, "ExternalInput")
    add("iden", [128, 128], F32, "ExternalInput")
    add("yout", [128, DCH * NB * JB * 128], F32, "ExternalOutput")
    if standalone:
        add("xstate", [128, DCH * NCOL], F32, "ExternalInput")
    return io


def build_phase4_standalone():
    stack = ExitStack()
    nc = bass.Bass("TRN2", target_bir_lowering=False)
    io = declare_io_phase4(nc, True)
    with stack:
        k = K(nc, stack)
        alloc_common(k, io, stream=False)
        sem, val = emit_phase4(k, io)
        k.wait("sync", sem, val)
        k.finalize()
    return nc, io_names(io)


def phase4_in_map(P, Q, xstate):
    return {
        "w1T": P["w1T"], "w2T": P["w2T"], "cw": P["cw"], "cb": P["cb"], "cmrow": Q["cmrow"],
        "ones1": P["ones1"], "onesn": P["onesn"], "iden": P["iden"], "xstate": xstate,
    }



_BUILD_CACHE = {}


def _built(name, fn):
    if name not in _BUILD_CACHE:
        _BUILD_CACHE[name] = fn()
    return _BUILD_CACHE[name]


def host_assemble_yout(outs):
    y = np.empty((1024, 2, TSEQ), dtype=np.float32)
    for c in range(NCORES):
        o = np.asarray(outs[c], np.float32).reshape(128, DCH, 2, JB, 128)
        o = o.transpose(1, 0, 2, 3, 4).reshape(1024, 2, JB, 128)
        for j in range(JB):
            y[:, :, 1024 * j + 128 * c:1024 * j + 128 * c + 128] = o[:, :, j, :]
    return np.ascontiguousarray(y.transpose(1, 2, 0))


def kernel(**inputs):
    P = host_prep(inputs)
    Qs = [host_core_inputs(P, c) for c in range(NCORES)]

    nc1, n1 = _built("p1", build_phase1_standalone)
    maps = [{n1[kk]: v for kk, v in phase1_in_map(P, Qs[c]).items()} for c in range(NCORES)]
    r1 = run_bass_kernel_spmd(nc1, maps, core_ids=list(range(NCORES)))
    xs = [np.asarray(r1.results[c][n1["xout"]]) for c in range(NCORES)]

    nc2, n2 = _built("p2", build_phase2_standalone)
    maps = [{n2[kk]: v for kk, v in phase2_in_map(P, xs[c]).items()} for c in range(NCORES)]
    r2 = run_bass_kernel_spmd(nc2, maps, core_ids=list(range(NCORES)))
    xs = [np.asarray(r2.results[c][n2["xout"]]) for c in range(NCORES)]

    nc3, n3 = _built("p3", build_phase3_standalone)
    maps = [{n3[kk]: v for kk, v in phase3_in_map(P, xs[c]).items()} for c in range(NCORES)]
    r3 = run_bass_kernel_spmd(nc3, maps, core_ids=list(range(NCORES)))
    xs = [np.asarray(r3.results[c][n3["xout"]]) for c in range(NCORES)]

    nc4, n4 = _built("p4", build_phase4_standalone)
    maps = [{n4[kk]: v for kk, v in phase4_in_map(P, Qs[c], xs[c]).items()} for c in range(NCORES)]
    r4 = run_bass_kernel_spmd(nc4, maps, core_ids=list(range(NCORES)))
    return host_assemble_yout([r4.results[c][n4["yout"]] for c in range(NCORES)])


# version 03
# speedup vs baseline: 1.0194x; 1.0194x over previous
import numpy as np
import ml_dtypes
from contextlib import ExitStack
import concourse.bass as bass
import concourse.mybir as mybir
from concourse.bass_utils import run_bass_kernel_spmd

F32 = mybir.dt.float32
BF16 = mybir.dt.bfloat16
I32 = mybir.dt.int32
AF = mybir.ActivationFunctionType
ALU = mybir.AluOpType
AXX = mybir.AxisListType.X

NCORES = 8
D = 1024
DCH = 8
TSEQ = 8192
NB = 2
JB = 8
CW = 130
BLK = 2 * CW
NCOL = JB * BLK
QW = JB * CW
ST = 416
NT = NCOL // ST
QT = 520
HS = 64
D1 = 2048
DH = 4096
NKC = TSEQ // 128
KC_LIM = None
P4DBG = ()
P2DBG = ()
SKIP_SV = False
N_LIM = None

_UID = [0]


def _u():
    _UID[0] += 1
    return f"_{_UID[0]}"


class K:
    ENGINES = ("sync", "tensor", "vector", "scalar", "gpsimd")

    def __init__(self, nc, stack):
        self.nc = nc
        self.root = stack
        self.scopes = [stack]
        self.ops = {e: [] for e in self.ENGINES}
        self.sems = {}
        self.uid = _u()
        self.t = {}

    def push_scope(self):
        s = ExitStack()
        self.scopes.append(s)
        return s

    def pop_scope(self):
        self.scopes.pop().close()

    def sb(self, name, shape, dtype):
        return self.scopes[-1].enter_context(self.nc.sbuf_tensor(name + self.uid, shape, dtype))

    def ps(self, name, shape, dtype=F32):
        return self.scopes[-1].enter_context(self.nc.psum_tensor(name + self.uid, shape, dtype))

    def _sem(self, name, fresh=False):
        if name not in self.sems:
            if not fresh and getattr(self, "sem_pool", None):
                self.sems[name] = self.sem_pool.pop()
            else:
                h = self.root.enter_context(self.nc.semaphore(name + self.uid))
                self.sems[name] = [h, 0]
        return self.sems[name]

    def release_sems(self, prefixes):
        if not hasattr(self, "sem_pool"):
            self.sem_pool = []
        for n in list(self.sems):
            if any(n.startswith(p) for p in prefixes):
                if n in getattr(self, "swdge_names", set()):
                    continue
                self.sem_pool.append(self.sems.pop(n))

    def val(self, name):
        return self._sem(name)[1]

    def _post(self, eng, fn, inc, inck):
        if inc is None:
            self.ops[eng].append(fn)
            return None
        s = self._sem(inc)
        s[1] += inck
        h = s[0]
        self.ops[eng].append(lambda e, fn=fn, h=h, k=inck: fn(e).then_inc(h, k))
        return s[1]

    def wait(self, eng, sem, v):
        if not v:
            return
        h = self._sem(sem)[0]
        assert v <= self._sem(sem)[1], f"wait beyond emitted incs: {sem} {v}"
        self.ops[eng].append(lambda e, h=h, v=v: e.wait_ge(h, v))

    def dma(self, q, out, in_, inc, **kw):
        if q == "gpsimd" and inc is not None:
            self._sem(inc, fresh=True)
            if not hasattr(self, "swdge_names"):
                self.swdge_names = set()
            self.swdge_names.add(inc)
        return self._post(q, lambda e, o=out, i=in_, kw=kw: e.dma_start(o, i, **kw), inc, 16)

    def mm(self, out, lhsT, rhs, start, stop, inc=None):
        return self._post("tensor", lambda e, o=out, l=lhsT, r=rhs, s=start, p=stop:
                          e.matmul(o, l, r, start=s, stop=p), inc, 1)

    def tr(self, out, in_, iden, inc=None):
        return self._post("tensor", lambda e, o=out, i=in_, d=iden: e.transpose(o, i, d), inc, 1)

    def cp(self, eng, out, in_, inc=None):
        if eng == "scalar":
            return self._post(eng, lambda e, o=out, i=in_: e.copy(o, i), inc, 1)
        return self._post(eng, lambda e, o=out, i=in_: e.tensor_copy(o, i), inc, 1)

    def act(self, out, in_, func, bias=0.0, scale=1.0, inc=None):
        return self._post("scalar", lambda e, o=out, i=in_, f=func, b=bias, s=scale:
                          e.activation(o, i, f, bias=b, scale=s), inc, 1)

    def tt(self, eng, out, in0, in1, alu, inc=None):
        return self._post(eng, lambda e, o=out, a=in0, b=in1, p=alu:
                          e.tensor_tensor(o, a, b, p), inc, 1)

    def tsc(self, eng, out, in0, s1, s2, op0, op1=ALU.bypass, inc=None):
        return self._post(eng, lambda e, o=out, i=in0, s1=s1, s2=s2, p0=op0, p1=op1:
                          e.tensor_scalar(o, i, s1, s2, p0, p1), inc, 1)

    def stt(self, eng, out, in0, scalar, in1, op0, op1, inc=None):
        return self._post(eng, lambda e, o=out, a=in0, s=scalar, b=in1, p0=op0, p1=op1:
                          e.scalar_tensor_tensor(o, a, s, b, p0, p1), inc, 1)

    def red(self, eng, out, in_, alu, absval=False, inc=None):
        return self._post(eng, lambda e, o=out, i=in_, p=alu, a=absval:
                          e.tensor_reduce(o, i, AXX, p, apply_absolute_value=a), inc, 1)

    def recip(self, out, in_, inc=None):
        return self._post("vector", lambda e, o=out, i=in_: e.reciprocal(o, i), inc, 1)

    def memset(self, eng, ap, v, inc=None):
        return self._post(eng, lambda e, a=ap, v=v: e.memset(a, v), inc, 1)

    def nopi(self, eng, inc):
        return self._post(eng, lambda e: e.nop(), inc, 1)

    def barrier(self, tag):
        s = self._sem("bar_" + tag)
        h = s[0]
        s[1] += len(self.ENGINES)
        tgt = s[1]
        for eng in self.ENGINES:
            self.ops[eng].append(lambda e, h=h: e.nop().then_inc(h, 1))
        for eng in self.ENGINES:
            self.ops[eng].append(lambda e, h=h, t=tgt: e.wait_ge(h, t))

    def finalize(self):
        while len(self.scopes) > 1:
            self.pop_scope()
        with self.nc.Block() as block:
            for eng in self.ENGINES:
                getattr(block, eng)(lambda e, eng=eng: [f(e) for f in self.ops[eng]])



def _f32c(x):
    return np.ascontiguousarray(np.asarray(x), dtype=np.float32)


def host_prep(inputs):
    P = {}
    x = np.asarray(inputs["L0_attn_ids"])
    P["xT"] = _f32c(x.transpose(2, 0, 1))
    P["qkvT"] = _f32c(np.asarray(inputs["L0_attn_qkv_w"]).T)
    qb = _f32c(inputs["L0_attn_qkv_b"])
    P["bq"] = qb[0:64].reshape(64, 1).copy()
    P["bkv"] = qb[64:192].reshape(128, 1).copy()
    P["projT"] = _f32c(np.asarray(inputs["L0_attn_proj_w"]).T)
    P["proj_b"] = _f32c(inputs["L0_attn_proj_b"]).reshape(8, 128).T.copy()
    E = np.zeros((1024, 64), dtype=np.float32)
    E[np.arange(1024), np.arange(1024) % 64] = 1.0
    P["efold"] = E
    P["ones1"] = np.ones((1, 128), dtype=np.float32)
    P["onesn"] = np.full((128, 1), 1.0 / 1024.0, dtype=np.float32)
    P["iden"] = np.eye(128, dtype=np.float32)
    P["qTw"] = np.ascontiguousarray(np.asarray(inputs["L0_ffn_qweights"]).transpose(0, 2, 1))
    P["sTe"] = _f32c(np.repeat(np.asarray(inputs["L0_ffn_scales"]), 16, axis=2).transpose(0, 2, 1))
    P["biasA"] = np.ascontiguousarray(
        _f32c(inputs["L0_ffn_biases"]).reshape(6, 8, 128).transpose(2, 0, 1).reshape(128, 48))
    P["laT"] = _f32c(np.asarray(inputs["L0_ffn_lora_a"]).transpose(0, 2, 1))
    P["lbT"] = _f32c(np.asarray(inputs["L0_ffn_lora_b"]).transpose(0, 2, 1))
    P["lbT2"] = np.ascontiguousarray(P["lbT"].transpose(1, 0, 2).reshape(32, 6 * 1024))
    P["gamma"] = np.ascontiguousarray(
        _f32c(inputs["L0_ffn_ln_gamma"]).reshape(3, 8, 128).transpose(2, 0, 1).reshape(128, 24))
    P["beta"] = np.ascontiguousarray(
        _f32c(inputs["L0_ffn_ln_beta"]).reshape(3, 8, 128).transpose(2, 0, 1).reshape(128, 24))
    P["WuT"] = _f32c(np.asarray(inputs["L1_attn_Wu"]).T)
    P["WvT"] = _f32c(np.asarray(inputs["L1_attn_Wv"]).T)
    P["WoT"] = _f32c(np.asarray(inputs["L1_attn_Wo"]).T)
    P["bu"] = _f32c(inputs["L1_attn_bu"]).reshape(16, 128).T.copy()
    P["bv"] = _f32c(inputs["L1_attn_bv"]).reshape(16, 128).T.copy()
    P["bo"] = _f32c(inputs["L1_attn_bo"]).reshape(8, 128).T.copy()
    P["ze"] = _f32c(np.repeat(np.asarray(inputs["L1_attn_zero"])[:, 0], 256)).reshape(16, 128).T.copy()
    P["pe"] = _f32c(np.repeat(np.asarray(inputs["L1_attn_pos"])[:, 0], 256)).reshape(16, 128).T.copy()
    P["w1T"] = _f32c(np.asarray(inputs["L1_ffn_w1"]).T)
    P["w2T"] = _f32c(np.asarray(inputs["L1_ffn_w2"]).T)
    cw = _f32c(np.asarray(inputs["L1_ffn_conv_w"])[:, 0, :])
    P["cw"] = cw.reshape(32, 128, 3).transpose(1, 0, 2).copy()
    P["cb"] = _f32c(inputs["L1_ffn_conv_b"]).reshape(32, 128).T.copy()
    return P


def host_core_inputs(P, c):
    Q = {}
    xT = P["xT"]
    pos = np.empty((JB, CW), dtype=np.int64)
    for j in range(JB):
        pos[j] = 1024 * j + 128 * c + np.arange(CW) - 1
    posc = np.clip(pos, 0, TSEQ - 1)
    g = xT[:, :, posc.reshape(-1)].reshape(1024, 2, JB, CW)
    xin = np.zeros((1024, JB, CW, 2), dtype=np.float32)
    xin[:, :, :, 0] = g[:, 0]
    xin[:, :, :, 1] = g[:, 1]
    bad = (pos < 0) | (pos >= TSEQ)
    xin[:, bad, :] = 0.0
    Q["xin"] = np.ascontiguousarray(xin.reshape(1024, NCOL))
    m = np.arange(8)[:, None, None] * 128 + np.arange(128)[None, :, None]
    qq = 128 * c + np.arange(CW)[None, None, :] - 1
    Q["mask"] = np.ascontiguousarray((m <= qq).astype(np.float32).transpose(1, 0, 2).reshape(128, 8 * CW))
    cm = np.ones((1, NCOL), dtype=np.float32)
    for j in range(JB):
        for u in (0, CW - 1):
            for n in range(2):
                if pos[j, u] < 0 or pos[j, u] >= TSEQ:
                    cm[0, BLK * j + 2 * u + n] = 0.0
    Q["cmrow"] = cm
    return Q


def host_assemble(outs):
    y = np.empty((1024, 2, TSEQ), dtype=np.float32)
    for c in range(NCORES):
        o = np.asarray(outs[c]).reshape(1024, 2, JB, 128)
        for j in range(JB):
            y[:, :, 1024 * j + 128 * c:1024 * j + 128 * c + 128] = o[:, :, j, :]
    return np.ascontiguousarray(y.transpose(1, 2, 0))



def alloc_common(k, io, stream=True):
    if stream:
        k.t["X"] = k.sb("X", [128, DCH * NCOL], F32)
        k.t["XB"] = k.sb("XB", [128, DCH * NCOL], BF16)
    k.t["ones1_f"] = k.sb("ones1f", [1, 128], F32)
    k.t["onescol_f"] = k.sb("onescolf", [128, 1], F32)
    k.t["onesn_bf"] = k.sb("onesnb", [128, 1], BF16)
    k.t["iden_f"] = k.sb("idenf", [128, 128], F32)
    k.t["iden_b"] = k.sb("idenb", [128, 128], BF16)
    k.t["eps_ln"] = k.sb("epsln", [128, 1], F32)
    k.t["neg1"] = k.sb("neg1", [128, 1], F32)
    k.t["ones1_b"] = k.sb("ones1b", [1, 128], BF16)
    k.dma("sync", k.t["ones1_f"][:], io["ones1"], "cst")
    k.dma("gpsimd", k.t["onesn_bf"][:], io["onesn"], "cstg")
    k.dma("gpsimd", k.t["ones1_b"][:], io["ones1"], "cstg")
    k.dma("sync", k.t["iden_f"][:], io["iden"], "cst")
    k.dma("gpsimd", k.t["iden_b"][:], io["iden"], "cstg")
    k.memset("vector", k.t["eps_ln"][:], 1e-5, inc="cstv")
    k.memset("vector", k.t["neg1"][:], -1.0, inc="cstv")
    k.memset("vector", k.t["onescol_f"][:], 1.0, inc="cstv")
    for eng in ("tensor", "vector", "scalar"):
        k.wait(eng, "cst", k.val("cst"))
        k.wait(eng, "cstg", k.val("cstg"))
        k.wait(eng, "cstv", k.val("cstv"))


def Xc(k, ch, a, b, step=1):
    return k.t["X"][:, ch * NCOL + a: ch * NCOL + b: step] if step != 1 else \
        k.t["X"][:, ch * NCOL + a: ch * NCOL + b]


def XBc(k, ch, a, b):
    return k.t["XB"][:, ch * NCOL + a: ch * NCOL + b]


def emit_load_stream(k, io):
    X, XB = k.t["X"], k.t["XB"]
    src = io["xin"].rearrange("(c p) n -> p c n", p=128)
    k.dma("sync", X[:].rearrange("p (c n) -> p c n", c=DCH), src, "ldx")
    k.dma("gpsimd", XB[:].rearrange("p (c n) -> p c n", c=DCH), src, "ldxb")
    for eng in ("tensor", "vector", "scalar", "gpsimd"):
        k.wait(eng, "ldx", k.val("ldx"))
        k.wait(eng, "ldxb", k.val("ldxb"))


def emit_norm_stats(k, tag, src_bf, ncol, nt, tw, nch=DCH, rms=False, eps=1e-5):
    onesn = k.t["onesn_bf"]
    iden = k.t["iden_f"]
    ntr = (ncol + 127) // 128
    sq_scr = k.sb(f"sqs_{tag}", [128, 2 * tw], BF16)
    ps_s = k.ps(f"pss_{tag}", [1, 512], F32)
    ps_m = k.ps(f"psm_{tag}", [1, 512], F32) if not rms else None
    srow = k.sb(f"srow_{tag}", [1, ncol], F32)
    mrow = k.sb(f"mrow_{tag}", [1, ncol], F32) if not rms else None
    ps_tm = k.ps(f"pstm_{tag}", [128, 2 * ntr], F32)
    tmm = k.sb(f"tmm_{tag}", [128, 4 * ntr], F32)
    ps_rb = ps_s
    ps_rb2 = ps_m if not rms else ps_s
    arow = k.sb(f"arow_{tag}", [1, ncol], F32)
    brow = k.sb(f"brow_{tag}", [1, ncol], F32) if not rms else None

    qs0 = k.val("ns_qs")
    sv0 = k.val("ns_sqv")
    k.wait("scalar", "ns_qs", qs0)
    k.wait("vector", "ns_rb", k.val("ns_rb"))
    k.wait("vector", "ns_tm", k.val("ns_tm"))
    isq = 0
    for t in range(nt):
        a = t * tw
        for ch in range(nch):
            half = isq % 2
            scr = sq_scr[:, half * tw:(half + 1) * tw]
            if isq >= 2:
                k.wait("scalar", "ns_qs", qs0 + isq - 1)
            k.act(scr, src_bf(ch, a, a + tw), AF.Square, inc="ns_sqv")
            if not rms:
                k.mm(ps_m[:, 0:tw], onesn[:], src_bf(ch, a, a + tw),
                     start=(ch == 0), stop=(ch == nch - 1))
            k.wait("tensor", "ns_sqv", sv0 + isq + 1)
            k.mm(ps_s[:, 0:tw], onesn[:], scr, start=(ch == 0), stop=(ch == nch - 1),
                 inc="ns_qs")
            isq += 1
        k.wait("vector", "ns_qs", k.val("ns_qs"))
        k.cp("vector", srow[0:1, a:a + tw], ps_s[:, 0:tw], inc="ns_rows")
        if not rms:
            k.cp("vector", mrow[0:1, a:a + tw], ps_m[:, 0:tw], inc="ns_rows")
        k.wait("tensor", "ns_rows", k.val("ns_rows"))
    k.memset("vector", ps_tm[:], 0.0, inc="ns_tmz")
    k.wait("tensor", "ns_tmz", k.val("ns_tmz"))
    for kk in range(ntr):
        w = min(128, ncol - kk * 128)
        k.tr(ps_tm[0:w, kk:kk + 1], srow[0:1, kk * 128:kk * 128 + w], iden[0:1, 0:1],
             inc="ns_tm")
        if not rms:
            k.tr(ps_tm[0:w, ntr + kk:ntr + kk + 1], mrow[0:1, kk * 128:kk * 128 + w],
                 iden[0:1, 0:1], inc="ns_tm")
    k.wait("vector", "ns_tm", k.val("ns_tm"))
    k.cp("vector", tmm[:, 0:ntr], ps_tm[:, 0:ntr], inc="ns_tmc")
    if not rms:
        k.cp("vector", tmm[:, ntr:2 * ntr], ps_tm[:, ntr:2 * ntr], inc="ns_tmc")
        k.wait("vector", "ns_tmc", k.val("ns_tmc"))
        k.tt("vector", tmm[:, 3 * ntr:4 * ntr], tmm[:, ntr:2 * ntr], tmm[:, ntr:2 * ntr],
             ALU.mult, inc="ns_tmc")
        k.wait("vector", "ns_tmc", k.val("ns_tmc"))
        k.tt("vector", tmm[:, 0:ntr], tmm[:, 0:ntr], tmm[:, 3 * ntr:4 * ntr], ALU.subtract,
             inc="ns_tmc")
        k.wait("scalar", "ns_tmc", k.val("ns_tmc"))
        k.act(tmm[:, 2 * ntr:3 * ntr], tmm[:, 0:ntr], AF.Sqrt, bias=k.t["eps_ln"][:],
              inc="ns_sq2")
        k.wait("vector", "ns_sq2", k.val("ns_sq2"))
        k.recip(tmm[:, 2 * ntr:3 * ntr], tmm[:, 2 * ntr:3 * ntr], inc="ns_rc")
        k.wait("vector", "ns_rc", k.val("ns_rc"))
        k.stt("vector", tmm[:, ntr:2 * ntr], tmm[:, ntr:2 * ntr], -1.0,
              tmm[:, 2 * ntr:3 * ntr], ALU.mult, ALU.mult, inc="ns_rc")
    else:
        k.wait("scalar", "ns_tmc", k.val("ns_tmc"))
        k.act(tmm[:, 2 * ntr:3 * ntr], tmm[:, 0:ntr], AF.Sqrt, inc="ns_sq2")
        k.wait("vector", "ns_sq2", k.val("ns_sq2"))
        k.tsc("vector", tmm[:, 2 * ntr:3 * ntr], tmm[:, 2 * ntr:3 * ntr], eps, None, ALU.add,
              inc="ns_sq2e")
        k.wait("vector", "ns_sq2e", k.val("ns_sq2e"))
        k.recip(tmm[:, 2 * ntr:3 * ntr], tmm[:, 2 * ntr:3 * ntr], inc="ns_rc")
    k.wait("tensor", "ns_rc", k.val("ns_rc"))
    for kk0 in range(0, ntr, 4):
        kks = list(range(kk0, min(kk0 + 4, ntr)))
        for (pp, colblk, dst) in ((ps_rb, 2, arow),) + ((() if rms else ((ps_rb2, 1, brow),))):
            for kk in kks:
                w = min(128, ncol - kk * 128)
                k.tr(pp[0:1, (kk - kk0) * 128:(kk - kk0) * 128 + w],
                     tmm[0:w, colblk * ntr + kk:colblk * ntr + kk + 1], iden[0:w, 0:w],
                     inc="ns_rb")
            k.wait("scalar", "ns_rb", k.val("ns_rb"))
            w4 = min(512, ncol - kk0 * 128)
            k.cp("scalar", dst[0:1, kk0 * 128:kk0 * 128 + w4], pp[0:1, 0:w4],
                 inc="ns_rbe")
        k.wait("tensor", "ns_rbe", k.val("ns_rbe"))
    return arow, brow, "ns_rbe", k.val("ns_rbe")


def emit_postln(k, tag, pre):
    X, XB = k.t["X"], k.t["XB"]
    ones1 = k.t["ones1_f"]
    k.push_scope()
    for eng in ("scalar", "vector", "gpsimd", "tensor"):
        for (ps, pv) in pre:
            k.wait(eng, ps, pv)
    engs = ("scalar", "vector", "gpsimd")
    for t in range(NT):
        a = t * ST
        for ch in range(DCH):
            k.cp(engs[(t * DCH + ch) % 3], XBc(k, ch, a, a + ST), Xc(k, ch, a, a + ST),
                 inc="pl_cst")
    vc = k.val("pl_cst")
    for eng in ("scalar", "tensor"):
        k.wait(eng, "pl_cst", vc)
    arow, brow, ssem, sval = emit_norm_stats(
        k, tag, lambda ch, a, b: XBc(k, ch, a, b), NCOL, NT, ST)
    ps_a = k.ps(f"psa_{tag}", [128, ST], F32)
    ps_b = k.ps(f"psb_{tag}", [128, ST], F32)
    a_sb = k.sb(f"asb_{tag}", [128, ST], F32)
    b_sb = k.sb(f"bsb_{tag}", [128, ST], F32)
    k.wait("tensor", ssem, sval)
    for t in range(NT):
        a = t * ST
        if t > 0:
            k.wait("tensor", "pl_abe", k.val("pl_abe"))
            k.wait("scalar", "pl_nm", k.val("pl_nm"))
            k.wait("scalar", "pl_na", k.val("pl_na"))
        k.mm(ps_a[:], ones1[:], arow[0:1, a:a + ST], start=True, stop=True)
        k.mm(ps_b[:], ones1[:], brow[0:1, a:a + ST], start=True, stop=True, inc="pl_ab")
        k.wait("scalar", "pl_ab", k.val("pl_ab"))
        k.cp("scalar", a_sb[:], ps_a[:], inc="pl_abe")
        k.cp("scalar", b_sb[:], ps_b[:], inc="pl_abe")
        vab = k.val("pl_abe")
        k.wait("gpsimd", "pl_abe", vab)
        k.wait("vector", "pl_abe", vab)
        for ch in range(DCH):
            xa = Xc(k, ch, a, a + ST)
            xba = XBc(k, ch, a, a + ST)
            vm = k.tt("gpsimd", xa, xa, a_sb[:], ALU.mult, inc="pl_nm")
            k.wait("vector", "pl_nm", vm)
            va = k.tt("vector", xa, xa, b_sb[:], ALU.add, inc="pl_na")
            k.wait("scalar", "pl_na", va)
            k.cp("scalar", xba, xa, inc="pl_nc")
    k.wait("tensor", "pl_abe", k.val("pl_abe"))
    out = ("pl_nc", k.val("pl_nc"))
    k.pop_scope()
    return out



def emit_phase1(k, io, upto="pl"):
    X, XB = k.t["X"], k.t["XB"]
    iden_b = k.t["iden_b"]
    k.push_scope()
    qkvT = k.sb("qkvT", [128, DCH * 192], BF16)
    bq = k.sb("bq", [64, 1], F32)
    bkv = k.sb("bkv", [128, 1], F32)
    projT = k.sb("projT", [128, DCH * 1024], BF16)
    pbias = k.sb("pbias", [128, 8], F32)
    efold = k.sb("efold", [128, DCH * 64], BF16)
    masks = k.sb("masks", [128, 8 * CW], BF16)
    kv_sb = k.sb("kv_sb", [128, NB * TSEQ], BF16)
    vT_sb = k.sb("vT_sb", [128, NB * NKC * 64], BF16)
    q_sb = k.sb("q_sb", [64, NB * QW], BF16)
    outh = k.sb("outh", [64, NB * QW], BF16)
    stsb = k.sb("stsb", [128, 2 * QW], BF16)
    pweff = k.sb("pweff", [64, 1024], BF16)

    load_wT_bf(k, io["qkvT"], qkvT, DCH, 192, "p1w")
    load_wT_bf(k, io["projT"], projT, DCH, 1024, "p1w")
    load_wT_bf(k, io["efold"], efold, DCH, 64, "p1w")
    k.dma("gpsimd", masks[:], io["mask"], "p1w")
    k.dma("sync", bq[:], io["bq"], "p1b")
    k.dma("sync", bkv[:], io["bkv"], "p1b")
    k.dma("sync", pbias[:], io["proj_b"], "p1b")
    vw = k.val("p1w")
    vb = k.val("p1b")
    for eng in ("tensor", "vector", "scalar"):
        k.wait(eng, "p1w", vw)
        k.wait(eng, "p1b", vb)

    KVT = 512
    nkv = TSEQ // KVT
    xkv = k.sb("xkv", [128, 2 * DCH * KVT], BF16)
    k.push_scope()
    ps_kv = [k.ps(f"pskv{i}", [128, KVT], F32) for i in range(2)]
    ps_vt = k.ps("psvt", [128, 256], BF16)
    for n in range(NB):
        for tks in range(nkv):
            ti = n * nkv + tks
            buf = ti % 2
            pkv = ps_kv[ti % 2]
            s0 = tks * KVT
            src = io["xT"].rearrange("(c p) n t -> p c n t", p=128)[:, :, n, s0:s0 + KVT]
            dst = xkv[:, buf * DCH * KVT:(buf + 1) * DCH * KVT].rearrange(
                "p (c w) -> p c w", c=DCH)
            if ti >= 2:
                k.wait("gpsimd", "p1kvmm", k.val("p1kvmm") - 1)
            vld = k.dma("gpsimd", dst, src, f"p1kv{buf}")
            k.wait("tensor", f"p1kv{buf}", vld)
            if ti >= 2:
                k.wait("tensor", "p1kve", k.val("p1kve") - 1)
            for ch in range(DCH):
                k.mm(pkv[:], qkvT[:, ch * 192 + 64:(ch + 1) * 192],
                     xkv[:, buf * DCH * KVT + ch * KVT: buf * DCH * KVT + (ch + 1) * KVT],
                     start=(ch == 0), stop=(ch == DCH - 1),
                     inc=("p1kvmm" if ch == DCH - 1 else None))
            k.wait("vector", "p1kvmm", k.val("p1kvmm"))
            vev = k.tsc("vector", kv_sb[:, n * TSEQ + s0:n * TSEQ + s0 + KVT], pkv[:],
                        bkv[:], None, ALU.add, inc="p1kve")
            k.wait("tensor", "p1kve", vev)
            if ti >= 1:
                k.wait("tensor", "p1vte", k.val("p1vte"))
            for pz in range(KVT // 128):
                k.tr(ps_vt[:, pz * 64:(pz + 1) * 64],
                     kv_sb[64:128, n * TSEQ + s0 + pz * 128:n * TSEQ + s0 + (pz + 1) * 128],
                     iden_b[64:128, 64:128], inc=("p1vt" if pz == 3 else None))
            k.wait("scalar", "p1vt", k.val("p1vt"))
            base = n * NKC * 64 + (s0 // 128) * 64
            k.cp("scalar", vT_sb[:, base:base + 256], ps_vt[:], inc="p1vte")

    k.wait("tensor", "p1vte", k.val("p1vte"))
    k.wait("tensor", "p1kve", k.val("p1kve"))
    k.pop_scope()

    if upto == "a":
        k.t["dbg"] = (kv_sb, "p1vte", k.val("p1vte"))
        return None
    k.push_scope()
    ps_q = k.ps("psq", [64, QT], F32)
    for n in range(NB):
        for t in range(QW // QT):
            a = t * QT
            if n + t > 0:
                k.wait("tensor", "p1qe", k.val("p1qe"))
            for (lo, hi) in ((0, 512), (512, QT)):
                for ch in range(DCH):
                    k.mm(ps_q[:, lo:hi], qkvT[:, ch * 192:ch * 192 + 64],
                         XB[:, ch * NCOL + 2 * (a + lo) + n: ch * NCOL + 2 * (a + hi) + n - 1: 2],
                         start=(ch == 0), stop=(ch == DCH - 1),
                         inc=("p1q" if (ch == DCH - 1 and hi == QT) else None))
            k.wait("vector", "p1q", k.val("p1q"))
            k.tsc("vector", q_sb[:, n * QW + a:n * QW + a + QT], ps_q[:], bq[:], None,
                  ALU.add, inc="p1qe")
    k.wait("tensor", "p1qe", k.val("p1qe"))
    k.pop_scope()

    if upto == "b":
        k.t["dbg"] = (q_sb, "p1qe", k.val("p1qe"))
        return None
    k.push_scope()
    ps_o1 = k.ps("pso1", [64, 512], F32)
    ps_o2 = k.ps("pso2", [64, 512], F32)
    ps_o3 = k.ps("pso3", [64, QW - 1024], F32)
    ps_s1 = [k.ps(f"pss1{i}", [128, 512], F32) for i in range(2)]
    ps_s2 = [k.ps(f"pss2{i}", [128, 512], F32) for i in range(2)]
    ps_s3 = [k.ps("pss3", [128, QW - 1024], F32)] * 2
    OP = ((0, 512), (512, 1024), (1024, QW))

    def stp(dbuf, lo, hi):
        for i, (p0, p1) in enumerate(OP):
            if lo >= p0 and hi <= p1:
                return (ps_s1, ps_s2, ps_s3)[i][dbuf][:, lo - p0:hi - p0]
        raise AssertionError((lo, hi))

    nkc_l = KC_LIM or NKC
    for n in range(N_LIM or NB):
        if n == 1:
            k.wait("tensor", "p1oe", k.val("p1oe"))
        for kc in range(nkc_l):
            jmin = kc // 8
            mloc = kc % 8
            qstart = jmin * CW
            dbuf = kc % 2
            pieces = [(max(p0, qstart), p1) for (p0, p1) in OP if max(p0, qstart) < p1]
            idx = n * NKC + kc
            if idx >= 2:
                k.wait("tensor", "p1sv", k.val("p1sv") - 1)
            if idx >= 1:
                k.wait("tensor", "p1se3", k.val("p1se3"))
            for pj, (lo, hi) in enumerate(pieces):
                k.mm(stp(dbuf, lo, hi), kv_sb[0:64, n * TSEQ + kc * 128:n * TSEQ + (kc + 1) * 128],
                     q_sb[0:64, n * QW + lo:n * QW + hi], start=True, stop=True,
                     inc=("p1st" if pj == len(pieces) - 1 else None))
            vst = k.val("p1st")
            k.wait("vector", "p1st", vst)
            k.wait("scalar", "p1st", vst)
            for pi, (lo, hi) in enumerate(pieces):
                dhi = min(hi, qstart + CW)
                plo = max(lo, dhi)
                if lo < dhi:
                    k.tt("vector", stsb[:, dbuf * QW + lo:dbuf * QW + dhi],
                         stp(dbuf, lo, dhi),
                         masks[:, mloc * CW + (lo - qstart):mloc * CW + (dhi - qstart)],
                         ALU.mult, inc=("p1se3" if lo >= 1024 else "p1sev"))
                if plo < hi:
                    s3b = plo >= 1024
                    k.cp("vector", stsb[:, dbuf * QW + plo:dbuf * QW + hi],
                         stp(dbuf, plo, hi), inc=("p1se3" if s3b else "p1sev"))
            k.wait("tensor", "p1sev", k.val("p1sev"))
            k.wait("tensor", "p1ses", k.val("p1ses"))
            k.wait("tensor", "p1se3", k.val("p1se3"))
            subs = []
            for (lo, hi) in (() if SKIP_SV else pieces):
                for i, (p0, p1) in enumerate(OP):
                    l2, h2 = max(lo, p0), min(hi, p1)
                    if l2 < h2:
                        subs.append((i, l2, h2))
            for sj, (i, l2, h2) in enumerate(subs):
                po = (ps_o1, ps_o2, ps_o3)[i]
                p0 = OP[i][0]
                last = min(nkc_l - 1, NKC - 1 if p0 >= 512 else (4 * 8 - 1))
                k.mm(po[:, l2 - p0:h2 - p0],
                     vT_sb[:, n * NKC * 64 + kc * 64:n * NKC * 64 + (kc + 1) * 64],
                     stsb[:, dbuf * QW + l2:dbuf * QW + h2],
                     start=(kc == 0), stop=(kc == last),
                     inc=("p1sv" if sj == len(subs) - 1 else None))
            if SKIP_SV:
                k.nopi("tensor", "p1sv")
        k.wait("scalar", "p1sv", k.val("p1sv"))
        if not SKIP_SV:
            k.cp("scalar", outh[:, n * QW:n * QW + 512], ps_o1[:], inc="p1oe")
            k.cp("scalar", outh[:, n * QW + 512:n * QW + 1024], ps_o2[:], inc="p1oe")
            k.cp("scalar", outh[:, n * QW + 1024:(n + 1) * QW], ps_o3[:], inc="p1oe")
        else:
            k.nopi("scalar", "p1oe")
    k.wait("tensor", "p1oe", k.val("p1oe"))
    k.wait("tensor", "p1sev", k.val("p1sev"))
    k.wait("tensor", "p1ses", k.val("p1ses"))
    k.wait("tensor", "p1se3", k.val("p1se3"))
    k.pop_scope()

    if upto == "c1":
        k.t["dbg"] = (stsb, "p1se3", k.val("p1se3"))
        return None
    if upto == "c":
        k.t["dbg"] = (outh, "p1oe", k.val("p1oe"))
        return None
    k.push_scope()
    ps_pw = k.ps("pspw", [64, 1024], F32)
    for ch in range(DCH):
        for half in range(2):
            k.mm(ps_pw[:, half * 512:(half + 1) * 512], efold[:, ch * 64:(ch + 1) * 64],
                 projT[:, ch * 1024 + half * 512:ch * 1024 + (half + 1) * 512],
                 start=(ch == 0), stop=(ch == DCH - 1),
                 inc=("p1pw" if (ch == DCH - 1 and half == 1) else None))
    k.wait("vector", "p1pw", k.val("p1pw"))
    k.cp("vector", pweff[:], ps_pw[:], inc="p1pwe")
    k.wait("tensor", "p1pwe", k.val("p1pwe"))

    ps_pr = [k.ps(f"pspr{i}", [128, QT], F32) for i in range(2)]
    ti = 0
    for n in range(NB):
        for t in range(QW // QT):
            a = t * QT
            pp = ps_pr[ti % 2]
            for oc in range(DCH):
                for (lo, hi) in ((0, 512), (512, QT)):
                    k.mm(pp[:, lo:hi], pweff[:, oc * 128:(oc + 1) * 128],
                         outh[:, n * QW + a + lo:n * QW + a + hi], start=True, stop=True,
                         inc=("p1pr" if hi == QT else None))
                k.wait("vector", "p1pr", k.val("p1pr"))
                k.stt("vector",
                      X[:, oc * NCOL + 2 * a + n: oc * NCOL + 2 * (a + QT) + n - 1: 2],
                      pp[:, 0:QT], pbias[:, oc:oc + 1],
                      X[:, oc * NCOL + 2 * a + n: oc * NCOL + 2 * (a + QT) + n - 1: 2],
                      ALU.add, ALU.add, inc="p1pre")
                k.wait("tensor", "p1pre", k.val("p1pre"))
            ti += 1
    pre = ("p1pre", k.val("p1pre"))
    k.pop_scope()
    k.pop_scope()
    if upto == "e":
        return pre
    return emit_postln(k, "pl1", [pre])


def load_wT_bf(k, hbm, sbuf, nch, width, sem):
    src = hbm.rearrange("(c p) m -> p c m", p=128)
    dst = sbuf[:].rearrange("p (c w) -> p c w", c=nch)
    return k.dma("gpsimd", dst, src, sem)



def declare_io_phase1(nc, standalone):
    io = {}
    add = lambda n, s, d, kind: io.__setitem__(n, nc.dram_tensor(n + _u(), s, d, kind=kind).ap())
    add("xin", [1024, NCOL], F32, "ExternalInput")
    add("xT", [1024, NB, TSEQ], F32, "ExternalInput")
    add("qkvT", [1024, 192], F32, "ExternalInput")
    add("bq", [64, 1], F32, "ExternalInput")
    add("bkv", [128, 1], F32, "ExternalInput")
    add("projT", [1024, 1024], F32, "ExternalInput")
    add("proj_b", [128, 8], F32, "ExternalInput")
    add("efold", [1024, 64], F32, "ExternalInput")
    add("mask", [128, 8 * CW], F32, "ExternalInput")
    add("ones1", [1, 128], F32, "ExternalInput")
    add("onesn", [128, 1], F32, "ExternalInput")
    add("iden", [128, 128], F32, "ExternalInput")
    if standalone:
        add("xout", [128, DCH * NCOL], F32, "ExternalOutput")
    return io


def io_names(io):
    return {kk: ap.tensor.name for kk, ap in io.items()}


def build_phase1_standalone(upto="pl"):
    stack = ExitStack()
    nc = bass.Bass("TRN2", target_bir_lowering=False)
    io = declare_io_phase1(nc, True)
    dbg_shape = [None]
    with stack:
        k = K(nc, stack)
        alloc_common(k, io)
        emit_load_stream(k, io)
        r = emit_phase1(k, io, upto=upto)
        if r is None:
            tile, sem, val = k.t["dbg"]
            shp = [tile[:].shape[0], tile[:].shape[1]]
            dbg_shape[0] = shp
            dbgout = nc.dram_tensor("dbgout" + _u(), shp, tile[:].tensor.dtype,
                                    kind="ExternalOutput")
            io["dbgout"] = dbgout.ap()
            k.wait("sync", sem, val)
            k.dma("sync", io["dbgout"], tile[:], "out")
        elif upto == "e":
            sem, val = r
            k.wait("sync", sem, val)
            k.dma("sync", io["xout"], k.t["X"][:], "out")
        else:
            sem, val = r
            k.wait("sync", sem, val)
            k.dma("sync", io["xout"], k.t["X"][:], "out")
        k.wait("sync", "out", k.val("out"))
        k.finalize()
    return nc, io_names(io)


def phase1_in_map(P, Q):
    return {
        "xin": Q["xin"], "xT": P["xT"], "qkvT": P["qkvT"], "bq": P["bq"], "bkv": P["bkv"],
        "projT": P["projT"], "proj_b": P["proj_b"], "efold": P["efold"], "mask": Q["mask"],
        "ones1": P["ones1"], "onesn": P["onesn"], "iden": P["iden"],
    }





def emit_dequant(k, io, m, wbuf, qbufs, sbufs, waitsem=None, waitval=0):
    neg1 = k.t["neg1"]
    if waitsem:
        k.wait("gpsimd", waitsem, waitval)
    k.wait("sync", "dq_m", k.val("dq_m"))
    for hc in range(2 * DCH):
        ch, hh = hc // 2, hc % 2
        b = hc % 2
        if hc >= 2:
            k.wait("sync", "dq_m", k.val("dq_m") - 1)
        qsrc = io["qTw"][m].rearrange("(c p) w -> p c w", p=128)[:, ch, hh * 512:(hh + 1) * 512]
        ssrc = io["sTe"][m].rearrange("(c p) w -> p c w", p=128)[:, ch, hh * 512:(hh + 1) * 512]
        vqd = k.dma("sync", qbufs[b][:], qsrc, f"dq_q{b}")
        vsd = k.dma("sync", sbufs[b][:], ssrc, f"dq_s{b}")
        k.wait("scalar", f"dq_q{b}", vqd)
        if hc >= 1:
            k.wait("scalar", "dq_m", k.val("dq_m"))
        vc = k.act(k.t["qf_scr"][:], qbufs[b][:], AF.Copy,
                   scale=1.0 / 7.5, inc=f"dq_c{b}")
        k.wait("vector", f"dq_c{b}", vc)
        k.wait("vector", f"dq_s{b}", vsd)
        k.stt("vector", wbuf[:, ch * 1024 + hh * 512: ch * 1024 + (hh + 1) * 512],
              k.t["qf_scr"][:], -1.0, sbufs[b][:], ALU.add, ALU.mult, inc="dq_m")
    return "dq_m", k.val("dq_m")


def emit_phase2(k, io):
    X, XB = k.t["X"], k.t["XB"]
    k.push_scope()
    wbufs = [k.sb(f"w2b{i}", [128, DCH * 1024], BF16) for i in range(3)]
    k.t["qf_scr"] = k.sb("qfscr", [128, 512], F32)
    qbufs = [k.sb(f"qbuf{i}", [128, 512], I32) for i in range(2)]
    sbufs = [k.sb(f"sbuf{i}", [128, 512], F32) for i in range(2)]
    laT = k.sb("laT", [128, 6 * DCH * 32], BF16)
    lbT = k.sb("lbT", [32, 6 * 1024], BF16)
    biasA = k.sb("biasA", [128, 6 * 8], F32)
    gam = k.sb("gam", [128, 3 * 8], F32)
    bet = k.sb("bet", [128, 3 * 8], F32)
    Y1 = k.sb("Y1", [128, DCH * ST], F32)
    VB = k.sb("VB", [128, DCH * ST], BF16)
    G = VB
    t1s = k.sb("t1s", [32, 2 * ST], BF16)
    a_sb = k.sb("p2asb", [128, ST], F32)
    b_sb = k.sb("p2bsb", [128, ST], F32)

    k.dma("gpsimd", laT[:].rearrange("p (m c w) -> p (m c) w", m=6, w=32),
          io["laT"].rearrange("m (c p) w -> p (m c) w", p=128), "p2wg")
    k.dma("gpsimd", lbT[:].rearrange("p (m w) -> p m w", m=6), io["lbT"], "p2wg")
    k.dma("sync", biasA[:], io["biasA"], "p2ws")
    k.dma("sync", gam[:], io["gamma"], "p2ws")
    k.dma("sync", bet[:], io["beta"], "p2ws")
    for eng in ("tensor", "vector", "scalar", "gpsimd"):
        k.wait(eng, "p2wg", k.val("p2wg"))
        k.wait(eng, "p2ws", k.val("p2ws"))

    if "gam" in P2DBG:
        k.wait("sync", "p2ws", k.val("p2ws"))
        vdg = k.dma("sync", io["dbg_gam"], gam[:], "p2dbgc")
        k.wait("sync", "p2dbgc", vdg)
    emit_dequant(k, io, 0, wbufs[0], qbufs, sbufs)
    emit_dequant(k, io, 1, wbufs[1], qbufs, sbufs)

    for i in range(3):
        i0, i1 = 2 * i, 2 * i + 1
        wA, wB = wbufs[i0 % 3], wbufs[i1 % 3]
        k.wait("tensor", "dq_m", k.val("dq_m"))
        k.push_scope()
        ps_l = k.ps(f"psl{i}", [32, ST], F32)
        ps_m = [k.ps(f"psm{i}_{j}", [128, ST], F32) for j in range(2)]
        ps_a = k.ps(f"psa{i}", [128, ST], F32)
        ps_b = k.ps(f"psb{i}", [128, ST], F32)
        for t in range(NT):
            a = t * ST
            k.wait("tensor", "p2t1e", k.val("p2t1e"))
            for ch in range(DCH):
                k.mm(ps_l[:], laT[:, (i0 * DCH + ch) * 32:(i0 * DCH + ch + 1) * 32],
                     XBc(k, ch, a, a + ST), start=(ch == 0), stop=(ch == DCH - 1),
                     inc=("p2t1" if ch == DCH - 1 else None))
            k.wait("scalar", "p2t1", k.val("p2t1"))
            k.cp("scalar", t1s[:, 0:ST], ps_l[:], inc="p2t1e")
            k.wait("tensor", "p2t1e", k.val("p2t1e"))
            for oc in range(DCH):
                p = oc % 2
                pm = ps_m[p]
                k.wait("tensor", f"p2ae{p}", k.val(f"p2ae{p}"))
                k.wait("tensor", f"p2be{p}", k.val(f"p2be{p}"))
                for ch in range(DCH):
                    k.mm(pm[:], wA[:, ch * 1024 + oc * 128: ch * 1024 + (oc + 1) * 128],
                         XBc(k, ch, a, a + ST), start=(ch == 0), stop=False)
                k.mm(pm[:], lbT[:, i0 * 1024 + oc * 128: i0 * 1024 + (oc + 1) * 128],
                     t1s[:, 0:ST], start=False, stop=True, inc="p2a")
                k.wait("vector", "p2a", k.val("p2a"))
                vae = k.tsc("vector", Y1[:, oc * ST:(oc + 1) * ST], pm[:],
                            biasA[:, i0 * 8 + oc: i0 * 8 + oc + 1], None, ALU.add,
                            inc=f"p2ae{p}")
                k.wait("scalar", f"p2ae{p}", vae)
                k.cp("scalar", VB[:, oc * ST:(oc + 1) * ST], Y1[:, oc * ST:(oc + 1) * ST],
                     inc="p2vb")
            vvb = k.val("p2vb")
            k.wait("scalar", "p2vb", vvb)
            k.wait("tensor", "p2vb", vvb)
            if i == 0 and t == 0 and "y1" in P2DBG:
                k.wait("sync", "p2vb", vvb)
                k.dma("sync", io["dbg_y1"], Y1[:], "p2dbga")
                k.wait("gpsimd", "p2dbga", k.val("p2dbga"))
                k.wait("vector", "p2dbga", k.val("p2dbga"))
            k.push_scope()
            arow, brow, ssem, sval = emit_norm_stats(
                k, f"ln{i}_{t}", lambda ch, aa, bb: VB[:, ch * ST + aa: ch * ST + bb],
                ST, 1, ST)
            k.wait("tensor", ssem, sval)
            k.pop_scope()
            k.wait("tensor", "p2abe", k.val("p2abe"))
            k.mm(ps_a[:], k.t["ones1_f"][:], arow[0:1, :], start=True, stop=True)
            k.mm(ps_b[:], k.t["ones1_f"][:], brow[0:1, :], start=True, stop=True, inc="p2ab")
            k.wait("scalar", "p2ab", k.val("p2ab"))
            k.wait("scalar", "p2nm", k.val("p2nm"))
            k.wait("scalar", "p2na", k.val("p2na"))
            k.cp("scalar", a_sb[:], ps_a[:], inc="p2abe")
            k.cp("scalar", b_sb[:], ps_b[:], inc="p2abe")
            vabe = k.val("p2abe")
            k.wait("gpsimd", "p2abe", vabe)
            k.wait("vector", "p2abe", vabe)
            for ch in range(DCH):
                ya = Y1[:, ch * ST:(ch + 1) * ST]
                vm = k.tt("gpsimd", ya, ya, a_sb[:], ALU.mult, inc="p2nm")
                k.wait("vector", "p2nm", vm)
                va = k.tt("vector", ya, ya, b_sb[:], ALU.add, inc="p2na")
                k.wait("vector", "p2na", va)
                vg = k.tsc("vector", ya, ya, gam[:, i * 8 + ch: i * 8 + ch + 1],
                           bet[:, i * 8 + ch: i * 8 + ch + 1], ALU.mult, ALU.add, inc="p2ng")
                k.wait("scalar", "p2ng", vg)
                if i == 0 and t == 0 and "z" in P2DBG:
                    k.wait("sync", "p2ng", vg)
                    vdz = k.dma("sync", io["dbg_z"].rearrange("p (c w) -> p c w", c=DCH)[:, ch, :],
                                ya, f"p2dbgz{ch}")
                    k.wait("vector", f"p2dbgz{ch}", vdz)
                    k.wait("gpsimd", f"p2dbgz{ch}", vdz)
                k.act(G[:, ch * ST:(ch + 1) * ST], ya, AF.Gelu, inc="p2gel")
            vgel = k.val("p2gel")
            k.wait("tensor", "p2gel", vgel)
            if i == 0 and t == 0 and "g" in P2DBG:
                k.wait("sync", "p2gel", vgel)
                k.dma("sync", io["dbg_g"], G[:], "p2dbgb")
                k.wait("gpsimd", "p2dbgb", k.val("p2dbgb"))
                k.wait("tensor", "p2dbgb", k.val("p2dbgb"))
            k.wait("tensor", "p2t1e", k.val("p2t1e"))
            for ch in range(DCH):
                k.mm(ps_l[:], laT[:, (i1 * DCH + ch) * 32:(i1 * DCH + ch + 1) * 32],
                     G[:, ch * ST:(ch + 1) * ST], start=(ch == 0), stop=(ch == DCH - 1),
                     inc=("p2t1b" if ch == DCH - 1 else None))
            k.wait("scalar", "p2t1b", k.val("p2t1b"))
            k.cp("scalar", t1s[:, ST:2 * ST], ps_l[:], inc="p2t1e")
            k.wait("tensor", "p2t1e", k.val("p2t1e"))
            for oc in range(DCH):
                p = oc % 2
                pm = ps_m[p]
                k.wait("tensor", f"p2ae{p}", k.val(f"p2ae{p}"))
                k.wait("tensor", f"p2be{p}", k.val(f"p2be{p}"))
                for ch in range(DCH):
                    k.mm(pm[:], wB[:, ch * 1024 + oc * 128: ch * 1024 + (oc + 1) * 128],
                         G[:, ch * ST:(ch + 1) * ST], start=(ch == 0), stop=False)
                k.mm(pm[:], lbT[:, i1 * 1024 + oc * 128: i1 * 1024 + (oc + 1) * 128],
                     t1s[:, ST:2 * ST], start=False, stop=True, inc="p2b")
                k.wait("vector", "p2b", k.val("p2b"))
                vx = k.stt("vector", Xc(k, oc, a, a + ST), pm[:],
                           biasA[:, i1 * 8 + oc: i1 * 8 + oc + 1], Xc(k, oc, a, a + ST),
                           ALU.add, ALU.add, inc=f"p2be{p}")
                if i < 2:
                    k.wait("scalar", f"p2be{p}", vx)
                    k.act(XBc(k, oc, a, a + ST), Xc(k, oc, a, a + ST), AF.Copy, inc="p2xb")
        for s in ("p2ae0", "p2ae1", "p2be0", "p2be1", "p2t1e"):
            k.wait("tensor", s, k.val(s))
        k.pop_scope()
        if i < 2:
            ws = ("p2be0", k.val("p2be0"))
            emit_dequant(k, io, 2 * i + 2, wbufs[(2 * i + 2) % 3], qbufs, sbufs, ws[0], ws[1])
            emit_dequant(k, io, 2 * i + 3, wbufs[(2 * i + 3) % 3], qbufs, sbufs,
                         "p2be1", k.val("p2be1"))
            k.wait("tensor", "p2xb", k.val("p2xb"))
    for s in ("p2be0", "p2be1"):
        k.wait("vector", s, k.val(s))
        k.wait("gpsimd", s, k.val(s))
    k.pop_scope()
    k.push_scope()
    xrb = [k.sb(f"xrb{j}", [128, DCH * ST], F32) for j in range(2)]
    for s in ("p2be0", "p2be1"):
        k.wait("sync", s, k.val(s))
    for t in range(NT):
        a = t * ST
        b = t % 2
        if t >= 2:
            k.wait("sync", f"p2ra{b}", k.val(f"p2ra{b}"))
        vd = k.dma("sync", xrb[b][:].rearrange("p (c w) -> p c w", c=DCH),
                   io["x1ref"].rearrange("p (c n) -> p c n", c=DCH)[:, :, a:a + ST],
                   f"p2rx{b}")
        for ch in range(DCH):
            eng = ("vector", "gpsimd")[ch % 2]
            k.wait(eng, f"p2rx{b}", vd)
            k.tt(eng, Xc(k, ch, a, a + ST), Xc(k, ch, a, a + ST),
                 xrb[b][:, ch * ST:(ch + 1) * ST], ALU.add, inc=f"p2ra{b}")
    pre = [(f"p2ra0", k.val("p2ra0")), (f"p2ra1", k.val("p2ra1"))]
    k.pop_scope()
    return emit_postln(k, "pl2", pre)


def declare_io_phase2(nc, standalone):
    io = {}
    add = lambda n, s, d, kind: io.__setitem__(n, nc.dram_tensor(n + _u(), s, d, kind=kind).ap())
    if "y1" in P2DBG:
        add("dbg_y1", [128, DCH * ST], F32, "ExternalOutput")
    if "gam" in P2DBG:
        add("dbg_gam", [128, 24], F32, "ExternalOutput")
    if "z" in P2DBG:
        add("dbg_z", [128, DCH * ST], F32, "ExternalOutput")
    if "g" in P2DBG:
        add("dbg_g", [128, DCH * ST], BF16, "ExternalOutput")
    add("qTw", [6, 1024, 1024], I32, "ExternalInput")
    add("sTe", [6, 1024, 1024], F32, "ExternalInput")
    add("biasA", [128, 48], F32, "ExternalInput")
    add("laT", [6, 1024, 32], F32, "ExternalInput")
    add("lbT", [32, 6 * 1024], F32, "ExternalInput")
    add("gamma", [128, 24], F32, "ExternalInput")
    add("beta", [128, 24], F32, "ExternalInput")
    add("ones1", [1, 128], F32, "ExternalInput")
    add("onesn", [128, 1], F32, "ExternalInput")
    add("iden", [128, 128], F32, "ExternalInput")
    if standalone:
        add("xstate", [128, DCH * NCOL], F32, "ExternalInput")
        add("xout", [128, DCH * NCOL], F32, "ExternalOutput")
        io["x1ref"] = io["xstate"]
    return io


def build_phase2_standalone():
    stack = ExitStack()
    nc = bass.Bass("TRN2", target_bir_lowering=False)
    io = declare_io_phase2(nc, True)
    with stack:
        k = K(nc, stack)
        alloc_common(k, io)
        k.dma("sync", k.t["X"][:], io["xstate"], "ldx")
        k.dma("gpsimd", k.t["XB"][:], io["xstate"], "ldxb")
        for eng in ("tensor", "vector", "scalar", "gpsimd"):
            k.wait(eng, "ldx", k.val("ldx"))
            k.wait(eng, "ldxb", k.val("ldxb"))
        sem, val = emit_phase2(k, io)
        k.wait("sync", sem, val)
        k.dma("sync", io["xout"], k.t["X"][:], "out")
        k.wait("sync", "out", k.val("out"))
        k.finalize()
    return nc, io_names(io)


def phase2_in_map(P, xstate):
    return {
        "qTw": P["qTw"], "sTe": P["sTe"], "biasA": P["biasA"], "laT": P["laT"],
        "lbT": P["lbT2"], "gamma": P["gamma"], "beta": P["beta"],
        "ones1": P["ones1"], "onesn": P["onesn"], "iden": P["iden"], "xstate": xstate,
    }



def emit_phase3(k, io):
    X, XB = k.t["X"], k.t["XB"]
    ones1 = k.t["ones1_f"]
    k.push_scope()
    g_all = k.sb("g_all", [128, 16 * NCOL], BF16)
    bu = k.sb("bu3", [128, 16], F32)
    bv = k.sb("bv3", [128, 16], F32)
    bo = k.sb("bo3", [128, 8], F32)
    ze = k.sb("ze3", [128, 16], F32)
    pe = k.sb("pe3", [128, 16], F32)
    arow3 = k.sb("arow3k", [1, NCOL], BF16)
    a_sb = k.sb("a3sb", [128, ST], F32)
    xn = XB
    k.t["p3u"] = k.sb("p3u", [128, 2 * ST], F32)
    k.dma("sync", bu[:], io["bu"], "p3b")
    k.dma("sync", bv[:], io["bv"], "p3b")
    k.dma("sync", bo[:], io["bo"], "p3b")
    k.dma("sync", ze[:], io["ze"], "p3b")
    k.dma("sync", pe[:], io["pe"], "p3b")
    for eng in ("tensor", "vector", "scalar", "gpsimd"):
        k.wait(eng, "p3b", k.val("p3b"))

    k.push_scope()
    ar, _, ssem, sval = emit_norm_stats(k, "rms3", lambda ch, a, b: XBc(k, ch, a, b),
                                        NCOL, NT, ST, rms=True, eps=1e-8)
    k.wait("vector", ssem, sval)
    k.cp("vector", arow3[0:1, :], ar[0:1, :], inc="p3ar")
    k.wait("tensor", "p3ar", k.val("p3ar"))
    for s in ("ns_qs", "ns_tm", "ns_rb", "ns_rbe", "p3ar"):
        k.wait("gpsimd", s, k.val(s))
    k.pop_scope()

    ps_a = k.ps("p3psa", [128, ST], F32)
    ps_v = [k.ps(f"p3psv{j}", [128, ST], F32) for j in range(2)]

    onesb = k.t["ones1_b"]

    def emit_xn(t):
        a = t * ST
        k.wait("tensor", "p3ae", k.val("p3ae"))
        k.mm(ps_a[:], onesb[:], arow3[0:1, a:a + ST], start=True, stop=True, inc="p3abc")
        k.wait("scalar", "p3abc", k.val("p3abc"))
        k.wait("scalar", "p3xnw", k.val("p3xnw"))
        k.cp("scalar", a_sb[:], ps_a[:], inc="p3ae")
        k.wait("gpsimd", "p3ae", k.val("p3ae"))
        k.wait("gpsimd", "p3mm", k.val("p3mm"))
        for ch in range(DCH):
            k.tt("gpsimd", xn[:, ch * ST:(ch + 1) * ST], Xc(k, ch, a, a + ST), a_sb[:],
                 ALU.mult, inc="p3xnw")
        k.wait("tensor", "p3xnw", k.val("p3xnw"))

    k.push_scope()
    ww = k.sb("w3w", [128, 8 * 1024], BF16)
    for half in range(2):
        k.wait("gpsimd", "p3mx", k.val("p3mx"))
        k.wait("gpsimd", "p3mm", k.val("p3mm"))
        vw = k.dma("gpsimd", ww[:].rearrange("p (c w) -> p c w", c=DCH),
                   io["WvT"].rearrange("(c p) w -> p c w", p=128)[:, :, half * 1024:(half + 1) * 1024],
                   f"p3w{half}")
        k.wait("tensor", f"p3w{half}", vw)
        for t in range(NT):
            a = t * ST
            emit_xn(t)
            for oc in range(8):
                pv = ps_v[oc % 2]
                occ = half * 8 + oc
                k.wait("tensor", "p3ve", k.val("p3ve"))
                for ch in range(DCH):
                    k.mm(pv[:], ww[:, ch * 1024 + oc * 128: ch * 1024 + (oc + 1) * 128],
                         xn[:, ch * ST:(ch + 1) * ST], start=(ch == 0), stop=(ch == DCH - 1),
                         inc=("p3mm" if ch == DCH - 1 else None))
                k.wait("scalar", "p3mm", k.val("p3mm"))
                k.wait("scalar", f"p3mx{oc % 2}", k.val(f"p3mx{oc % 2}"))
                vsl = k.act(k.t["p3u"][:, (oc % 2) * ST:(oc % 2 + 1) * ST], pv[:], AF.Silu,
                            bias=bv[:, occ:occ + 1], inc="p3ve")
                vch = k.t["p3u"][:, (oc % 2) * ST:(oc % 2 + 1) * ST]
                gch = g_all[:, occ * NCOL + a: occ * NCOL + a + ST]
                k.wait("vector", "p3ve", vsl)
                vA = k.tsc("vector", gch, vch, ze[:, occ:occ + 1], None, ALU.mult,
                           inc=f"p3mx{oc % 2}")
                k.wait("vector", f"p3mx{oc % 2}", vA)
                k.stt("vector",
                      g_all[:, occ * NCOL + a + 1: occ * NCOL + a + ST: 2],
                      vch[:, 0:ST - 1:2], pe[:, occ:occ + 1],
                      g_all[:, occ * NCOL + a + 1: occ * NCOL + a + ST: 2],
                      ALU.mult, ALU.add, inc=f"p3mx{oc % 2}")
    for half in range(2):
        k.wait("gpsimd", "p3mx0", k.val("p3mx0"))
        k.wait("gpsimd", "p3mx1", k.val("p3mx1"))
        k.wait("gpsimd", "p3mm", k.val("p3mm"))
        vw = k.dma("gpsimd", ww[:].rearrange("p (c w) -> p c w", c=DCH),
                   io["WuT"].rearrange("(c p) w -> p c w", p=128)[:, :, half * 1024:(half + 1) * 1024],
                   f"p3wu{half}")
        k.wait("tensor", f"p3wu{half}", vw)
        for t in range(NT):
            a = t * ST
            emit_xn(t)
            for oc in range(8):
                pv = ps_v[oc % 2]
                occ = half * 8 + oc
                k.wait("tensor", "p3ve", k.val("p3ve"))
                for ch in range(DCH):
                    k.mm(pv[:], ww[:, ch * 1024 + oc * 128: ch * 1024 + (oc + 1) * 128],
                         xn[:, ch * ST:(ch + 1) * ST], start=(ch == 0), stop=(ch == DCH - 1),
                         inc=("p3mm" if ch == DCH - 1 else None))
                k.wait("scalar", "p3mm", k.val("p3mm"))
                k.wait("scalar", f"p3mx{oc % 2}", k.val(f"p3mx{oc % 2}"))
                vsl = k.act(k.t["p3u"][:, (oc % 2) * ST:(oc % 2 + 1) * ST], pv[:], AF.Silu,
                            bias=bu[:, occ:occ + 1], inc="p3ve")
                uch = k.t["p3u"][:, (oc % 2) * ST:(oc % 2 + 1) * ST]
                gch = g_all[:, occ * NCOL + a: occ * NCOL + a + ST]
                k.wait("vector", "p3ve", vsl)
                k.tt("vector", gch, gch, uch, ALU.mult, inc=f"p3mx{oc % 2}")
    k.wait("tensor", "p3mm", k.val("p3mm"))
    k.pop_scope()
    k.wait("gpsimd", "p3mx0", k.val("p3mx0"))
    k.wait("gpsimd", "p3mx1", k.val("p3mx1"))
    k.wait("gpsimd", "p3mm", k.val("p3mm"))
    wo = k.sb("w3o", [128, 16 * 1024], BF16)
    vwo = k.dma("gpsimd", wo[:].rearrange("p (c w) -> p c w", c=16),
                io["WoT"].rearrange("(c p) w -> p c w", p=128), "p3wo")
    k.wait("tensor", "p3wo", vwo)
    k.wait("tensor", "p3ve", k.val("p3ve"))
    tsb = k.t["p3u"][:, 0:ST]
    for t in range(NT):
        a = t * ST
        for oc in range(DCH):
            pv = ps_v[oc % 2]
            k.wait("tensor", "p3oe", k.val("p3oe"))
            for kc in range(16):
                k.mm(pv[:], wo[:, kc * 1024 + oc * 128: kc * 1024 + (oc + 1) * 128],
                     g_all[:, kc * NCOL + a: kc * NCOL + a + ST],
                     start=(kc == 0), stop=(kc == 15),
                     inc=("p3om" if kc == 15 else None))
            k.wait("vector", "p3om", k.val("p3om"))
            vt = k.stt("vector", tsb, pv[:], bo[:, oc:oc + 1], Xc(k, oc, a, a + ST),
                       ALU.add, ALU.add, inc="p3oe")
            k.wait("gpsimd", "p3oe", vt)
            k.tt("gpsimd", Xc(k, oc, a, a + ST), tsb, Xc(k, oc, a, a + ST), ALU.add,
                 inc="p3xo")
            k.wait("vector", "p3xo", k.val("p3xo"))
    for s in ("p3xo",):
        k.wait("tensor", s, k.val(s))
        k.wait("scalar", s, k.val(s))
    k.wait("tensor", "p3oe", k.val("p3oe"))
    pre = [("p3xo", k.val("p3xo"))]
    k.pop_scope()
    return emit_postln(k, "pl3", pre)


def declare_io_phase3(nc, standalone):
    io = {}
    add = lambda n, s, d, kind: io.__setitem__(n, nc.dram_tensor(n + _u(), s, d, kind=kind).ap())
    add("WuT", [1024, 2048], F32, "ExternalInput")
    add("WvT", [1024, 2048], F32, "ExternalInput")
    add("WoT", [2048, 1024], F32, "ExternalInput")
    add("bu", [128, 16], F32, "ExternalInput")
    add("bv", [128, 16], F32, "ExternalInput")
    add("bo", [128, 8], F32, "ExternalInput")
    add("ze", [128, 16], F32, "ExternalInput")
    add("pe", [128, 16], F32, "ExternalInput")
    add("ones1", [1, 128], F32, "ExternalInput")
    add("onesn", [128, 1], F32, "ExternalInput")
    add("iden", [128, 128], F32, "ExternalInput")
    if standalone:
        add("xstate", [128, DCH * NCOL], F32, "ExternalInput")
        add("xout", [128, DCH * NCOL], F32, "ExternalOutput")
    return io


def build_phase3_standalone():
    stack = ExitStack()
    nc = bass.Bass("TRN2", target_bir_lowering=False)
    io = declare_io_phase3(nc, True)
    with stack:
        k = K(nc, stack)
        alloc_common(k, io)
        k.dma("sync", k.t["X"][:], io["xstate"], "ldx")
        k.dma("gpsimd", k.t["XB"][:], io["xstate"], "ldxb")
        for eng in ("tensor", "vector", "scalar", "gpsimd"):
            k.wait(eng, "ldx", k.val("ldx"))
            k.wait(eng, "ldxb", k.val("ldxb"))
        sem, val = emit_phase3(k, io)
        k.wait("sync", sem, val)
        k.dma("sync", io["xout"], k.t["X"][:], "out")
        k.wait("sync", "out", k.val("out"))
        k.finalize()
    return nc, io_names(io)


def phase3_in_map(P, xstate):
    return {
        "WuT": P["WuT"], "WvT": P["WvT"], "WoT": P["WoT"], "bu": P["bu"], "bv": P["bv"],
        "bo": P["bo"], "ze": P["ze"], "pe": P["pe"],
        "ones1": P["ones1"], "onesn": P["onesn"], "iden": P["iden"], "xstate": xstate,
    }



def emit_wquant(k, io, which, wq, nch, width, kout):
    onescol = k.t["onescol_f"]
    ones1 = k.t["ones1_f"]
    iden = k.t["iden_f"]
    k.push_scope()
    wstr = [k.sb(f"wqs{which}{j}", [128, width], F32) for j in range(2)]
    parts = k.sb(f"wqp{which}", [128, nch], F32)
    scr = k.sb(f"wqc{which}", [128, width], F32)
    scri = k.sb(f"wqi{which}", [128, width], F32)[:]
    mgt = k.sb(f"wqg{which}", [128, 1], F32)
    k.memset("gpsimd", mgt[:], 12582912.0, inc="p4mg")
    k.wait("gpsimd", "p4mg", k.val("p4mg"))
    sm = k.sb(f"wqm{which}", [1, 16], F32)
    swpp = k.sb(f"wqw{which}", [128, 2], F32)
    ps_s = k.ps(f"wqps{which}", [128, 64], F32)
    src_h = io[which].rearrange("(c p) m -> p c m", p=128)
    for ch in range(nch):
        b = ch % 2
        if ch >= 2:
            k.wait("sync", "p4wa", k.val("p4wa") - 1)
        vd = k.dma("sync", wstr[b][:], src_h[:, ch, :], f"p4wd{b}")
        k.wait("vector", f"p4wd{b}", vd)
        k.red("vector", parts[:, ch:ch + 1], wstr[b][:], ALU.add, absval=True, inc="p4wa")
    k.wait("tensor", "p4wa", k.val("p4wa"))
    k.mm(ps_s[0:nch, 0:1], parts[:, 0:nch], onescol[:], start=True, stop=True, inc="p4wm")
    k.wait("vector", "p4wm", k.val("p4wm"))
    k.cp("vector", parts[0:nch, 0:1], ps_s[0:nch, 0:1], inc="p4wm2")
    k.wait("tensor", "p4wm2", k.val("p4wm2"))
    k.tr(ps_s[0:1, 4:4 + nch], parts[0:nch, 0:1], iden[0:nch, 0:nch], inc="p4wt")
    k.wait("vector", "p4wt", k.val("p4wt"))
    k.red("vector", sm[0:1, 0:1], ps_s[0:1, 4:4 + nch], ALU.add, inc="p4wr")
    k.wait("vector", "p4wr", k.val("p4wr"))
    k.tsc("vector", sm[0:1, 1:2], sm[0:1, 0:1], 1.0 / (nch * 128 * width), 1e-5,
          ALU.mult, ALU.max, inc="p4wr")
    k.wait("vector", "p4wr", k.val("p4wr"))
    k.recip(sm[0:1, 2:3], sm[0:1, 1:2], inc="p4wr")
    k.wait("tensor", "p4wr", k.val("p4wr"))
    k.mm(ps_s[:, 1:2], ones1[:], sm[0:1, 2:3], start=True, stop=True)
    k.mm(ps_s[:, 2:3], ones1[:], sm[0:1, 1:2], start=True, stop=True, inc="p4wb")
    k.wait("scalar", "p4wb", k.val("p4wb"))
    k.cp("scalar", swpp[:, 0:1], ps_s[:, 1:2], inc="p4we")
    k.cp("scalar", swpp[:, 1:2], ps_s[:, 2:3], inc="p4we")
    vwe = k.val("p4we")
    for eng in ("vector", "gpsimd", "sync"):
        k.wait(eng, "p4we", vwe)
    MAGIC = 12582912.0
    k.wait("sync", "p4wa", k.val("p4wa"))
    for ch in range(nch):
        b = ch % 2
        if ch >= 2:
            k.wait("sync", "p4q1", k.val("p4q1") - 1)
        vd = k.dma("sync", wstr[b][:], src_h[:, ch, :], f"p4we{b}")
        k.wait("vector", f"p4we{b}", vd)
        if ch >= 1:
            k.wait("vector", "p4q2", k.val("p4q2"))
        v1 = k.tsc("vector", scr[:], wstr[b][:], swpp[:, 0:1], MAGIC, ALU.mult, ALU.add,
                   inc="p4q1")
        k.wait("gpsimd", "p4q1", v1)
        if ch >= 1:
            k.wait("gpsimd", "p4q4", k.val("p4q4"))
        v2 = k.tt("gpsimd", scri, scr[:], mgt[:].broadcast_to([128, width]), ALU.subtract,
                  inc="p4q2")
        k.wait("vector", "p4q2", v2)
        v3 = k.tsc("vector", scri, scri, 1.0, -1.0, ALU.min, ALU.max, inc="p4q3")
        k.wait("vector", "p4q3", v3)
        k.tsc("vector", wq[:, ch * width:(ch + 1) * width], scri, swpp[:, 1:2], None,
              ALU.mult, inc="p4q4")
    for eng in ("sync", "gpsimd", "scalar", "tensor"):
        k.wait(eng, "p4q4", k.val("p4q4"))
    k.pop_scope()


def emit_phase4(k, io):
    MAGIC = 12582912.0
    ones1 = k.t["ones1_f"]
    iden = k.t["iden_f"]
    idenb = k.t["iden_b"]
    k.push_scope()
    w1q = k.sb("w1q", [128, DCH * DH], BF16)
    w2q = k.sb("w2q", [128, 32 * 1024], BF16)
    kq1 = k.sb("p4kq1", [128, 2], F32)
    k.memset("vector", kq1[:], 1.0 / 127.0, inc="p4kq")
    k.wait("vector", "p4kq", k.val("p4kq"))
    emit_wquant(k, io, "w1T", w1q, DCH, DH, None)
    emit_wquant(k, io, "w2T", w2q, 32, 1024, None)

    if "w1q" in P4DBG:
        k.wait("sync", "p4q4", k.val("p4q4"))
        k.dma("sync", io["dbg_w1q"], w1q[:], "p4dbga")
        k.wait("gpsimd", "p4dbga", k.val("p4dbga"))
    cw = k.sb("cw4", [128, 32 * 3], F32)
    cb4 = k.sb("cb4", [128, 32], F32)
    k.dma("sync", cw[:], io["cw"].rearrange("p c t -> p (c t)"), "p4b")
    k.dma("sync", cb4[:], io["cb"], "p4b")
    for eng in ("tensor", "vector", "scalar", "gpsimd"):
        k.wait(eng, "p4b", k.val("p4b"))

    TW = BLK
    OW = 256
    QW4 = 8 * OW
    xc = k.sb("p4x", [128, DCH * TW], F32)
    cmr = k.sb("p4cm", [1, TW], F32)
    xq = k.sb("p4xq", [128, DCH * TW], BF16)
    hbuf = k.sb("p4h", [128, 16 * TW], BF16)
    tA = k.sb("p4tA", [128, QW4], F32)
    tB = k.sb("p4tB", [128, QW4], F32)
    cbuf = k.sb("p4c", [128, 32 * OW], BF16)
    y4n = k.sb("p4yn", [128, DCH * OW], F32)
    rowq = k.sb("p4rq", [1, 2 * TW], F32)
    rowq2 = k.sb("p4rq2", [1, 2 * OW], F32)
    cmt = k.sb("p4cmt", [128, 8], F32)
    rb1 = k.sb("p4rb1", [128, 2 * TW], F32)
    rb2 = k.sb("p4rb2", [128, 2 * OW], F32)
    rb3 = k.sb("p4rb3", [128, 2 * OW], F32)
    ps_mm = [k.ps(f"p4pm{j}", [128, TW], F32) for j in range(2)]
    ps_t = k.ps("p4pt", [128, 1024], BF16)
    ps_bc = [k.ps(f"p4pb{j}", [128, TW], F32) for j in range(2)]
    k.memset("vector", cmt[:], 0.0)

    for t in range(JB):
        a = t * TW
        k.wait("sync", "p4y", k.val("p4y"))
        k.wait("sync", "p4qm", k.val("p4qm"))
        k.wait("sync", "p4cst", k.val("p4cst"))
        vx = k.dma("sync", xc[:].rearrange("p (c w) -> p c w", c=DCH),
                   io["xstate"].rearrange("p (c n) -> p c n", c=DCH)[:, :, a:a + TW], "p4xd")
        vcm = k.dma("sync", cmr[:], io["cmrow"][0:1, a:a + TW], "p4cmd")
        engs = ("scalar", "vector", "gpsimd")
        for eng in engs:
            k.wait(eng, "p4xd", vx)
            k.wait(eng, "p4hm", k.val("p4hm"))
            k.wait(eng, "ns_qs", k.val("ns_qs"))
        for ch in range(DCH):
            k.cp(engs[ch % 3], xq[:, ch * TW:(ch + 1) * TW], xc[:, ch * TW:(ch + 1) * TW],
                 inc="p4cst")
        k.wait("tensor", "p4cst", k.val("p4cst"))
        pieces = []
        off, pi = 0, 0
        while off < TW:
            w = min(128, TW - off)
            pieces.append((off, w, pi))
            k.wait("tensor", "p4red", k.val("p4red"))
            for ch in range(DCH):
                k.tr(ps_t[0:w, ch * 128:(ch + 1) * 128],
                     xq[:, ch * TW + off: ch * TW + off + w], idenb[:],
                     inc=("p4tr" if ch == DCH - 1 else None))
            k.wait("vector", "p4tr", k.val("p4tr"))
            k.red("vector", cmt[0:w, pi:pi + 1], ps_t[0:w, :], ALU.max, absval=True,
                  inc="p4red")
            off += w
            pi += 1
        npiece = pi
        k.wait("vector", "p4red", k.val("p4red"))
        k.tsc("vector", cmt[:, 4:4 + npiece], cmt[:, 0:npiece], 1e-5, None, ALU.max,
              inc="p4sc")
        k.wait("vector", "p4sc", k.val("p4sc"))
        k.recip(cmt[:, 0:npiece], cmt[:, 4:4 + npiece], inc="p4sc")
        k.wait("vector", "p4sc", k.val("p4sc"))
        k.tsc("vector", cmt[:, 0:npiece], cmt[:, 0:npiece], 127.0, None, ALU.mult,
              inc="p4sc")
        k.tsc("vector", cmt[:, 4:4 + npiece], cmt[:, 4:4 + npiece], kq1[:, 0:1], None,
              ALU.mult, inc="p4sc")
        k.wait("tensor", "p4sc", k.val("p4sc"))
        k.wait("tensor", "p4rbe", k.val("p4rbe"))
        for (off, w, pi2) in pieces:
            k.tr(ps_bc[0][0:1, off:off + w], cmt[0:w, pi2:pi2 + 1], iden[0:w, 0:w])
            k.tr(ps_bc[1][0:1, off:off + w], cmt[0:w, 4 + pi2:5 + pi2], iden[0:w, 0:w],
                 inc=("p4trr" if off + w >= TW else None))
        k.wait("scalar", "p4trr", k.val("p4trr"))
        k.wait("scalar", "p4bc", k.val("p4bc"))
        k.cp("scalar", rowq[0:1, 0:TW], ps_bc[0][0:1, 0:TW], inc="p4row")
        k.cp("scalar", rowq[0:1, TW:2 * TW], ps_bc[1][0:1, 0:TW], inc="p4row")
        k.wait("vector", "p4row", k.val("p4row"))
        k.wait("vector", "p4cmd", vcm)
        k.tt("vector", rowq[0:1, 0:TW], rowq[0:1, 0:TW], cmr[0:1, 0:TW], ALU.mult,
             inc="p4rowm")
        k.wait("tensor", "p4rowm", k.val("p4rowm"))
        k.mm(ps_bc[0][:], ones1[:], rowq[0:1, 0:TW], start=True, stop=True)
        k.mm(ps_bc[1][:], ones1[:], rowq[0:1, TW:2 * TW], start=True, stop=True, inc="p4bc")
        k.wait("scalar", "p4bc", k.val("p4bc"))
        k.wait("scalar", "p4qm", k.val("p4qm"))
        k.wait("scalar", "p4he", k.val("p4he"))
        k.cp("scalar", rb1[:, 0:TW], ps_bc[0][:], inc="p4rbe")
        k.cp("scalar", rb1[:, TW:2 * TW], ps_bc[1][:], inc="p4rbe")
        vrbe = k.val("p4rbe")
        k.wait("gpsimd", "p4rbe", vrbe)
        k.wait("vector", "p4rbe", vrbe)
        if t == 0 and "rb1" in P4DBG:
            k.wait("sync", "p4rbe", vrbe)
            k.dma("sync", io["dbg_rb1"], rb1[:], "p4dbgg")
            k.wait("gpsimd", "p4dbgg", k.val("p4dbgg"))
        for ch in range(DCH):
            s = ch % 2
            sl = tA[:, s * TW:(s + 1) * TW]
            k.wait("gpsimd", f"p4qa{s}", k.val(f"p4qa{s}"))
            if ch < 2:
                k.wait("gpsimd", "p4cv", k.val("p4cv"))
                k.wait("gpsimd", "p4qm2", k.val("p4qm2"))
            vq1 = k.tt("gpsimd", sl, xc[:, ch * TW:(ch + 1) * TW], rb1[:, 0:TW], ALU.mult,
                       inc="p4qm")
            k.wait("vector", "p4qm", vq1)
            vq2 = k.tsc("vector", sl, sl, MAGIC, None, ALU.add, inc=f"p4qa{s}")
            k.wait("vector", f"p4qa{s}", vq2)
            k.tsc("vector", xq[:, ch * TW:(ch + 1) * TW], sl, MAGIC, None, ALU.subtract,
                  inc=f"p4qa{s}")
        for s in range(2):
            k.wait("tensor", f"p4qa{s}", k.val(f"p4qa{s}"))
        if t == 0 and "xq" in P4DBG:
            k.wait("sync", "p4qa0", k.val("p4qa0"))
            k.wait("sync", "p4qa1", k.val("p4qa1"))
            k.dma("sync", io["dbg_xq"], xq[:], "p4dbgb")
            k.wait("gpsimd", "p4dbgb", k.val("p4dbgb"))
        k.wait("tensor", "p4oe", k.val("p4oe"))
        for hh in range(2):
            for hc16 in range(16):
                hc = hh * 16 + hc16
                pm = ps_mm[hc16 % 2]
                if hc16 >= 2 or hh > 0 or t > 0:
                    k.wait("tensor", "p4he", k.val("p4he") - 1)
                for ch in range(DCH):
                    k.mm(pm[:], w1q[:, ch * DH + hc * 128: ch * DH + (hc + 1) * 128],
                         xq[:, ch * TW:(ch + 1) * TW], start=(ch == 0), stop=(ch == DCH - 1),
                         inc=("p4hm" if ch == DCH - 1 else None))
                k.wait("vector", "p4hm", k.val("p4hm"))
                if hc16 < 2:
                    k.wait("vector", "p4cv", k.val("p4cv"))
                k.tt("vector", hbuf[:, hc16 * TW:(hc16 + 1) * TW], pm[:],
                     rb1[:, TW:2 * TW], ALU.mult, inc="p4he")
            vhe = k.val("p4he")
            if t == 0 and hh == 0 and "h" in P4DBG:
                k.wait("sync", "p4he", vhe)
                k.dma("sync", io["dbg_h"], hbuf[:], "p4dbgc")
                k.wait("gpsimd", "p4dbgc", k.val("p4dbgc"))
                k.wait("vector", "p4dbgc", k.val("p4dbgc"))
            for qq in range(2):
                q0 = qq * 8
                h3 = hbuf[:].rearrange("p (c w) -> p c w", c=16)[:, q0:q0 + 8, :]
                cwr = cw[:].rearrange("p (c t) -> p c t", c=32)
                cw3 = lambda tap: cwr[:, hh * 16 + q0:hh * 16 + q0 + 8, tap:tap + 1] \
                    .broadcast_to([128, 8, OW])
                cb3 = cb4[:, hh * 16 + q0:hh * 16 + q0 + 8].unsqueeze(2) \
                    .broadcast_to([128, 8, OW])
                tA3 = tA[:].rearrange("p (c w) -> p c w", c=8)
                tB3 = tB[:].rearrange("p (c w) -> p c w", c=8)
                for eng in ("vector", "gpsimd"):
                    k.wait(eng, "p4he", vhe)
                    k.wait(eng, "p4cq", k.val("p4cq"))
                    k.wait(eng, "p4qa0", k.val("p4qa0"))
                    k.wait(eng, "p4qa1", k.val("p4qa1"))
                    k.wait(eng, "p4yb", k.val("p4yb"))
                v1 = k.tt("vector", tA3, h3[:, :, 0:OW], cw3(0), ALU.mult, inc="p4cv")
                v2 = k.tt("gpsimd", tB3, h3[:, :, 2:2 + OW], cw3(1), ALU.mult, inc="p4cv")
                k.wait("vector", "p4cv", v2)
                v3 = k.tt("vector", tA3, tA3, tB3, ALU.add, inc="p4cv")
                k.wait("gpsimd", "p4cv", v3)
                v4 = k.tt("gpsimd", tB3, h3[:, :, 4:4 + OW], cw3(2), ALU.mult, inc="p4cv")
                k.wait("vector", "p4cv", v4)
                v5 = k.tt("vector", tA3, tA3, tB3, ALU.add, inc="p4cv")
                k.wait("gpsimd", "p4cv", v5)
                v6 = k.tt("gpsimd", tA3, tA3, cb3, ALU.add, inc="p4cv")
                k.wait("scalar", "p4cv", v6)
                k.wait("scalar", "p4hqw", k.val("p4hqw"))
                k.act(cbuf[:, (hh * 16 + q0) * OW:(hh * 16 + q0 + 8) * OW], tA[:], AF.Silu,
                      inc="p4cq")
        vcq = k.val("p4cq")
        if t == 0 and "c" in P4DBG:
            k.wait("sync", "p4cq", vcq)
            k.dma("sync", io["dbg_c"], cbuf[:], "p4dbgd")
            k.wait("gpsimd", "p4dbgd", k.val("p4dbgd"))
            k.wait("vector", "p4dbgd", k.val("p4dbgd"))
        k.wait("tensor", "p4cq", vcq)
        for cp2 in range(2):
            k.wait("vector", "p4sc", k.val("p4sc"))
            for qq in range(4):
                k.wait("tensor", "p4red", k.val("p4red"))
                for ci in range(8):
                    ch = 8 * qq + ci
                    k.tr(ps_t[:, ci * 128:(ci + 1) * 128],
                         cbuf[:, ch * OW + cp2 * 128: ch * OW + cp2 * 128 + 128], idenb[:],
                         inc=("p4tr" if ci == 7 else None))
                k.wait("vector", "p4tr", k.val("p4tr"))
                k.red("vector", cmt[:, qq:qq + 1], ps_t[:], ALU.max, absval=True,
                      inc="p4red")
            k.wait("vector", "p4red", k.val("p4red"))
            k.tt("vector", cmt[:, 0:1], cmt[:, 0:1], cmt[:, 1:2], ALU.max, inc="p4sc")
            k.tt("vector", cmt[:, 2:3], cmt[:, 2:3], cmt[:, 3:4], ALU.max, inc="p4sc")
            k.wait("vector", "p4sc", k.val("p4sc"))
            k.tt("vector", cmt[:, 4 + cp2:5 + cp2], cmt[:, 0:1], cmt[:, 2:3], ALU.max,
                 inc="p4sc")
        k.wait("vector", "p4sc", k.val("p4sc"))
        k.tsc("vector", cmt[:, 6:8], cmt[:, 4:6], 1e-5, None, ALU.max, inc="p4sc")
        k.wait("vector", "p4sc", k.val("p4sc"))
        k.recip(cmt[:, 4:6], cmt[:, 6:8], inc="p4sc")
        k.wait("vector", "p4sc", k.val("p4sc"))
        k.tsc("vector", cmt[:, 4:6], cmt[:, 4:6], 127.0, None, ALU.mult, inc="p4sc")
        k.tsc("vector", cmt[:, 6:8], cmt[:, 6:8], kq1[:, 1:2], None, ALU.mult, inc="p4sc")
        k.wait("tensor", "p4sc", k.val("p4sc"))
        k.wait("tensor", "p4rbe2", k.val("p4rbe2"))
        for cp2 in range(2):
            k.tr(ps_bc[0][0:1, cp2 * 128:(cp2 + 1) * 128], cmt[:, 4 + cp2:5 + cp2], iden[:])
            k.tr(ps_bc[1][0:1, cp2 * 128:(cp2 + 1) * 128], cmt[:, 6 + cp2:7 + cp2], iden[:],
                 inc=("p4trr2" if cp2 == 1 else None))
        k.wait("scalar", "p4trr2", k.val("p4trr2"))
        k.wait("scalar", "p4bc2", k.val("p4bc2"))
        k.cp("scalar", rowq2[0:1, 0:OW], ps_bc[0][0:1, 0:OW], inc="p4row2")
        k.cp("scalar", rowq2[0:1, OW:2 * OW], ps_bc[1][0:1, 0:OW], inc="p4row2")
        k.wait("tensor", "p4row2", k.val("p4row2"))
        k.mm(ps_bc[0][:, 0:OW], ones1[:], rowq2[0:1, 0:OW], start=True, stop=True)
        k.mm(ps_bc[1][:, 0:OW], ones1[:], rowq2[0:1, OW:2 * OW], start=True, stop=True,
             inc="p4bc2")
        k.wait("scalar", "p4bc2", k.val("p4bc2"))
        k.wait("scalar", "p4qm2", k.val("p4qm2"))
        k.wait("scalar", "p4oe", k.val("p4oe"))
        k.cp("scalar", rb2[:, 0:OW], ps_bc[0][:, 0:OW], inc="p4rbe2")
        k.cp("scalar", rb2[:, OW:2 * OW], ps_bc[1][:, 0:OW], inc="p4rbe2")
        vrbe2 = k.val("p4rbe2")
        k.wait("gpsimd", "p4rbe2", vrbe2)
        k.wait("vector", "p4rbe2", vrbe2)
        for qq in range(4):
            q0 = qq * 8
            r4b = rb2[:, 0:OW].unsqueeze(1).broadcast_to([128, 8, OW])
            c3 = cbuf[:].rearrange("p (c w) -> p c w", c=32)[:, q0:q0 + 8, :]
            tA3 = tA[:].rearrange("p (c w) -> p c w", c=8)
            k.wait("gpsimd", "p4cq", vcq)
            k.wait("gpsimd", "p4hqw", k.val("p4hqw"))
            vq1 = k.tt("gpsimd", tA3, c3, r4b, ALU.mult, inc="p4qm2")
            k.wait("vector", "p4qm2", vq1)
            vq2 = k.tsc("vector", tA[:], tA[:], MAGIC, None, ALU.add, inc="p4hqw")
            k.wait("vector", "p4hqw", vq2)
            k.tsc("vector", c3, tA3, MAGIC, None, ALU.subtract, inc="p4hqw")
        k.wait("tensor", "p4hqw", k.val("p4hqw"))
        if t == 0 and "hq" in P4DBG:
            k.wait("sync", "p4hqw", k.val("p4hqw"))
            k.dma("sync", io["dbg_hq"], cbuf[:], "p4dbge")
            k.wait("gpsimd", "p4dbge", k.val("p4dbge"))
            k.wait("tensor", "p4dbge", k.val("p4dbge"))
        for oc in range(DCH):
            pm = ps_mm[oc % 2]
            if oc >= 2:
                k.wait("tensor", "p4oe", k.val("p4oe") - 1)
            for hc in range(32):
                k.mm(pm[:, 0:OW], w2q[:, hc * 1024 + oc * 128: hc * 1024 + (oc + 1) * 128],
                     cbuf[:, hc * OW:(hc + 1) * OW], start=(hc == 0), stop=(hc == 31),
                     inc=("p4om" if hc == 31 else None))
            k.wait("vector", "p4om", k.val("p4om"))
            k.wait("vector", "p4yb", k.val("p4yb"))
            v1 = k.tt("vector", tB[:, 0:OW], pm[:, 0:OW], rb2[:, OW:2 * OW], ALU.mult,
                      inc="p4oe")
            k.wait("gpsimd", "p4oe", v1)
            k.wait("gpsimd", "p4fa", k.val("p4fa"))
            k.wait("gpsimd", "p4out", k.val("p4out"))
            k.tt("gpsimd",
                 y4n[:].rearrange("p (c n w) -> p c n w", c=DCH, n=2)[:, oc],
                 tB[:, 0:OW].rearrange("p (w n) -> p n w", n=2),
                 xc[:, oc * TW + 2: oc * TW + 2 + OW].rearrange("p (w n) -> p n w", n=2),
                 ALU.add, inc="p4y")
            k.wait("vector", "p4y", k.val("p4y"))
            k.nopi("vector", "p4yb")
        vy = k.val("p4y")
        if t == 0 and "y" in P4DBG:
            k.wait("sync", "p4y", vy)
            k.dma("sync", io["dbg_y"], y4n[:], "p4dbgf")
            k.wait("gpsimd", "p4dbgf", k.val("p4dbgf"))
            k.wait("scalar", "p4dbgf", k.val("p4dbgf"))
            k.wait("vector", "p4dbgf", k.val("p4dbgf"))
        k.wait("tensor", "p4y", vy)
        engs = ("scalar", "vector", "gpsimd")
        for eng in engs:
            k.wait(eng, "p4y", vy)
        for ch in range(DCH):
            k.cp(engs[ch % 3], xq[:, ch * OW:(ch + 1) * OW], y4n[:, ch * OW:(ch + 1) * OW],
                 inc="p4yc")
        vyc = k.val("p4yc")
        k.wait("scalar", "p4yc", vyc)
        k.wait("tensor", "p4yc", vyc)
        k.push_scope()
        arow, brow, ssem, sval = emit_norm_stats(
            k, f"p4s{t}", lambda ch, aa, bb: xq[:, ch * OW + aa: ch * OW + bb], OW, 1, OW)
        k.wait("tensor", ssem, sval)
        k.pop_scope()
        k.wait("tensor", "p4rbe3", k.val("p4rbe3"))
        k.mm(ps_bc[0][:, 0:OW], ones1[:], arow[0:1, :], start=True, stop=True)
        k.mm(ps_bc[1][:, 0:OW], ones1[:], brow[0:1, :], start=True, stop=True, inc="p4bc3")
        k.wait("scalar", "p4bc3", k.val("p4bc3"))
        k.wait("scalar", "p4fm", k.val("p4fm"))
        k.wait("scalar", "p4fa", k.val("p4fa"))
        k.cp("scalar", rb3[:, 0:OW], ps_bc[0][:, 0:OW], inc="p4rbe3")
        k.cp("scalar", rb3[:, OW:2 * OW], ps_bc[1][:, 0:OW], inc="p4rbe3")
        vab = k.val("p4rbe3")
        for eng in ("vector", "gpsimd"):
            k.wait(eng, "p4rbe3", vab)
        for ch in range(DCH):
            ya = y4n[:, ch * OW:(ch + 1) * OW]
            vm = k.tt("gpsimd", ya, ya, rb3[:, 0:OW], ALU.mult, inc="p4fm")
            k.wait("vector", "p4fm", vm)
            k.tt("vector", ya, ya, rb3[:, OW:2 * OW], ALU.add, inc="p4fa")
        vfn = k.val("p4fa")
        k.wait("sync", "p4fa", vfn)
        k.dma("sync",
              io["yout"].rearrange("p (c n j w) -> p (c n) j w", c=DCH, n=2, j=JB, w=128)[:, :, t, :],
              y4n[:].rearrange("p (cn w) -> p cn w", w=128), "p4out")
    k.wait("sync", "p4out", k.val("p4out"))
    k.pop_scope()
    return "p4out", k.val("p4out")


def declare_io_phase4(nc, standalone):
    io = {}
    add = lambda n, s, d, kind: io.__setitem__(n, nc.dram_tensor(n + _u(), s, d, kind=kind).ap())
    for nm, shp, dt in (("dbg_w1q", [128, DCH * DH], BF16), ("dbg_xq", [128, DCH * BLK], BF16),
                        ("dbg_rb1", [128, 2 * BLK], F32),
                        ("dbg_h", [128, 16 * BLK], BF16), ("dbg_c", [128, 32 * 256], BF16),
                        ("dbg_hq", [128, 32 * 256], BF16), ("dbg_y", [128, DCH * 256], F32)):
        if nm[4:] in P4DBG:
            add(nm, shp, dt, "ExternalOutput")
    add("w1T", [1024, DH], F32, "ExternalInput")
    add("w2T", [DH, 1024], F32, "ExternalInput")
    add("cw", [128, 32, 3], F32, "ExternalInput")
    add("cb", [128, 32], F32, "ExternalInput")
    add("cmrow", [1, NCOL], F32, "ExternalInput")
    add("ones1", [1, 128], F32, "ExternalInput")
    add("onesn", [128, 1], F32, "ExternalInput")
    add("iden", [128, 128], F32, "ExternalInput")
    add("yout", [128, DCH * NB * JB * 128], F32, "ExternalOutput")
    if standalone:
        add("xstate", [128, DCH * NCOL], F32, "ExternalInput")
    return io


def build_phase4_standalone():
    stack = ExitStack()
    nc = bass.Bass("TRN2", target_bir_lowering=False)
    io = declare_io_phase4(nc, True)
    with stack:
        k = K(nc, stack)
        alloc_common(k, io, stream=False)
        sem, val = emit_phase4(k, io)
        k.wait("sync", sem, val)
        k.finalize()
    return nc, io_names(io)


def phase4_in_map(P, Q, xstate):
    return {
        "w1T": P["w1T"], "w2T": P["w2T"], "cw": P["cw"], "cb": P["cb"], "cmrow": Q["cmrow"],
        "ones1": P["ones1"], "onesn": P["onesn"], "iden": P["iden"], "xstate": xstate,
    }



_BUILD_CACHE = {}


def _built(name, fn):
    if name not in _BUILD_CACHE:
        _BUILD_CACHE[name] = fn()
    return _BUILD_CACHE[name]


def host_assemble_yout(outs):
    y = np.empty((1024, 2, TSEQ), dtype=np.float32)
    for c in range(NCORES):
        o = np.asarray(outs[c], np.float32).reshape(128, DCH, 2, JB, 128)
        o = o.transpose(1, 0, 2, 3, 4).reshape(1024, 2, JB, 128)
        for j in range(JB):
            y[:, :, 1024 * j + 128 * c:1024 * j + 128 * c + 128] = o[:, :, j, :]
    return np.ascontiguousarray(y.transpose(1, 2, 0))


def kernel_unfused(**inputs):
    P = host_prep(inputs)
    Qs = [host_core_inputs(P, c) for c in range(NCORES)]

    nc1, n1 = _built("p1", build_phase1_standalone)
    maps = [{n1[kk]: v for kk, v in phase1_in_map(P, Qs[c]).items()} for c in range(NCORES)]
    r1 = run_bass_kernel_spmd(nc1, maps, core_ids=list(range(NCORES)))
    xs = [np.asarray(r1.results[c][n1["xout"]]) for c in range(NCORES)]

    nc2, n2 = _built("p2", build_phase2_standalone)
    maps = [{n2[kk]: v for kk, v in phase2_in_map(P, xs[c]).items()} for c in range(NCORES)]
    r2 = run_bass_kernel_spmd(nc2, maps, core_ids=list(range(NCORES)))
    xs = [np.asarray(r2.results[c][n2["xout"]]) for c in range(NCORES)]

    nc3, n3 = _built("p3", build_phase3_standalone)
    maps = [{n3[kk]: v for kk, v in phase3_in_map(P, xs[c]).items()} for c in range(NCORES)]
    r3 = run_bass_kernel_spmd(nc3, maps, core_ids=list(range(NCORES)))
    xs = [np.asarray(r3.results[c][n3["xout"]]) for c in range(NCORES)]

    nc4, n4 = _built("p4", build_phase4_standalone)
    maps = [{n4[kk]: v for kk, v in phase4_in_map(P, Qs[c], xs[c]).items()} for c in range(NCORES)]
    r4 = run_bass_kernel_spmd(nc4, maps, core_ids=list(range(NCORES)))
    return host_assemble_yout([r4.results[c][n4["yout"]] for c in range(NCORES)])



def declare_io_fused(nc):
    io = {}
    add = lambda n, s, d, kind: io.__setitem__(n, nc.dram_tensor(n + _u(), s, d, kind=kind).ap())
    add("xin", [1024, NCOL], F32, "ExternalInput")
    add("xT", [1024, NB, TSEQ], F32, "ExternalInput")
    add("qkvT", [1024, 192], F32, "ExternalInput")
    add("bq", [64, 1], F32, "ExternalInput")
    add("bkv", [128, 1], F32, "ExternalInput")
    add("projT", [1024, 1024], F32, "ExternalInput")
    add("proj_b", [128, 8], F32, "ExternalInput")
    add("efold", [1024, 64], F32, "ExternalInput")
    add("mask", [128, 8 * CW], F32, "ExternalInput")
    add("ones1", [1, 128], F32, "ExternalInput")
    add("onesn", [128, 1], F32, "ExternalInput")
    add("iden", [128, 128], F32, "ExternalInput")
    add("qTw", [6, 1024, 1024], I32, "ExternalInput")
    add("sTe", [6, 1024, 1024], F32, "ExternalInput")
    add("biasA", [128, 48], F32, "ExternalInput")
    add("laT", [6, 1024, 32], F32, "ExternalInput")
    add("lbT", [32, 6 * 1024], F32, "ExternalInput")
    add("gamma", [128, 24], F32, "ExternalInput")
    add("beta", [128, 24], F32, "ExternalInput")
    add("WuT", [1024, 2048], F32, "ExternalInput")
    add("WvT", [1024, 2048], F32, "ExternalInput")
    add("WoT", [2048, 1024], F32, "ExternalInput")
    add("bu", [128, 16], F32, "ExternalInput")
    add("bv", [128, 16], F32, "ExternalInput")
    add("bo", [128, 8], F32, "ExternalInput")
    add("ze", [128, 16], F32, "ExternalInput")
    add("pe", [128, 16], F32, "ExternalInput")
    add("w1T", [1024, DH], F32, "ExternalInput")
    add("w2T", [DH, 1024], F32, "ExternalInput")
    add("cw", [128, 32, 3], F32, "ExternalInput")
    add("cb", [128, 32], F32, "ExternalInput")
    add("cmrow", [1, NCOL], F32, "ExternalInput")
    add("yout", [128, DCH * NB * JB * 128], F32, "ExternalOutput")
    io["xspill"] = nc.dram_tensor("xspill" + _u(), [128, DCH * NCOL], F32).ap()
    return io


def _phase_gate(k, sem, val):
    for eng in K.ENGINES:
        k.wait(eng, sem, val)


def build_fused():
    stack = ExitStack()
    nc = bass.Bass("TRN2", target_bir_lowering=False)
    io = declare_io_fused(nc)
    with stack:
        k = K(nc, stack)
        alloc_common(k, io, stream=False)
        k.push_scope()
        k.t["X"] = k.sb("X", [128, DCH * NCOL], F32)
        k.t["XB"] = k.sb("XB", [128, DCH * NCOL], BF16)
        emit_load_stream(k, io)
        s1, v1 = emit_phase1(k, io)
        _phase_gate(k, s1, v1)
        k.release_sems(("p1", "ldx"))
        k.dma("sync", io["xspill"], k.t["X"][:], "fsp1")
        k.wait("sync", "fsp1", k.val("fsp1"))
        io["x1ref"] = io["xspill"]
        s2, v2 = emit_phase2(k, io)
        _phase_gate(k, s2, v2)
        k.release_sems(("p2", "dq_", "fsp1"))
        s3, v3 = emit_phase3(k, io)
        _phase_gate(k, s3, v3)
        k.release_sems(("p3",))
        k.dma("sync", io["xspill"], k.t["X"][:], "fsp2")
        k.wait("sync", "fsp2", k.val("fsp2"))
        _phase_gate(k, "fsp2", k.val("fsp2"))
        k.pop_scope()
        io["xstate"] = io["xspill"]
        s4, v4 = emit_phase4(k, io)
        k.wait("sync", s4, v4)
        k.finalize()
    return nc, io_names(io)


def fused_in_map(P, Q):
    return {
        "xin": Q["xin"], "xT": P["xT"], "qkvT": P["qkvT"], "bq": P["bq"], "bkv": P["bkv"],
        "projT": P["projT"], "proj_b": P["proj_b"], "efold": P["efold"], "mask": Q["mask"],
        "ones1": P["ones1"], "onesn": P["onesn"], "iden": P["iden"],
        "qTw": P["qTw"], "sTe": P["sTe"], "biasA": P["biasA"], "laT": P["laT"],
        "lbT": P["lbT2"], "gamma": P["gamma"], "beta": P["beta"],
        "WuT": P["WuT"], "WvT": P["WvT"], "WoT": P["WoT"], "bu": P["bu"], "bv": P["bv"],
        "bo": P["bo"], "ze": P["ze"], "pe": P["pe"],
        "w1T": P["w1T"], "w2T": P["w2T"], "cw": P["cw"], "cb": P["cb"], "cmrow": Q["cmrow"],
    }


FUSED = True


def kernel_fused(**inputs):
    P = host_prep(inputs)
    Qs = [host_core_inputs(P, c) for c in range(NCORES)]
    nc, nm = _built("fused", build_fused)
    maps = [{nm[kk]: v for kk, v in fused_in_map(P, Qs[c]).items()} for c in range(NCORES)]
    r = run_bass_kernel_spmd(nc, maps, core_ids=list(range(NCORES)))
    return host_assemble_yout([r.results[c][nm["yout"]] for c in range(NCORES)])


def kernel(**inputs):
    if FUSED:
        return kernel_fused(**inputs)
    return kernel_unfused(**inputs)


# version 05
# speedup vs baseline: 1.0202x; 1.0009x over previous
import numpy as np
import ml_dtypes
from contextlib import ExitStack
import concourse.bass as bass
import concourse.mybir as mybir
from concourse.bass_utils import run_bass_kernel_spmd

F32 = mybir.dt.float32
BF16 = mybir.dt.bfloat16
I32 = mybir.dt.int32
AF = mybir.ActivationFunctionType
ALU = mybir.AluOpType
AXX = mybir.AxisListType.X

NCORES = 8
D = 1024
DCH = 8
TSEQ = 8192
NB = 2
JB = 8
CW = 130
BLK = 2 * CW
NCOL = JB * BLK
QW = JB * CW
ST = 416
NT = NCOL // ST
QT = 520
HS = 64
D1 = 2048
DH = 4096
NKC = TSEQ // 128
KC_LIM = None
P4DBG = ()
P2DBG = ()
SKIP_SV = False
N_LIM = None

_UID = [0]


def _u():
    _UID[0] += 1
    return f"_{_UID[0]}"


class K:
    ENGINES = ("sync", "tensor", "vector", "scalar", "gpsimd")

    def __init__(self, nc, stack):
        self.nc = nc
        self.root = stack
        self.scopes = [stack]
        self.ops = {e: [] for e in self.ENGINES}
        self.sems = {}
        self.uid = _u()
        self.t = {}

    def push_scope(self):
        s = ExitStack()
        self.scopes.append(s)
        return s

    def pop_scope(self):
        self.scopes.pop().close()

    def sb(self, name, shape, dtype):
        return self.scopes[-1].enter_context(self.nc.sbuf_tensor(name + self.uid, shape, dtype))

    def ps(self, name, shape, dtype=F32):
        return self.scopes[-1].enter_context(self.nc.psum_tensor(name + self.uid, shape, dtype))

    def _sem(self, name, fresh=False):
        if name not in self.sems:
            if not fresh and getattr(self, "sem_pool", None):
                self.sems[name] = self.sem_pool.pop()
            else:
                h = self.root.enter_context(self.nc.semaphore(name + self.uid))
                self.sems[name] = [h, 0]
        return self.sems[name]

    def release_sems(self, prefixes):
        if not hasattr(self, "sem_pool"):
            self.sem_pool = []
        for n in list(self.sems):
            if any(n.startswith(p) for p in prefixes):
                if n in getattr(self, "swdge_names", set()):
                    continue
                self.sem_pool.append(self.sems.pop(n))

    def val(self, name):
        return self._sem(name)[1]

    def _post(self, eng, fn, inc, inck):
        if inc is None:
            self.ops[eng].append(fn)
            return None
        s = self._sem(inc)
        s[1] += inck
        h = s[0]
        self.ops[eng].append(lambda e, fn=fn, h=h, k=inck: fn(e).then_inc(h, k))
        return s[1]

    def wait(self, eng, sem, v):
        if not v:
            return
        h = self._sem(sem)[0]
        assert v <= self._sem(sem)[1], f"wait beyond emitted incs: {sem} {v}"
        self.ops[eng].append(lambda e, h=h, v=v: e.wait_ge(h, v))

    def dma(self, q, out, in_, inc, **kw):
        if q == "gpsimd" and inc is not None:
            self._sem(inc, fresh=True)
            if not hasattr(self, "swdge_names"):
                self.swdge_names = set()
            self.swdge_names.add(inc)
        return self._post(q, lambda e, o=out, i=in_, kw=kw: e.dma_start(o, i, **kw), inc, 16)

    def mm(self, out, lhsT, rhs, start, stop, inc=None):
        return self._post("tensor", lambda e, o=out, l=lhsT, r=rhs, s=start, p=stop:
                          e.matmul(o, l, r, start=s, stop=p), inc, 1)

    def tr(self, out, in_, iden, inc=None):
        return self._post("tensor", lambda e, o=out, i=in_, d=iden: e.transpose(o, i, d), inc, 1)

    def cp(self, eng, out, in_, inc=None):
        if eng == "scalar":
            return self._post(eng, lambda e, o=out, i=in_: e.copy(o, i), inc, 1)
        return self._post(eng, lambda e, o=out, i=in_: e.tensor_copy(o, i), inc, 1)

    def act(self, out, in_, func, bias=0.0, scale=1.0, inc=None):
        return self._post("scalar", lambda e, o=out, i=in_, f=func, b=bias, s=scale:
                          e.activation(o, i, f, bias=b, scale=s), inc, 1)

    def tt(self, eng, out, in0, in1, alu, inc=None):
        return self._post(eng, lambda e, o=out, a=in0, b=in1, p=alu:
                          e.tensor_tensor(o, a, b, p), inc, 1)

    def tsc(self, eng, out, in0, s1, s2, op0, op1=ALU.bypass, inc=None):
        return self._post(eng, lambda e, o=out, i=in0, s1=s1, s2=s2, p0=op0, p1=op1:
                          e.tensor_scalar(o, i, s1, s2, p0, p1), inc, 1)

    def stt(self, eng, out, in0, scalar, in1, op0, op1, inc=None):
        return self._post(eng, lambda e, o=out, a=in0, s=scalar, b=in1, p0=op0, p1=op1:
                          e.scalar_tensor_tensor(o, a, s, b, p0, p1), inc, 1)

    def red(self, eng, out, in_, alu, absval=False, inc=None):
        return self._post(eng, lambda e, o=out, i=in_, p=alu, a=absval:
                          e.tensor_reduce(o, i, AXX, p, apply_absolute_value=a), inc, 1)

    def recip(self, out, in_, inc=None):
        return self._post("vector", lambda e, o=out, i=in_: e.reciprocal(o, i), inc, 1)

    def memset(self, eng, ap, v, inc=None):
        return self._post(eng, lambda e, a=ap, v=v: e.memset(a, v), inc, 1)

    def nopi(self, eng, inc):
        return self._post(eng, lambda e: e.nop(), inc, 1)

    def barrier(self, tag):
        s = self._sem("bar_" + tag)
        h = s[0]
        s[1] += len(self.ENGINES)
        tgt = s[1]
        for eng in self.ENGINES:
            self.ops[eng].append(lambda e, h=h: e.nop().then_inc(h, 1))
        for eng in self.ENGINES:
            self.ops[eng].append(lambda e, h=h, t=tgt: e.wait_ge(h, t))

    def finalize(self):
        while len(self.scopes) > 1:
            self.pop_scope()
        with self.nc.Block() as block:
            for eng in self.ENGINES:
                getattr(block, eng)(lambda e, eng=eng: [f(e) for f in self.ops[eng]])



def _f32c(x):
    return np.ascontiguousarray(np.asarray(x), dtype=np.float32)


def host_prep(inputs):
    P = {}
    x = np.asarray(inputs["L0_attn_ids"])
    P["xT"] = _f32c(x.transpose(2, 0, 1))
    P["qkvT"] = _f32c(np.asarray(inputs["L0_attn_qkv_w"]).T)
    qb = _f32c(inputs["L0_attn_qkv_b"])
    P["bq"] = qb[0:64].reshape(64, 1).copy()
    P["bkv"] = qb[64:192].reshape(128, 1).copy()
    P["projT"] = _f32c(np.asarray(inputs["L0_attn_proj_w"]).T)
    P["proj_b"] = _f32c(inputs["L0_attn_proj_b"]).reshape(8, 128).T.copy()
    E = np.zeros((1024, 64), dtype=np.float32)
    E[np.arange(1024), np.arange(1024) % 64] = 1.0
    P["efold"] = E
    P["ones1"] = np.ones((1, 128), dtype=np.float32)
    P["onesn"] = np.full((128, 1), 1.0 / 1024.0, dtype=np.float32)
    P["iden"] = np.eye(128, dtype=np.float32)
    P["qTw"] = np.ascontiguousarray(np.asarray(inputs["L0_ffn_qweights"]).transpose(0, 2, 1))
    P["sTe"] = _f32c(np.repeat(np.asarray(inputs["L0_ffn_scales"]), 16, axis=2).transpose(0, 2, 1))
    P["biasA"] = np.ascontiguousarray(
        _f32c(inputs["L0_ffn_biases"]).reshape(6, 8, 128).transpose(2, 0, 1).reshape(128, 48))
    P["laT"] = _f32c(np.asarray(inputs["L0_ffn_lora_a"]).transpose(0, 2, 1))
    P["lbT"] = _f32c(np.asarray(inputs["L0_ffn_lora_b"]).transpose(0, 2, 1))
    P["lbT2"] = np.ascontiguousarray(P["lbT"].transpose(1, 0, 2).reshape(32, 6 * 1024))
    P["gamma"] = np.ascontiguousarray(
        _f32c(inputs["L0_ffn_ln_gamma"]).reshape(3, 8, 128).transpose(2, 0, 1).reshape(128, 24))
    P["beta"] = np.ascontiguousarray(
        _f32c(inputs["L0_ffn_ln_beta"]).reshape(3, 8, 128).transpose(2, 0, 1).reshape(128, 24))
    P["WuT"] = _f32c(np.asarray(inputs["L1_attn_Wu"]).T)
    P["WvT"] = _f32c(np.asarray(inputs["L1_attn_Wv"]).T)
    P["WoT"] = _f32c(np.asarray(inputs["L1_attn_Wo"]).T)
    P["bu"] = _f32c(inputs["L1_attn_bu"]).reshape(16, 128).T.copy()
    P["bv"] = _f32c(inputs["L1_attn_bv"]).reshape(16, 128).T.copy()
    P["bo"] = _f32c(inputs["L1_attn_bo"]).reshape(8, 128).T.copy()
    P["ze"] = _f32c(np.repeat(np.asarray(inputs["L1_attn_zero"])[:, 0], 256)).reshape(16, 128).T.copy()
    P["pe"] = _f32c(np.repeat(np.asarray(inputs["L1_attn_pos"])[:, 0], 256)).reshape(16, 128).T.copy()
    P["w1T"] = _f32c(np.asarray(inputs["L1_ffn_w1"]).T)
    P["w2T"] = _f32c(np.asarray(inputs["L1_ffn_w2"]).T)
    cw = _f32c(np.asarray(inputs["L1_ffn_conv_w"])[:, 0, :])
    P["cw"] = cw.reshape(32, 128, 3).transpose(1, 0, 2).copy()
    P["cb"] = _f32c(inputs["L1_ffn_conv_b"]).reshape(32, 128).T.copy()
    return P


def host_core_inputs(P, c):
    Q = {}
    xT = P["xT"]
    pos = np.empty((JB, CW), dtype=np.int64)
    for j in range(JB):
        pos[j] = 1024 * j + 128 * c + np.arange(CW) - 1
    posc = np.clip(pos, 0, TSEQ - 1)
    g = xT[:, :, posc.reshape(-1)].reshape(1024, 2, JB, CW)
    xin = np.zeros((1024, JB, CW, 2), dtype=np.float32)
    xin[:, :, :, 0] = g[:, 0]
    xin[:, :, :, 1] = g[:, 1]
    bad = (pos < 0) | (pos >= TSEQ)
    xin[:, bad, :] = 0.0
    Q["xin"] = np.ascontiguousarray(xin.reshape(1024, NCOL))
    m = np.arange(8)[:, None, None] * 128 + np.arange(128)[None, :, None]
    qq = 128 * c + np.arange(CW)[None, None, :] - 1
    Q["mask"] = np.ascontiguousarray((m <= qq).astype(np.float32).transpose(1, 0, 2).reshape(128, 8 * CW))
    cm = np.ones((1, NCOL), dtype=np.float32)
    for j in range(JB):
        for u in (0, CW - 1):
            for n in range(2):
                if pos[j, u] < 0 or pos[j, u] >= TSEQ:
                    cm[0, BLK * j + 2 * u + n] = 0.0
    Q["cmrow"] = cm
    return Q


def host_assemble(outs):
    y = np.empty((1024, 2, TSEQ), dtype=np.float32)
    for c in range(NCORES):
        o = np.asarray(outs[c]).reshape(1024, 2, JB, 128)
        for j in range(JB):
            y[:, :, 1024 * j + 128 * c:1024 * j + 128 * c + 128] = o[:, :, j, :]
    return np.ascontiguousarray(y.transpose(1, 2, 0))



def alloc_common(k, io, stream=True):
    if stream:
        k.t["X"] = k.sb("X", [128, DCH * NCOL], F32)
        k.t["XB"] = k.sb("XB", [128, DCH * NCOL], BF16)
    k.t["ones1_f"] = k.sb("ones1f", [1, 128], F32)
    k.t["onescol_f"] = k.sb("onescolf", [128, 1], F32)
    k.t["onesn_bf"] = k.sb("onesnb", [128, 1], BF16)
    k.t["iden_f"] = k.sb("idenf", [128, 128], F32)
    k.t["iden_b"] = k.sb("idenb", [128, 128], BF16)
    k.t["eps_ln"] = k.sb("epsln", [128, 1], F32)
    k.t["neg1"] = k.sb("neg1", [128, 1], F32)
    k.t["ones1_b"] = k.sb("ones1b", [1, 128], BF16)
    k.dma("sync", k.t["ones1_f"][:], io["ones1"], "cst")
    k.dma("gpsimd", k.t["onesn_bf"][:], io["onesn"], "cstg")
    k.dma("gpsimd", k.t["ones1_b"][:], io["ones1"], "cstg")
    k.dma("sync", k.t["iden_f"][:], io["iden"], "cst")
    k.dma("gpsimd", k.t["iden_b"][:], io["iden"], "cstg")
    k.memset("vector", k.t["eps_ln"][:], 1e-5, inc="cstv")
    k.memset("vector", k.t["neg1"][:], -1.0, inc="cstv")
    k.memset("vector", k.t["onescol_f"][:], 1.0, inc="cstv")
    for eng in ("tensor", "vector", "scalar"):
        k.wait(eng, "cst", k.val("cst"))
        k.wait(eng, "cstg", k.val("cstg"))
        k.wait(eng, "cstv", k.val("cstv"))


def Xc(k, ch, a, b, step=1):
    return k.t["X"][:, ch * NCOL + a: ch * NCOL + b: step] if step != 1 else \
        k.t["X"][:, ch * NCOL + a: ch * NCOL + b]


def XBc(k, ch, a, b):
    return k.t["XB"][:, ch * NCOL + a: ch * NCOL + b]


def emit_load_stream(k, io):
    X, XB = k.t["X"], k.t["XB"]
    src = io["xin"].rearrange("(c p) n -> p c n", p=128)
    k.dma("sync", X[:].rearrange("p (c n) -> p c n", c=DCH), src, "ldx")
    k.dma("gpsimd", XB[:].rearrange("p (c n) -> p c n", c=DCH), src, "ldxb")
    for eng in ("tensor", "vector", "scalar", "gpsimd"):
        k.wait(eng, "ldx", k.val("ldx"))
        k.wait(eng, "ldxb", k.val("ldxb"))


def emit_norm_stats(k, tag, src_bf, ncol, nt, tw, nch=DCH, rms=False, eps=1e-5):
    onesn = k.t["onesn_bf"]
    iden = k.t["iden_f"]
    ntr = (ncol + 127) // 128
    sq_scr = k.sb(f"sqs_{tag}", [128, 2 * tw], BF16)
    ps_s = k.ps(f"pss_{tag}", [1, 512], F32)
    ps_m = k.ps(f"psm_{tag}", [1, 512], F32) if not rms else None
    srow = k.sb(f"srow_{tag}", [1, ncol], F32)
    mrow = k.sb(f"mrow_{tag}", [1, ncol], F32) if not rms else None
    ps_tm = k.ps(f"pstm_{tag}", [128, 2 * ntr], F32)
    tmm = k.sb(f"tmm_{tag}", [128, 4 * ntr], F32)
    ps_rb = ps_s
    ps_rb2 = ps_m if not rms else ps_s
    arow = k.sb(f"arow_{tag}", [1, ncol], F32)
    brow = k.sb(f"brow_{tag}", [1, ncol], F32) if not rms else None

    qs0 = k.val("ns_qs")
    sv0 = k.val("ns_sqv")
    k.wait("scalar", "ns_qs", qs0)
    k.wait("vector", "ns_rb", k.val("ns_rb"))
    k.wait("vector", "ns_tm", k.val("ns_tm"))
    isq = 0
    for t in range(nt):
        a = t * tw
        for ch in range(nch):
            half = isq % 2
            scr = sq_scr[:, half * tw:(half + 1) * tw]
            if isq >= 2:
                k.wait("scalar", "ns_qs", qs0 + isq - 1)
            k.act(scr, src_bf(ch, a, a + tw), AF.Square, inc="ns_sqv")
            if not rms:
                k.mm(ps_m[:, 0:tw], onesn[:], src_bf(ch, a, a + tw),
                     start=(ch == 0), stop=(ch == nch - 1))
            k.wait("tensor", "ns_sqv", sv0 + isq + 1)
            k.mm(ps_s[:, 0:tw], onesn[:], scr, start=(ch == 0), stop=(ch == nch - 1),
                 inc="ns_qs")
            isq += 1
        k.wait("vector", "ns_qs", k.val("ns_qs"))
        k.cp("vector", srow[0:1, a:a + tw], ps_s[:, 0:tw], inc="ns_rows")
        if not rms:
            k.cp("vector", mrow[0:1, a:a + tw], ps_m[:, 0:tw], inc="ns_rows")
        k.wait("tensor", "ns_rows", k.val("ns_rows"))
    k.memset("vector", ps_tm[:], 0.0, inc="ns_tmz")
    k.wait("tensor", "ns_tmz", k.val("ns_tmz"))
    for kk in range(ntr):
        w = min(128, ncol - kk * 128)
        k.tr(ps_tm[0:w, kk:kk + 1], srow[0:1, kk * 128:kk * 128 + w], iden[0:1, 0:1],
             inc="ns_tm")
        if not rms:
            k.tr(ps_tm[0:w, ntr + kk:ntr + kk + 1], mrow[0:1, kk * 128:kk * 128 + w],
                 iden[0:1, 0:1], inc="ns_tm")
    k.wait("vector", "ns_tm", k.val("ns_tm"))
    k.cp("vector", tmm[:, 0:ntr], ps_tm[:, 0:ntr], inc="ns_tmc")
    if not rms:
        k.cp("vector", tmm[:, ntr:2 * ntr], ps_tm[:, ntr:2 * ntr], inc="ns_tmc")
        k.wait("vector", "ns_tmc", k.val("ns_tmc"))
        k.tt("vector", tmm[:, 3 * ntr:4 * ntr], tmm[:, ntr:2 * ntr], tmm[:, ntr:2 * ntr],
             ALU.mult, inc="ns_tmc")
        k.wait("vector", "ns_tmc", k.val("ns_tmc"))
        k.tt("vector", tmm[:, 0:ntr], tmm[:, 0:ntr], tmm[:, 3 * ntr:4 * ntr], ALU.subtract,
             inc="ns_tmc")
        k.wait("scalar", "ns_tmc", k.val("ns_tmc"))
        k.act(tmm[:, 2 * ntr:3 * ntr], tmm[:, 0:ntr], AF.Sqrt, bias=k.t["eps_ln"][:],
              inc="ns_sq2")
        k.wait("vector", "ns_sq2", k.val("ns_sq2"))
        k.recip(tmm[:, 2 * ntr:3 * ntr], tmm[:, 2 * ntr:3 * ntr], inc="ns_rc")
        k.wait("vector", "ns_rc", k.val("ns_rc"))
        k.stt("vector", tmm[:, ntr:2 * ntr], tmm[:, ntr:2 * ntr], -1.0,
              tmm[:, 2 * ntr:3 * ntr], ALU.mult, ALU.mult, inc="ns_rc")
    else:
        k.wait("scalar", "ns_tmc", k.val("ns_tmc"))
        k.act(tmm[:, 2 * ntr:3 * ntr], tmm[:, 0:ntr], AF.Sqrt, inc="ns_sq2")
        k.wait("vector", "ns_sq2", k.val("ns_sq2"))
        k.tsc("vector", tmm[:, 2 * ntr:3 * ntr], tmm[:, 2 * ntr:3 * ntr], eps, None, ALU.add,
              inc="ns_sq2e")
        k.wait("vector", "ns_sq2e", k.val("ns_sq2e"))
        k.recip(tmm[:, 2 * ntr:3 * ntr], tmm[:, 2 * ntr:3 * ntr], inc="ns_rc")
    k.wait("tensor", "ns_rc", k.val("ns_rc"))
    for kk0 in range(0, ntr, 4):
        kks = list(range(kk0, min(kk0 + 4, ntr)))
        for (pp, colblk, dst) in ((ps_rb, 2, arow),) + ((() if rms else ((ps_rb2, 1, brow),))):
            for kk in kks:
                w = min(128, ncol - kk * 128)
                k.tr(pp[0:1, (kk - kk0) * 128:(kk - kk0) * 128 + w],
                     tmm[0:w, colblk * ntr + kk:colblk * ntr + kk + 1], iden[0:w, 0:w],
                     inc="ns_rb")
            k.wait("scalar", "ns_rb", k.val("ns_rb"))
            w4 = min(512, ncol - kk0 * 128)
            k.cp("scalar", dst[0:1, kk0 * 128:kk0 * 128 + w4], pp[0:1, 0:w4],
                 inc="ns_rbe")
        k.wait("tensor", "ns_rbe", k.val("ns_rbe"))
    return arow, brow, "ns_rbe", k.val("ns_rbe")


def emit_postln(k, tag, pre):
    X, XB = k.t["X"], k.t["XB"]
    ones1 = k.t["ones1_f"]
    k.push_scope()
    for eng in ("scalar", "vector", "gpsimd", "tensor"):
        for (ps, pv) in pre:
            k.wait(eng, ps, pv)
    engs = ("scalar", "vector", "gpsimd")
    for t in range(NT):
        a = t * ST
        for ch in range(DCH):
            k.cp(engs[(t * DCH + ch) % 3], XBc(k, ch, a, a + ST), Xc(k, ch, a, a + ST),
                 inc="pl_cst")
    vc = k.val("pl_cst")
    for eng in ("scalar", "tensor"):
        k.wait(eng, "pl_cst", vc)
    arow, brow, ssem, sval = emit_norm_stats(
        k, tag, lambda ch, a, b: XBc(k, ch, a, b), NCOL, NT, ST)
    ps_a = k.ps(f"psa_{tag}", [128, ST], F32)
    ps_b = k.ps(f"psb_{tag}", [128, ST], F32)
    a_sb = k.sb(f"asb_{tag}", [128, ST], F32)
    b_sb = k.sb(f"bsb_{tag}", [128, ST], F32)
    k.wait("tensor", ssem, sval)
    for t in range(NT):
        a = t * ST
        if t > 0:
            k.wait("tensor", "pl_abe", k.val("pl_abe"))
            k.wait("scalar", "pl_nm", k.val("pl_nm"))
            k.wait("scalar", "pl_na", k.val("pl_na"))
        k.mm(ps_a[:], ones1[:], arow[0:1, a:a + ST], start=True, stop=True)
        k.mm(ps_b[:], ones1[:], brow[0:1, a:a + ST], start=True, stop=True, inc="pl_ab")
        k.wait("scalar", "pl_ab", k.val("pl_ab"))
        k.cp("scalar", a_sb[:], ps_a[:], inc="pl_abe")
        k.cp("scalar", b_sb[:], ps_b[:], inc="pl_abe")
        vab = k.val("pl_abe")
        k.wait("gpsimd", "pl_abe", vab)
        k.wait("vector", "pl_abe", vab)
        for ch in range(DCH):
            xa = Xc(k, ch, a, a + ST)
            xba = XBc(k, ch, a, a + ST)
            vm = k.tt("gpsimd", xa, xa, a_sb[:], ALU.mult, inc="pl_nm")
            k.wait("vector", "pl_nm", vm)
            va = k.tt("vector", xa, xa, b_sb[:], ALU.add, inc="pl_na")
            k.wait("scalar", "pl_na", va)
            k.cp("scalar", xba, xa, inc="pl_nc")
    k.wait("tensor", "pl_abe", k.val("pl_abe"))
    out = ("pl_nc", k.val("pl_nc"))
    k.pop_scope()
    return out



def emit_phase1(k, io, upto="pl"):
    X, XB = k.t["X"], k.t["XB"]
    iden_b = k.t["iden_b"]
    k.push_scope()
    qkvT = k.sb("qkvT", [128, DCH * 192], BF16)
    bq = k.sb("bq", [64, 1], F32)
    bkv = k.sb("bkv", [128, 1], F32)
    projT = k.sb("projT", [128, DCH * 1024], BF16)
    pbias = k.sb("pbias", [128, 8], F32)
    efold = k.sb("efold", [128, DCH * 64], BF16)
    masks = k.sb("masks", [128, 8 * CW], BF16)
    kv_sb = k.sb("kv_sb", [128, NB * TSEQ], BF16)
    vT_sb = k.sb("vT_sb", [128, NB * NKC * 64], BF16)
    q_sb = k.sb("q_sb", [64, NB * QW], BF16)
    outh = k.sb("outh", [64, NB * QW], BF16)
    stsb = k.sb("stsb", [128, 2 * QW], BF16)
    pweff = k.sb("pweff", [64, 1024], BF16)

    load_wT_bf(k, io["qkvT"], qkvT, DCH, 192, "p1w")
    load_wT_bf(k, io["projT"], projT, DCH, 1024, "p1w")
    load_wT_bf(k, io["efold"], efold, DCH, 64, "p1w")
    k.dma("gpsimd", masks[:], io["mask"], "p1w")
    k.dma("sync", bq[:], io["bq"], "p1b")
    k.dma("sync", bkv[:], io["bkv"], "p1b")
    k.dma("sync", pbias[:], io["proj_b"], "p1b")
    vw = k.val("p1w")
    vb = k.val("p1b")
    for eng in ("tensor", "vector", "scalar"):
        k.wait(eng, "p1w", vw)
        k.wait(eng, "p1b", vb)

    KVT = 512
    nkv = TSEQ // KVT
    xkv = k.sb("xkv", [128, 2 * DCH * KVT], BF16)
    k.push_scope()
    ps_kv = [k.ps(f"pskv{i}", [128, KVT], F32) for i in range(2)]
    ps_vt = k.ps("psvt", [128, 256], BF16)
    for n in range(NB):
        for tks in range(nkv):
            ti = n * nkv + tks
            buf = ti % 2
            pkv = ps_kv[ti % 2]
            s0 = tks * KVT
            src = io["xT"].rearrange("(c p) n t -> p c n t", p=128)[:, :, n, s0:s0 + KVT]
            dst = xkv[:, buf * DCH * KVT:(buf + 1) * DCH * KVT].rearrange(
                "p (c w) -> p c w", c=DCH)
            if ti >= 2:
                k.wait("gpsimd", "p1kvmm", k.val("p1kvmm") - 1)
            vld = k.dma("gpsimd", dst, src, f"p1kv{buf}")
            k.wait("tensor", f"p1kv{buf}", vld)
            if ti >= 2:
                k.wait("tensor", "p1kve", k.val("p1kve") - 1)
            for ch in range(DCH):
                k.mm(pkv[:], qkvT[:, ch * 192 + 64:(ch + 1) * 192],
                     xkv[:, buf * DCH * KVT + ch * KVT: buf * DCH * KVT + (ch + 1) * KVT],
                     start=(ch == 0), stop=(ch == DCH - 1),
                     inc=("p1kvmm" if ch == DCH - 1 else None))
            k.wait("vector", "p1kvmm", k.val("p1kvmm"))
            vev = k.tsc("vector", kv_sb[:, n * TSEQ + s0:n * TSEQ + s0 + KVT], pkv[:],
                        bkv[:], None, ALU.add, inc="p1kve")
            k.wait("tensor", "p1kve", vev)
            if ti >= 1:
                k.wait("tensor", "p1vte", k.val("p1vte"))
            for pz in range(KVT // 128):
                k.tr(ps_vt[:, pz * 64:(pz + 1) * 64],
                     kv_sb[64:128, n * TSEQ + s0 + pz * 128:n * TSEQ + s0 + (pz + 1) * 128],
                     iden_b[64:128, 64:128], inc=("p1vt" if pz == 3 else None))
            k.wait("scalar", "p1vt", k.val("p1vt"))
            base = n * NKC * 64 + (s0 // 128) * 64
            k.cp("scalar", vT_sb[:, base:base + 256], ps_vt[:], inc="p1vte")

    k.wait("tensor", "p1vte", k.val("p1vte"))
    k.wait("tensor", "p1kve", k.val("p1kve"))
    k.pop_scope()

    if upto == "a":
        k.t["dbg"] = (kv_sb, "p1vte", k.val("p1vte"))
        return None
    k.push_scope()
    ps_q = k.ps("psq", [64, QT], F32)
    for n in range(NB):
        for t in range(QW // QT):
            a = t * QT
            if n + t > 0:
                k.wait("tensor", "p1qe", k.val("p1qe"))
            for (lo, hi) in ((0, 512), (512, QT)):
                for ch in range(DCH):
                    k.mm(ps_q[:, lo:hi], qkvT[:, ch * 192:ch * 192 + 64],
                         XB[:, ch * NCOL + 2 * (a + lo) + n: ch * NCOL + 2 * (a + hi) + n - 1: 2],
                         start=(ch == 0), stop=(ch == DCH - 1),
                         inc=("p1q" if (ch == DCH - 1 and hi == QT) else None))
            k.wait("vector", "p1q", k.val("p1q"))
            k.tsc("vector", q_sb[:, n * QW + a:n * QW + a + QT], ps_q[:], bq[:], None,
                  ALU.add, inc="p1qe")
    k.wait("tensor", "p1qe", k.val("p1qe"))
    k.pop_scope()

    if upto == "b":
        k.t["dbg"] = (q_sb, "p1qe", k.val("p1qe"))
        return None
    k.push_scope()
    ps_o1 = k.ps("pso1", [64, 512], F32)
    ps_o2 = k.ps("pso2", [64, 512], F32)
    ps_o3 = k.ps("pso3", [64, QW - 1024], F32)
    ps_s1 = [k.ps(f"pss1{i}", [128, 512], F32) for i in range(2)]
    ps_s2 = [k.ps(f"pss2{i}", [128, 512], F32) for i in range(2)]
    ps_s3 = [k.ps("pss3", [128, QW - 1024], F32)] * 2
    OP = ((0, 512), (512, 1024), (1024, QW))

    def stp(dbuf, lo, hi):
        for i, (p0, p1) in enumerate(OP):
            if lo >= p0 and hi <= p1:
                return (ps_s1, ps_s2, ps_s3)[i][dbuf][:, lo - p0:hi - p0]
        raise AssertionError((lo, hi))

    nkc_l = KC_LIM or NKC
    for n in range(N_LIM or NB):
        if n == 1:
            k.wait("tensor", "p1oe", k.val("p1oe"))
        for kc in range(nkc_l):
            jmin = kc // 8
            mloc = kc % 8
            qstart = jmin * CW
            dbuf = kc % 2
            pieces = [(max(p0, qstart), p1) for (p0, p1) in OP if max(p0, qstart) < p1]
            idx = n * NKC + kc
            if idx >= 2:
                k.wait("tensor", "p1sv", k.val("p1sv") - 1)
            if idx >= 1:
                k.wait("tensor", "p1se3", k.val("p1se3"))
            for pj, (lo, hi) in enumerate(pieces):
                k.mm(stp(dbuf, lo, hi), kv_sb[0:64, n * TSEQ + kc * 128:n * TSEQ + (kc + 1) * 128],
                     q_sb[0:64, n * QW + lo:n * QW + hi], start=True, stop=True,
                     inc=("p1st" if pj == len(pieces) - 1 else None))
            vst = k.val("p1st")
            k.wait("vector", "p1st", vst)
            k.wait("scalar", "p1st", vst)
            for pi, (lo, hi) in enumerate(pieces):
                dhi = min(hi, qstart + CW)
                plo = max(lo, dhi)
                if lo < dhi:
                    k.tt("vector", stsb[:, dbuf * QW + lo:dbuf * QW + dhi],
                         stp(dbuf, lo, dhi),
                         masks[:, mloc * CW + (lo - qstart):mloc * CW + (dhi - qstart)],
                         ALU.mult, inc=("p1se3" if lo >= 1024 else "p1sev"))
                if plo < hi:
                    s3b = plo >= 1024
                    k.cp("vector", stsb[:, dbuf * QW + plo:dbuf * QW + hi],
                         stp(dbuf, plo, hi), inc=("p1se3" if s3b else "p1sev"))
            k.wait("tensor", "p1sev", k.val("p1sev"))
            k.wait("tensor", "p1ses", k.val("p1ses"))
            k.wait("tensor", "p1se3", k.val("p1se3"))
            subs = []
            for (lo, hi) in (() if SKIP_SV else pieces):
                for i, (p0, p1) in enumerate(OP):
                    l2, h2 = max(lo, p0), min(hi, p1)
                    if l2 < h2:
                        subs.append((i, l2, h2))
            for sj, (i, l2, h2) in enumerate(subs):
                po = (ps_o1, ps_o2, ps_o3)[i]
                p0 = OP[i][0]
                last = min(nkc_l - 1, NKC - 1 if p0 >= 512 else (4 * 8 - 1))
                k.mm(po[:, l2 - p0:h2 - p0],
                     vT_sb[:, n * NKC * 64 + kc * 64:n * NKC * 64 + (kc + 1) * 64],
                     stsb[:, dbuf * QW + l2:dbuf * QW + h2],
                     start=(kc == 0), stop=(kc == last),
                     inc=("p1sv" if sj == len(subs) - 1 else None))
            if SKIP_SV:
                k.nopi("tensor", "p1sv")
        k.wait("scalar", "p1sv", k.val("p1sv"))
        if not SKIP_SV:
            k.cp("scalar", outh[:, n * QW:n * QW + 512], ps_o1[:], inc="p1oe")
            k.cp("scalar", outh[:, n * QW + 512:n * QW + 1024], ps_o2[:], inc="p1oe")
            k.cp("scalar", outh[:, n * QW + 1024:(n + 1) * QW], ps_o3[:], inc="p1oe")
        else:
            k.nopi("scalar", "p1oe")
    k.wait("tensor", "p1oe", k.val("p1oe"))
    k.wait("tensor", "p1sev", k.val("p1sev"))
    k.wait("tensor", "p1ses", k.val("p1ses"))
    k.wait("tensor", "p1se3", k.val("p1se3"))
    k.pop_scope()

    if upto == "c1":
        k.t["dbg"] = (stsb, "p1se3", k.val("p1se3"))
        return None
    if upto == "c":
        k.t["dbg"] = (outh, "p1oe", k.val("p1oe"))
        return None
    k.push_scope()
    ps_pw = k.ps("pspw", [64, 1024], F32)
    for ch in range(DCH):
        for half in range(2):
            k.mm(ps_pw[:, half * 512:(half + 1) * 512], efold[:, ch * 64:(ch + 1) * 64],
                 projT[:, ch * 1024 + half * 512:ch * 1024 + (half + 1) * 512],
                 start=(ch == 0), stop=(ch == DCH - 1),
                 inc=("p1pw" if (ch == DCH - 1 and half == 1) else None))
    k.wait("vector", "p1pw", k.val("p1pw"))
    k.cp("vector", pweff[:], ps_pw[:], inc="p1pwe")
    k.wait("tensor", "p1pwe", k.val("p1pwe"))

    ps_pr = [k.ps(f"pspr{i}", [128, QT], F32) for i in range(2)]
    ti = 0
    for n in range(NB):
        for t in range(QW // QT):
            a = t * QT
            pp = ps_pr[ti % 2]
            for oc in range(DCH):
                for (lo, hi) in ((0, 512), (512, QT)):
                    k.mm(pp[:, lo:hi], pweff[:, oc * 128:(oc + 1) * 128],
                         outh[:, n * QW + a + lo:n * QW + a + hi], start=True, stop=True,
                         inc=("p1pr" if hi == QT else None))
                k.wait("vector", "p1pr", k.val("p1pr"))
                k.stt("vector",
                      X[:, oc * NCOL + 2 * a + n: oc * NCOL + 2 * (a + QT) + n - 1: 2],
                      pp[:, 0:QT], pbias[:, oc:oc + 1],
                      X[:, oc * NCOL + 2 * a + n: oc * NCOL + 2 * (a + QT) + n - 1: 2],
                      ALU.add, ALU.add, inc="p1pre")
                k.wait("tensor", "p1pre", k.val("p1pre"))
            ti += 1
    pre = ("p1pre", k.val("p1pre"))
    k.pop_scope()
    k.pop_scope()
    if upto == "e":
        return pre
    return emit_postln(k, "pl1", [pre])


def load_wT_bf(k, hbm, sbuf, nch, width, sem):
    src = hbm.rearrange("(c p) m -> p c m", p=128)
    dst = sbuf[:].rearrange("p (c w) -> p c w", c=nch)
    return k.dma("gpsimd", dst, src, sem)



def declare_io_phase1(nc, standalone):
    io = {}
    add = lambda n, s, d, kind: io.__setitem__(n, nc.dram_tensor(n + _u(), s, d, kind=kind).ap())
    add("xin", [1024, NCOL], F32, "ExternalInput")
    add("xT", [1024, NB, TSEQ], F32, "ExternalInput")
    add("qkvT", [1024, 192], F32, "ExternalInput")
    add("bq", [64, 1], F32, "ExternalInput")
    add("bkv", [128, 1], F32, "ExternalInput")
    add("projT", [1024, 1024], F32, "ExternalInput")
    add("proj_b", [128, 8], F32, "ExternalInput")
    add("efold", [1024, 64], F32, "ExternalInput")
    add("mask", [128, 8 * CW], F32, "ExternalInput")
    add("ones1", [1, 128], F32, "ExternalInput")
    add("onesn", [128, 1], F32, "ExternalInput")
    add("iden", [128, 128], F32, "ExternalInput")
    if standalone:
        add("xout", [128, DCH * NCOL], F32, "ExternalOutput")
    return io


def io_names(io):
    return {kk: ap.tensor.name for kk, ap in io.items() if hasattr(ap, "tensor")}


def build_phase1_standalone(upto="pl"):
    stack = ExitStack()
    nc = bass.Bass("TRN2", target_bir_lowering=False)
    io = declare_io_phase1(nc, True)
    dbg_shape = [None]
    with stack:
        k = K(nc, stack)
        alloc_common(k, io)
        emit_load_stream(k, io)
        r = emit_phase1(k, io, upto=upto)
        if r is None:
            tile, sem, val = k.t["dbg"]
            shp = [tile[:].shape[0], tile[:].shape[1]]
            dbg_shape[0] = shp
            dbgout = nc.dram_tensor("dbgout" + _u(), shp, tile[:].tensor.dtype,
                                    kind="ExternalOutput")
            io["dbgout"] = dbgout.ap()
            k.wait("sync", sem, val)
            k.dma("sync", io["dbgout"], tile[:], "out")
        elif upto == "e":
            sem, val = r
            k.wait("sync", sem, val)
            k.dma("sync", io["xout"], k.t["X"][:], "out")
        else:
            sem, val = r
            k.wait("sync", sem, val)
            k.dma("sync", io["xout"], k.t["X"][:], "out")
        k.wait("sync", "out", k.val("out"))
        k.finalize()
    return nc, io_names(io)


def phase1_in_map(P, Q):
    return {
        "xin": Q["xin"], "xT": P["xT"], "qkvT": P["qkvT"], "bq": P["bq"], "bkv": P["bkv"],
        "projT": P["projT"], "proj_b": P["proj_b"], "efold": P["efold"], "mask": Q["mask"],
        "ones1": P["ones1"], "onesn": P["onesn"], "iden": P["iden"],
    }





def emit_dequant(k, io, m, wbuf, qbufs, sbufs, waitsem=None, waitval=0):
    neg1 = k.t["neg1"]
    if waitsem:
        k.wait("gpsimd", waitsem, waitval)
    k.wait("sync", "dq_m", k.val("dq_m"))
    for hc in range(2 * DCH):
        ch, hh = hc // 2, hc % 2
        b = hc % 2
        if hc >= 2:
            k.wait("sync", "dq_m", k.val("dq_m") - 1)
        qsrc = io["qTw"][m].rearrange("(c p) w -> p c w", p=128)[:, ch, hh * 512:(hh + 1) * 512]
        ssrc = io["sTe"][m].rearrange("(c p) w -> p c w", p=128)[:, ch, hh * 512:(hh + 1) * 512]
        vqd = k.dma("sync", qbufs[b][:], qsrc, f"dq_q{b}")
        vsd = k.dma("sync", sbufs[b][:], ssrc, f"dq_s{b}")
        k.wait("scalar", f"dq_q{b}", vqd)
        if hc >= 1:
            k.wait("scalar", "dq_m", k.val("dq_m"))
        vc = k.act(k.t["qf_scr"][:], qbufs[b][:], AF.Copy,
                   scale=1.0 / 7.5, inc=f"dq_c{b}")
        k.wait("vector", f"dq_c{b}", vc)
        k.wait("vector", f"dq_s{b}", vsd)
        k.stt("vector", wbuf[:, ch * 1024 + hh * 512: ch * 1024 + (hh + 1) * 512],
              k.t["qf_scr"][:], -1.0, sbufs[b][:], ALU.add, ALU.mult, inc="dq_m")
    return "dq_m", k.val("dq_m")


def emit_phase2(k, io):
    X, XB = k.t["X"], k.t["XB"]
    k.push_scope()
    wbufs = [k.sb(f"w2b{i}", [128, DCH * 1024], BF16) for i in range(3)]
    k.t["qf_scr"] = k.sb("qfscr", [128, 512], F32)
    qbufs = [k.sb(f"qbuf{i}", [128, 512], I32) for i in range(2)]
    sbufs = [k.sb(f"sbuf{i}", [128, 512], F32) for i in range(2)]
    laT = k.sb("laT", [128, 6 * DCH * 32], BF16)
    lbT = k.sb("lbT", [32, 6 * 1024], BF16)
    biasA = k.sb("biasA", [128, 6 * 8], F32)
    gam = k.sb("gam", [128, 3 * 8], F32)
    bet = k.sb("bet", [128, 3 * 8], F32)
    Y1 = k.sb("Y1", [128, DCH * ST], F32)
    VB = k.sb("VB", [128, DCH * ST], BF16)
    G = VB
    t1s = k.sb("t1s", [32, 2 * ST], BF16)
    a_sb = k.sb("p2asb", [128, ST], F32)
    b_sb = k.sb("p2bsb", [128, ST], F32)

    k.dma("gpsimd", laT[:].rearrange("p (m c w) -> p (m c) w", m=6, w=32),
          io["laT"].rearrange("m (c p) w -> p (m c) w", p=128), "p2wg")
    k.dma("gpsimd", lbT[:].rearrange("p (m w) -> p m w", m=6), io["lbT"], "p2wg")
    k.dma("sync", biasA[:], io["biasA"], "p2ws")
    k.dma("sync", gam[:], io["gamma"], "p2ws")
    k.dma("sync", bet[:], io["beta"], "p2ws")
    for eng in ("tensor", "vector", "scalar", "gpsimd"):
        k.wait(eng, "p2wg", k.val("p2wg"))
        k.wait(eng, "p2ws", k.val("p2ws"))

    if "gam" in P2DBG:
        k.wait("sync", "p2ws", k.val("p2ws"))
        vdg = k.dma("sync", io["dbg_gam"], gam[:], "p2dbgc")
        k.wait("sync", "p2dbgc", vdg)
    emit_dequant(k, io, 0, wbufs[0], qbufs, sbufs)
    emit_dequant(k, io, 1, wbufs[1], qbufs, sbufs)

    for i in range(3):
        i0, i1 = 2 * i, 2 * i + 1
        wA, wB = wbufs[i0 % 3], wbufs[i1 % 3]
        k.wait("tensor", "dq_m", k.val("dq_m"))
        k.push_scope()
        ps_l = k.ps(f"psl{i}", [32, ST], F32)
        ps_m = [k.ps(f"psm{i}_{j}", [128, ST], F32) for j in range(2)]
        ps_a = k.ps(f"psa{i}", [128, ST], F32)
        ps_b = k.ps(f"psb{i}", [128, ST], F32)
        for t in range(NT):
            a = t * ST
            k.wait("tensor", "p2t1e", k.val("p2t1e"))
            for ch in range(DCH):
                k.mm(ps_l[:], laT[:, (i0 * DCH + ch) * 32:(i0 * DCH + ch + 1) * 32],
                     XBc(k, ch, a, a + ST), start=(ch == 0), stop=(ch == DCH - 1),
                     inc=("p2t1" if ch == DCH - 1 else None))
            k.wait("scalar", "p2t1", k.val("p2t1"))
            k.cp("scalar", t1s[:, 0:ST], ps_l[:], inc="p2t1e")
            k.wait("tensor", "p2t1e", k.val("p2t1e"))
            for oc in range(DCH):
                p = oc % 2
                pm = ps_m[p]
                k.wait("tensor", f"p2ae{p}", k.val(f"p2ae{p}"))
                k.wait("tensor", f"p2be{p}", k.val(f"p2be{p}"))
                for ch in range(DCH):
                    k.mm(pm[:], wA[:, ch * 1024 + oc * 128: ch * 1024 + (oc + 1) * 128],
                         XBc(k, ch, a, a + ST), start=(ch == 0), stop=False)
                k.mm(pm[:], lbT[:, i0 * 1024 + oc * 128: i0 * 1024 + (oc + 1) * 128],
                     t1s[:, 0:ST], start=False, stop=True, inc="p2a")
                k.wait("vector", "p2a", k.val("p2a"))
                vae = k.tsc("vector", Y1[:, oc * ST:(oc + 1) * ST], pm[:],
                            biasA[:, i0 * 8 + oc: i0 * 8 + oc + 1], None, ALU.add,
                            inc=f"p2ae{p}")
                k.wait("scalar", f"p2ae{p}", vae)
                k.cp("scalar", VB[:, oc * ST:(oc + 1) * ST], Y1[:, oc * ST:(oc + 1) * ST],
                     inc="p2vb")
            vvb = k.val("p2vb")
            k.wait("scalar", "p2vb", vvb)
            k.wait("tensor", "p2vb", vvb)
            if i == 0 and t == 0 and "y1" in P2DBG:
                k.wait("sync", "p2vb", vvb)
                k.dma("sync", io["dbg_y1"], Y1[:], "p2dbga")
                k.wait("gpsimd", "p2dbga", k.val("p2dbga"))
                k.wait("vector", "p2dbga", k.val("p2dbga"))
            k.push_scope()
            arow, brow, ssem, sval = emit_norm_stats(
                k, f"ln{i}_{t}", lambda ch, aa, bb: VB[:, ch * ST + aa: ch * ST + bb],
                ST, 1, ST)
            k.wait("tensor", ssem, sval)
            k.pop_scope()
            k.wait("tensor", "p2abe", k.val("p2abe"))
            k.mm(ps_a[:], k.t["ones1_f"][:], arow[0:1, :], start=True, stop=True)
            k.mm(ps_b[:], k.t["ones1_f"][:], brow[0:1, :], start=True, stop=True, inc="p2ab")
            k.wait("scalar", "p2ab", k.val("p2ab"))
            k.wait("scalar", "p2nm", k.val("p2nm"))
            k.wait("scalar", "p2na", k.val("p2na"))
            k.cp("scalar", a_sb[:], ps_a[:], inc="p2abe")
            k.cp("scalar", b_sb[:], ps_b[:], inc="p2abe")
            vabe = k.val("p2abe")
            k.wait("gpsimd", "p2abe", vabe)
            k.wait("vector", "p2abe", vabe)
            for ch in range(DCH):
                ya = Y1[:, ch * ST:(ch + 1) * ST]
                vm = k.tt("gpsimd", ya, ya, a_sb[:], ALU.mult, inc="p2nm")
                k.wait("vector", "p2nm", vm)
                va = k.tt("vector", ya, ya, b_sb[:], ALU.add, inc="p2na")
                k.wait("vector", "p2na", va)
                vg = k.tsc("vector", ya, ya, gam[:, i * 8 + ch: i * 8 + ch + 1],
                           bet[:, i * 8 + ch: i * 8 + ch + 1], ALU.mult, ALU.add, inc="p2ng")
                k.wait("scalar", "p2ng", vg)
                if i == 0 and t == 0 and "z" in P2DBG:
                    k.wait("sync", "p2ng", vg)
                    vdz = k.dma("sync", io["dbg_z"].rearrange("p (c w) -> p c w", c=DCH)[:, ch, :],
                                ya, f"p2dbgz{ch}")
                    k.wait("vector", f"p2dbgz{ch}", vdz)
                    k.wait("gpsimd", f"p2dbgz{ch}", vdz)
                k.act(G[:, ch * ST:(ch + 1) * ST], ya, AF.Gelu, inc="p2gel")
            vgel = k.val("p2gel")
            k.wait("tensor", "p2gel", vgel)
            if i == 0 and t == 0 and "g" in P2DBG:
                k.wait("sync", "p2gel", vgel)
                k.dma("sync", io["dbg_g"], G[:], "p2dbgb")
                k.wait("gpsimd", "p2dbgb", k.val("p2dbgb"))
                k.wait("tensor", "p2dbgb", k.val("p2dbgb"))
            k.wait("tensor", "p2t1e", k.val("p2t1e"))
            for ch in range(DCH):
                k.mm(ps_l[:], laT[:, (i1 * DCH + ch) * 32:(i1 * DCH + ch + 1) * 32],
                     G[:, ch * ST:(ch + 1) * ST], start=(ch == 0), stop=(ch == DCH - 1),
                     inc=("p2t1b" if ch == DCH - 1 else None))
            k.wait("scalar", "p2t1b", k.val("p2t1b"))
            k.cp("scalar", t1s[:, ST:2 * ST], ps_l[:], inc="p2t1e")
            k.wait("tensor", "p2t1e", k.val("p2t1e"))
            for oc in range(DCH):
                p = oc % 2
                pm = ps_m[p]
                k.wait("tensor", f"p2ae{p}", k.val(f"p2ae{p}"))
                k.wait("tensor", f"p2be{p}", k.val(f"p2be{p}"))
                for ch in range(DCH):
                    k.mm(pm[:], wB[:, ch * 1024 + oc * 128: ch * 1024 + (oc + 1) * 128],
                         G[:, ch * ST:(ch + 1) * ST], start=(ch == 0), stop=False)
                k.mm(pm[:], lbT[:, i1 * 1024 + oc * 128: i1 * 1024 + (oc + 1) * 128],
                     t1s[:, ST:2 * ST], start=False, stop=True, inc="p2b")
                k.wait("vector", "p2b", k.val("p2b"))
                vx = k.stt("vector", Xc(k, oc, a, a + ST), pm[:],
                           biasA[:, i1 * 8 + oc: i1 * 8 + oc + 1], Xc(k, oc, a, a + ST),
                           ALU.add, ALU.add, inc=f"p2be{p}")
                if i < 2:
                    k.wait("scalar", f"p2be{p}", vx)
                    k.act(XBc(k, oc, a, a + ST), Xc(k, oc, a, a + ST), AF.Copy, inc="p2xb")
        for s in ("p2ae0", "p2ae1", "p2be0", "p2be1", "p2t1e"):
            k.wait("tensor", s, k.val(s))
        k.pop_scope()
        if i < 2:
            ws = ("p2be0", k.val("p2be0"))
            emit_dequant(k, io, 2 * i + 2, wbufs[(2 * i + 2) % 3], qbufs, sbufs, ws[0], ws[1])
            emit_dequant(k, io, 2 * i + 3, wbufs[(2 * i + 3) % 3], qbufs, sbufs,
                         "p2be1", k.val("p2be1"))
            k.wait("tensor", "p2xb", k.val("p2xb"))
    for s in ("p2be0", "p2be1"):
        k.wait("vector", s, k.val(s))
        k.wait("gpsimd", s, k.val(s))
    k.pop_scope()
    k.push_scope()
    xrb = [k.sb(f"xrb{j}", [128, DCH * ST], F32) for j in range(2)]
    for s in ("p2be0", "p2be1"):
        k.wait("sync", s, k.val(s))
    for t in range(NT):
        a = t * ST
        b = t % 2
        if t >= 2:
            k.wait("sync", f"p2ra{b}", k.val(f"p2ra{b}"))
        vd = k.dma("sync", xrb[b][:].rearrange("p (c w) -> p c w", c=DCH),
                   io["x1ref"].rearrange("p (c n) -> p c n", c=DCH)[:, :, a:a + ST],
                   f"p2rx{b}")
        for ch in range(DCH):
            eng = ("vector", "gpsimd")[ch % 2]
            k.wait(eng, f"p2rx{b}", vd)
            k.tt(eng, Xc(k, ch, a, a + ST), Xc(k, ch, a, a + ST),
                 xrb[b][:, ch * ST:(ch + 1) * ST], ALU.add, inc=f"p2ra{b}")
    pre = [(f"p2ra0", k.val("p2ra0")), (f"p2ra1", k.val("p2ra1"))]
    k.pop_scope()
    return emit_postln(k, "pl2", pre)


def declare_io_phase2(nc, standalone):
    io = {}
    add = lambda n, s, d, kind: io.__setitem__(n, nc.dram_tensor(n + _u(), s, d, kind=kind).ap())
    if "y1" in P2DBG:
        add("dbg_y1", [128, DCH * ST], F32, "ExternalOutput")
    if "gam" in P2DBG:
        add("dbg_gam", [128, 24], F32, "ExternalOutput")
    if "z" in P2DBG:
        add("dbg_z", [128, DCH * ST], F32, "ExternalOutput")
    if "g" in P2DBG:
        add("dbg_g", [128, DCH * ST], BF16, "ExternalOutput")
    add("qTw", [6, 1024, 1024], I32, "ExternalInput")
    add("sTe", [6, 1024, 1024], F32, "ExternalInput")
    add("biasA", [128, 48], F32, "ExternalInput")
    add("laT", [6, 1024, 32], F32, "ExternalInput")
    add("lbT", [32, 6 * 1024], F32, "ExternalInput")
    add("gamma", [128, 24], F32, "ExternalInput")
    add("beta", [128, 24], F32, "ExternalInput")
    add("ones1", [1, 128], F32, "ExternalInput")
    add("onesn", [128, 1], F32, "ExternalInput")
    add("iden", [128, 128], F32, "ExternalInput")
    if standalone:
        add("xstate", [128, DCH * NCOL], F32, "ExternalInput")
        add("xout", [128, DCH * NCOL], F32, "ExternalOutput")
        io["x1ref"] = io["xstate"]
    return io


def build_phase2_standalone():
    stack = ExitStack()
    nc = bass.Bass("TRN2", target_bir_lowering=False)
    io = declare_io_phase2(nc, True)
    with stack:
        k = K(nc, stack)
        alloc_common(k, io)
        k.dma("sync", k.t["X"][:], io["xstate"], "ldx")
        k.dma("gpsimd", k.t["XB"][:], io["xstate"], "ldxb")
        for eng in ("tensor", "vector", "scalar", "gpsimd"):
            k.wait(eng, "ldx", k.val("ldx"))
            k.wait(eng, "ldxb", k.val("ldxb"))
        sem, val = emit_phase2(k, io)
        k.wait("sync", sem, val)
        k.dma("sync", io["xout"], k.t["X"][:], "out")
        k.wait("sync", "out", k.val("out"))
        k.finalize()
    return nc, io_names(io)


def phase2_in_map(P, xstate):
    return {
        "qTw": P["qTw"], "sTe": P["sTe"], "biasA": P["biasA"], "laT": P["laT"],
        "lbT": P["lbT2"], "gamma": P["gamma"], "beta": P["beta"],
        "ones1": P["ones1"], "onesn": P["onesn"], "iden": P["iden"], "xstate": xstate,
    }



def emit_phase3(k, io):
    X, XB = k.t["X"], k.t["XB"]
    ones1 = k.t["ones1_f"]
    k.push_scope()
    g_all = k.sb("g_all", [128, 16 * NCOL], BF16)
    bu = k.sb("bu3", [128, 16], F32)
    bv = k.sb("bv3", [128, 16], F32)
    bo = k.sb("bo3", [128, 8], F32)
    ze = k.sb("ze3", [128, 16], F32)
    pe = k.sb("pe3", [128, 16], F32)
    arow3 = k.sb("arow3k", [1, NCOL], BF16)
    a_sb = k.sb("a3sb", [128, ST], F32)
    xn = XB
    k.t["p3u"] = k.sb("p3u", [128, 2 * ST], F32)
    k.dma("sync", bu[:], io["bu"], "p3b")
    k.dma("sync", bv[:], io["bv"], "p3b")
    k.dma("sync", bo[:], io["bo"], "p3b")
    k.dma("sync", ze[:], io["ze"], "p3b")
    k.dma("sync", pe[:], io["pe"], "p3b")
    for eng in ("tensor", "vector", "scalar", "gpsimd"):
        k.wait(eng, "p3b", k.val("p3b"))

    k.push_scope()
    ar, _, ssem, sval = emit_norm_stats(k, "rms3", lambda ch, a, b: XBc(k, ch, a, b),
                                        NCOL, NT, ST, rms=True, eps=1e-8)
    k.wait("vector", ssem, sval)
    k.cp("vector", arow3[0:1, :], ar[0:1, :], inc="p3ar")
    k.wait("tensor", "p3ar", k.val("p3ar"))
    for s in ("ns_qs", "ns_tm", "ns_rb", "ns_rbe", "p3ar"):
        k.wait("gpsimd", s, k.val(s))
    k.pop_scope()

    ps_a = k.ps("p3psa", [128, ST], F32)
    ps_v = [k.ps(f"p3psv{j}", [128, ST], F32) for j in range(2)]

    onesb = k.t["ones1_b"]

    def emit_xn(t):
        a = t * ST
        k.wait("tensor", "p3ae", k.val("p3ae"))
        k.mm(ps_a[:], onesb[:], arow3[0:1, a:a + ST], start=True, stop=True, inc="p3abc")
        k.wait("scalar", "p3abc", k.val("p3abc"))
        k.wait("scalar", "p3xnw", k.val("p3xnw"))
        k.cp("scalar", a_sb[:], ps_a[:], inc="p3ae")
        k.wait("gpsimd", "p3ae", k.val("p3ae"))
        k.wait("gpsimd", "p3mm", k.val("p3mm"))
        for ch in range(DCH):
            k.tt("gpsimd", xn[:, ch * ST:(ch + 1) * ST], Xc(k, ch, a, a + ST), a_sb[:],
                 ALU.mult, inc="p3xnw")
        k.wait("tensor", "p3xnw", k.val("p3xnw"))

    k.push_scope()
    ww = k.sb("w3w", [128, 8 * 1024], BF16)
    for half in range(2):
        k.wait("gpsimd", "p3mx", k.val("p3mx"))
        k.wait("gpsimd", "p3mm", k.val("p3mm"))
        vw = k.dma("gpsimd", ww[:].rearrange("p (c w) -> p c w", c=DCH),
                   io["WvT"].rearrange("(c p) w -> p c w", p=128)[:, :, half * 1024:(half + 1) * 1024],
                   f"p3w{half}")
        k.wait("tensor", f"p3w{half}", vw)
        for t in range(NT):
            a = t * ST
            emit_xn(t)
            for oc in range(8):
                pv = ps_v[oc % 2]
                occ = half * 8 + oc
                k.wait("tensor", f"p3ve{oc % 2}", k.val(f"p3ve{oc % 2}"))
                for ch in range(DCH):
                    k.mm(pv[:], ww[:, ch * 1024 + oc * 128: ch * 1024 + (oc + 1) * 128],
                         xn[:, ch * ST:(ch + 1) * ST], start=(ch == 0), stop=(ch == DCH - 1),
                         inc=("p3mm" if ch == DCH - 1 else None))
                k.wait("scalar", "p3mm", k.val("p3mm"))
                k.wait("scalar", f"p3mx{oc % 2}", k.val(f"p3mx{oc % 2}"))
                vsl = k.act(k.t["p3u"][:, (oc % 2) * ST:(oc % 2 + 1) * ST], pv[:], AF.Silu,
                            bias=bv[:, occ:occ + 1], inc=f"p3ve{oc % 2}")
                vch = k.t["p3u"][:, (oc % 2) * ST:(oc % 2 + 1) * ST]
                gch = g_all[:, occ * NCOL + a: occ * NCOL + a + ST]
                k.wait("vector", f"p3ve{oc % 2}", vsl)
                vA = k.tsc("vector", gch, vch, ze[:, occ:occ + 1], None, ALU.mult,
                           inc=f"p3mx{oc % 2}")
                k.wait("vector", f"p3mx{oc % 2}", vA)
                k.stt("vector",
                      g_all[:, occ * NCOL + a + 1: occ * NCOL + a + ST: 2],
                      vch[:, 0:ST - 1:2], pe[:, occ:occ + 1],
                      g_all[:, occ * NCOL + a + 1: occ * NCOL + a + ST: 2],
                      ALU.mult, ALU.add, inc=f"p3mx{oc % 2}")
    for half in range(2):
        k.wait("gpsimd", "p3mx0", k.val("p3mx0"))
        k.wait("gpsimd", "p3mx1", k.val("p3mx1"))
        k.wait("gpsimd", "p3mm", k.val("p3mm"))
        vw = k.dma("gpsimd", ww[:].rearrange("p (c w) -> p c w", c=DCH),
                   io["WuT"].rearrange("(c p) w -> p c w", p=128)[:, :, half * 1024:(half + 1) * 1024],
                   f"p3wu{half}")
        k.wait("tensor", f"p3wu{half}", vw)
        for t in range(NT):
            a = t * ST
            emit_xn(t)
            for oc in range(8):
                pv = ps_v[oc % 2]
                occ = half * 8 + oc
                k.wait("tensor", f"p3ve{oc % 2}", k.val(f"p3ve{oc % 2}"))
                for ch in range(DCH):
                    k.mm(pv[:], ww[:, ch * 1024 + oc * 128: ch * 1024 + (oc + 1) * 128],
                         xn[:, ch * ST:(ch + 1) * ST], start=(ch == 0), stop=(ch == DCH - 1),
                         inc=("p3mm" if ch == DCH - 1 else None))
                k.wait("scalar", "p3mm", k.val("p3mm"))
                k.wait("scalar", f"p3mx{oc % 2}", k.val(f"p3mx{oc % 2}"))
                vsl = k.act(k.t["p3u"][:, (oc % 2) * ST:(oc % 2 + 1) * ST], pv[:], AF.Silu,
                            bias=bu[:, occ:occ + 1], inc=f"p3ve{oc % 2}")
                uch = k.t["p3u"][:, (oc % 2) * ST:(oc % 2 + 1) * ST]
                gch = g_all[:, occ * NCOL + a: occ * NCOL + a + ST]
                k.wait("vector", f"p3ve{oc % 2}", vsl)
                k.tt("vector", gch, gch, uch, ALU.mult, inc=f"p3mx{oc % 2}")
    k.wait("tensor", "p3mm", k.val("p3mm"))
    k.pop_scope()
    k.wait("gpsimd", "p3mx0", k.val("p3mx0"))
    k.wait("gpsimd", "p3mx1", k.val("p3mx1"))
    k.wait("gpsimd", "p3mm", k.val("p3mm"))
    wo = k.sb("w3o", [128, 16 * 1024], BF16)
    vwo = k.dma("gpsimd", wo[:].rearrange("p (c w) -> p c w", c=16),
                io["WoT"].rearrange("(c p) w -> p c w", p=128), "p3wo")
    k.wait("tensor", "p3wo", vwo)
    k.wait("tensor", "p3ve0", k.val("p3ve0"))
    k.wait("tensor", "p3ve1", k.val("p3ve1"))
    for t in range(NT):
        a = t * ST
        for oc in range(DCH):
            s = oc % 2
            pv = ps_v[s]
            tsb = k.t["p3u"][:, s * ST:(s + 1) * ST]
            k.wait("tensor", f"p3oe{s}", k.val(f"p3oe{s}"))
            for kc in range(16):
                k.mm(pv[:], wo[:, kc * 1024 + oc * 128: kc * 1024 + (oc + 1) * 128],
                     g_all[:, kc * NCOL + a: kc * NCOL + a + ST],
                     start=(kc == 0), stop=(kc == 15),
                     inc=("p3om" if kc == 15 else None))
            k.wait("vector", "p3om", k.val("p3om"))
            k.wait("vector", f"p3xo{s}", k.val(f"p3xo{s}"))
            vt = k.stt("vector", tsb, pv[:], bo[:, oc:oc + 1], Xc(k, oc, a, a + ST),
                       ALU.add, ALU.add, inc=f"p3oe{s}")
            k.wait("gpsimd", f"p3oe{s}", vt)
            k.tt("gpsimd", Xc(k, oc, a, a + ST), tsb, Xc(k, oc, a, a + ST), ALU.add,
                 inc=f"p3xo{s}")
    for s in ("p3xo0", "p3xo1"):
        k.wait("tensor", s, k.val(s))
        k.wait("scalar", s, k.val(s))
    k.wait("tensor", "p3oe0", k.val("p3oe0"))
    k.wait("tensor", "p3oe1", k.val("p3oe1"))
    pre = [("p3xo0", k.val("p3xo0")), ("p3xo1", k.val("p3xo1"))]
    k.pop_scope()
    return emit_postln(k, "pl3", pre)


def declare_io_phase3(nc, standalone):
    io = {}
    add = lambda n, s, d, kind: io.__setitem__(n, nc.dram_tensor(n + _u(), s, d, kind=kind).ap())
    add("WuT", [1024, 2048], F32, "ExternalInput")
    add("WvT", [1024, 2048], F32, "ExternalInput")
    add("WoT", [2048, 1024], F32, "ExternalInput")
    add("bu", [128, 16], F32, "ExternalInput")
    add("bv", [128, 16], F32, "ExternalInput")
    add("bo", [128, 8], F32, "ExternalInput")
    add("ze", [128, 16], F32, "ExternalInput")
    add("pe", [128, 16], F32, "ExternalInput")
    add("ones1", [1, 128], F32, "ExternalInput")
    add("onesn", [128, 1], F32, "ExternalInput")
    add("iden", [128, 128], F32, "ExternalInput")
    if standalone:
        add("xstate", [128, DCH * NCOL], F32, "ExternalInput")
        add("xout", [128, DCH * NCOL], F32, "ExternalOutput")
    return io


def build_phase3_standalone():
    stack = ExitStack()
    nc = bass.Bass("TRN2", target_bir_lowering=False)
    io = declare_io_phase3(nc, True)
    with stack:
        k = K(nc, stack)
        alloc_common(k, io)
        k.dma("sync", k.t["X"][:], io["xstate"], "ldx")
        k.dma("gpsimd", k.t["XB"][:], io["xstate"], "ldxb")
        for eng in ("tensor", "vector", "scalar", "gpsimd"):
            k.wait(eng, "ldx", k.val("ldx"))
            k.wait(eng, "ldxb", k.val("ldxb"))
        sem, val = emit_phase3(k, io)
        k.wait("sync", sem, val)
        k.dma("sync", io["xout"], k.t["X"][:], "out")
        k.wait("sync", "out", k.val("out"))
        k.finalize()
    return nc, io_names(io)


def phase3_in_map(P, xstate):
    return {
        "WuT": P["WuT"], "WvT": P["WvT"], "WoT": P["WoT"], "bu": P["bu"], "bv": P["bv"],
        "bo": P["bo"], "ze": P["ze"], "pe": P["pe"],
        "ones1": P["ones1"], "onesn": P["onesn"], "iden": P["iden"], "xstate": xstate,
    }



def emit_wquant(k, io, which, wq_dram, nch, width, q="sync"):
    onescol = k.t["onescol_f"]
    ones1 = k.t["ones1_f"]
    iden = k.t["iden_f"]
    k.push_scope()
    wstr = [k.sb(f"wqs{which}{j}", [128, width], F32) for j in range(2)]
    wqo = [k.sb(f"wqo{which}{j}", [128, width], BF16) for j in range(2)]
    parts = k.sb(f"wqp{which}", [128, nch], F32)
    scr = k.sb(f"wqc{which}", [128, width], F32)
    scri = k.sb(f"wqi{which}", [128, width], F32)[:]
    mgt = k.sb(f"wqg{which}", [128, 1], F32)
    k.memset("gpsimd", mgt[:], 12582912.0, inc="p4mg")
    k.wait("gpsimd", "p4mg", k.val("p4mg"))
    sm = k.sb(f"wqm{which}", [1, 16], F32)
    swpp = k.sb(f"wqw{which}", [128, 2], F32)
    ps_s = k.ps(f"wqps{which}", [128, 64], F32)
    src_h = io[which].rearrange("(c p) m -> p c m", p=128)
    for ch in range(nch):
        b = ch % 2
        if ch >= 2:
            k.wait(q, "p4wa", k.val("p4wa") - 1)
        vd = k.dma(q, wstr[b][:], src_h[:, ch, :], f"wq{q}d{b}")
        k.wait("vector", f"wq{q}d{b}", vd)
        k.red("vector", parts[:, ch:ch + 1], wstr[b][:], ALU.add, absval=True, inc="p4wa")
    k.wait("tensor", "p4wa", k.val("p4wa"))
    k.mm(ps_s[0:nch, 0:1], parts[:, 0:nch], onescol[:], start=True, stop=True, inc="p4wm")
    k.wait("vector", "p4wm", k.val("p4wm"))
    k.cp("vector", parts[0:nch, 0:1], ps_s[0:nch, 0:1], inc="p4wm2")
    k.wait("tensor", "p4wm2", k.val("p4wm2"))
    k.tr(ps_s[0:1, 4:4 + nch], parts[0:nch, 0:1], iden[0:nch, 0:nch], inc="p4wt")
    k.wait("vector", "p4wt", k.val("p4wt"))
    k.red("vector", sm[0:1, 0:1], ps_s[0:1, 4:4 + nch], ALU.add, inc="p4wr")
    k.wait("vector", "p4wr", k.val("p4wr"))
    k.tsc("vector", sm[0:1, 1:2], sm[0:1, 0:1], 1.0 / (nch * 128 * width), 1e-5,
          ALU.mult, ALU.max, inc="p4wr")
    k.wait("vector", "p4wr", k.val("p4wr"))
    k.recip(sm[0:1, 2:3], sm[0:1, 1:2], inc="p4wr")
    k.wait("tensor", "p4wr", k.val("p4wr"))
    k.mm(ps_s[:, 1:2], ones1[:], sm[0:1, 2:3], start=True, stop=True)
    k.mm(ps_s[:, 2:3], ones1[:], sm[0:1, 1:2], start=True, stop=True, inc="p4wb")
    k.wait("scalar", "p4wb", k.val("p4wb"))
    k.cp("scalar", swpp[:, 0:1], ps_s[:, 1:2], inc="p4we")
    k.cp("scalar", swpp[:, 1:2], ps_s[:, 2:3], inc="p4we")
    vwe = k.val("p4we")
    for eng in ("vector", "gpsimd", "sync"):
        k.wait(eng, "p4we", vwe)
    MAGIC = 12582912.0
    k.wait(q, "p4wa", k.val("p4wa"))
    for ch in range(nch):
        b = ch % 2
        if ch >= 2:
            k.wait(q, "p4q1", k.val("p4q1") - 1)
        vd = k.dma(q, wstr[b][:], src_h[:, ch, :], f"wq{q}e{b}")
        k.wait("vector", f"wq{q}e{b}", vd)
        if ch >= 1:
            k.wait("vector", "p4q2", k.val("p4q2"))
        v1 = k.tsc("vector", scr[:], wstr[b][:], swpp[:, 0:1], MAGIC, ALU.mult, ALU.add,
                   inc="p4q1")
        k.wait("gpsimd", "p4q1", v1)
        if ch >= 1:
            k.wait("gpsimd", "p4q4", k.val("p4q4"))
        v2 = k.tt("gpsimd", scri, scr[:], mgt[:].broadcast_to([128, width]), ALU.subtract,
                  inc="p4q2")
        k.wait("vector", "p4q2", v2)
        v3 = k.tsc("vector", scri, scri, 1.0, -1.0, ALU.min, ALU.max, inc="p4q3")
        k.wait("vector", "p4q3", v3)
        if ch >= 2:
            k.wait("vector", f"wq{q}o{b}", k.val(f"wq{q}o{b}"))
        v4 = k.tsc("vector", wqo[b][:], scri, swpp[:, 1:2], None, ALU.mult, inc="p4q4")
        k.wait(q, "p4q4", v4)
        k.dma(q, wq_dram[:, ch * width:(ch + 1) * width], wqo[b][:], f"wq{q}o{b}")
    for b2 in range(2):
        k.wait(q, f"wq{q}o{b2}", k.val(f"wq{q}o{b2}"))
    for eng in ("sync", "gpsimd", "scalar", "tensor", "vector"):
        k.wait(eng, "p4q4", k.val("p4q4"))
    k.pop_scope()


def emit_phase4(k, io):
    MAGIC = 12582912.0
    ones1 = k.t["ones1_f"]
    iden = k.t["iden_f"]
    idenb = k.t["iden_b"]
    k.push_scope()
    kq1 = k.sb("p4kq1", [128, 2], F32)
    k.memset("vector", kq1[:], 1.0 / 127.0, inc="p4kq")
    k.wait("vector", "p4kq", k.val("p4kq"))
    if not io.get("wquant_done"):
        emit_wquant(k, io, "w1T", io["w1qd"], DCH, DH)
        emit_wquant(k, io, "w2T", io["w2qd"], 32, 1024)
    w1q = k.sb("w1q", [128, DCH * DH], BF16)
    w2q = k.sb("w2q", [128, 32 * 1024], BF16)
    k.dma("sync", w1q[:], io["w1qd"], "p4wl")
    k.dma("sync", w2q[:], io["w2qd"], "p4wl")
    for eng in ("tensor", "vector", "gpsimd", "scalar"):
        k.wait(eng, "p4wl", k.val("p4wl"))

    if "w1q" in P4DBG:
        k.wait("sync", "p4q4", k.val("p4q4"))
        k.dma("sync", io["dbg_w1q"], w1q[:], "p4dbga")
        k.wait("gpsimd", "p4dbga", k.val("p4dbga"))
    cw = k.sb("cw4", [128, 32 * 3], F32)
    cb4 = k.sb("cb4", [128, 32], F32)
    k.dma("sync", cw[:], io["cw"].rearrange("p c t -> p (c t)"), "p4b")
    k.dma("sync", cb4[:], io["cb"], "p4b")
    for eng in ("tensor", "vector", "scalar", "gpsimd"):
        k.wait(eng, "p4b", k.val("p4b"))

    TW = BLK
    OW = 256
    QW4 = 8 * OW
    xc = k.sb("p4x", [128, DCH * TW], F32)
    cmr = k.sb("p4cm", [1, TW], F32)
    xq = k.sb("p4xq", [128, DCH * TW], BF16)
    hbuf = k.sb("p4h", [128, 16 * TW], BF16)
    tA = k.sb("p4tA", [128, QW4], F32)
    tB = k.sb("p4tB", [128, QW4], F32)
    cbuf = k.sb("p4c", [128, 32 * OW], BF16)
    y4n = k.sb("p4yn", [128, DCH * OW], F32)
    rowq = k.sb("p4rq", [1, 2 * TW], F32)
    rowq2 = k.sb("p4rq2", [1, 2 * OW], F32)
    cmt = k.sb("p4cmt", [128, 8], F32)
    rb1 = k.sb("p4rb1", [128, 2 * TW], F32)
    rb2 = k.sb("p4rb2", [128, 2 * OW], F32)
    rb3 = k.sb("p4rb3", [128, 2 * OW], F32)
    ps_mm = [k.ps(f"p4pm{j}", [128, TW], F32) for j in range(2)]
    ps_t = k.ps("p4pt", [128, 1024], BF16)
    ps_bc = [k.ps(f"p4pb{j}", [128, TW], F32) for j in range(2)]
    k.memset("vector", cmt[:], 0.0)

    for t in range(JB):
        a = t * TW
        k.wait("sync", "p4y", k.val("p4y"))
        k.wait("sync", "p4qm", k.val("p4qm"))
        k.wait("sync", "p4cst", k.val("p4cst"))
        vx = k.dma("sync", xc[:].rearrange("p (c w) -> p c w", c=DCH),
                   io["xstate"].rearrange("p (c n) -> p c n", c=DCH)[:, :, a:a + TW], "p4xd")
        vcm = k.dma("sync", cmr[:], io["cmrow"][0:1, a:a + TW], "p4cmd")
        engs = ("scalar", "vector", "gpsimd")
        for eng in engs:
            k.wait(eng, "p4xd", vx)
            k.wait(eng, "p4hm", k.val("p4hm"))
            k.wait(eng, "ns_qs", k.val("ns_qs"))
        for ch in range(DCH):
            k.cp(engs[ch % 3], xq[:, ch * TW:(ch + 1) * TW], xc[:, ch * TW:(ch + 1) * TW],
                 inc="p4cst")
        k.wait("tensor", "p4cst", k.val("p4cst"))
        pieces = []
        off, pi = 0, 0
        while off < TW:
            w = min(128, TW - off)
            pieces.append((off, w, pi))
            k.wait("tensor", "p4red", k.val("p4red"))
            for ch in range(DCH):
                k.tr(ps_t[0:w, ch * 128:(ch + 1) * 128],
                     xq[:, ch * TW + off: ch * TW + off + w], idenb[:],
                     inc=("p4tr" if ch == DCH - 1 else None))
            k.wait("vector", "p4tr", k.val("p4tr"))
            k.red("vector", cmt[0:w, pi:pi + 1], ps_t[0:w, :], ALU.max, absval=True,
                  inc="p4red")
            off += w
            pi += 1
        npiece = pi
        k.wait("vector", "p4red", k.val("p4red"))
        k.tsc("vector", cmt[:, 4:4 + npiece], cmt[:, 0:npiece], 1e-5, None, ALU.max,
              inc="p4sc")
        k.wait("vector", "p4sc", k.val("p4sc"))
        k.recip(cmt[:, 0:npiece], cmt[:, 4:4 + npiece], inc="p4sc")
        k.wait("vector", "p4sc", k.val("p4sc"))
        k.tsc("vector", cmt[:, 0:npiece], cmt[:, 0:npiece], 127.0, None, ALU.mult,
              inc="p4sc")
        k.tsc("vector", cmt[:, 4:4 + npiece], cmt[:, 4:4 + npiece], kq1[:, 0:1], None,
              ALU.mult, inc="p4sc")
        k.wait("tensor", "p4sc", k.val("p4sc"))
        k.wait("tensor", "p4rbe", k.val("p4rbe"))
        for (off, w, pi2) in pieces:
            k.tr(ps_bc[0][0:1, off:off + w], cmt[0:w, pi2:pi2 + 1], iden[0:w, 0:w])
            k.tr(ps_bc[1][0:1, off:off + w], cmt[0:w, 4 + pi2:5 + pi2], iden[0:w, 0:w],
                 inc=("p4trr" if off + w >= TW else None))
        k.wait("scalar", "p4trr", k.val("p4trr"))
        k.wait("scalar", "p4bc", k.val("p4bc"))
        k.cp("scalar", rowq[0:1, 0:TW], ps_bc[0][0:1, 0:TW], inc="p4row")
        k.cp("scalar", rowq[0:1, TW:2 * TW], ps_bc[1][0:1, 0:TW], inc="p4row")
        k.wait("vector", "p4row", k.val("p4row"))
        k.wait("vector", "p4cmd", vcm)
        k.tt("vector", rowq[0:1, 0:TW], rowq[0:1, 0:TW], cmr[0:1, 0:TW], ALU.mult,
             inc="p4rowm")
        k.wait("tensor", "p4rowm", k.val("p4rowm"))
        k.mm(ps_bc[0][:], ones1[:], rowq[0:1, 0:TW], start=True, stop=True)
        k.mm(ps_bc[1][:], ones1[:], rowq[0:1, TW:2 * TW], start=True, stop=True, inc="p4bc")
        k.wait("scalar", "p4bc", k.val("p4bc"))
        k.wait("scalar", "p4qm", k.val("p4qm"))
        k.wait("scalar", "p4he", k.val("p4he"))
        k.cp("scalar", rb1[:, 0:TW], ps_bc[0][:], inc="p4rbe")
        k.cp("scalar", rb1[:, TW:2 * TW], ps_bc[1][:], inc="p4rbe")
        vrbe = k.val("p4rbe")
        k.wait("gpsimd", "p4rbe", vrbe)
        k.wait("vector", "p4rbe", vrbe)
        if t == 0 and "rb1" in P4DBG:
            k.wait("sync", "p4rbe", vrbe)
            k.dma("sync", io["dbg_rb1"], rb1[:], "p4dbgg")
            k.wait("gpsimd", "p4dbgg", k.val("p4dbgg"))
        for ch in range(DCH):
            s = ch % 2
            sl = tA[:, s * TW:(s + 1) * TW]
            k.wait("gpsimd", f"p4qa{s}", k.val(f"p4qa{s}"))
            if ch < 2:
                k.wait("gpsimd", "p4cv", k.val("p4cv"))
                k.wait("gpsimd", "p4qm2", k.val("p4qm2"))
            vq1 = k.tt("gpsimd", sl, xc[:, ch * TW:(ch + 1) * TW], rb1[:, 0:TW], ALU.mult,
                       inc="p4qm")
            k.wait("vector", "p4qm", vq1)
            vq2 = k.tsc("vector", sl, sl, MAGIC, None, ALU.add, inc=f"p4qa{s}")
            k.wait("vector", f"p4qa{s}", vq2)
            k.tsc("vector", xq[:, ch * TW:(ch + 1) * TW], sl, MAGIC, None, ALU.subtract,
                  inc=f"p4qa{s}")
        for s in range(2):
            k.wait("tensor", f"p4qa{s}", k.val(f"p4qa{s}"))
        if t == 0 and "xq" in P4DBG:
            k.wait("sync", "p4qa0", k.val("p4qa0"))
            k.wait("sync", "p4qa1", k.val("p4qa1"))
            k.dma("sync", io["dbg_xq"], xq[:], "p4dbgb")
            k.wait("gpsimd", "p4dbgb", k.val("p4dbgb"))
        k.wait("tensor", "p4oe", k.val("p4oe"))
        for hh in range(2):
            for hc16 in range(16):
                hc = hh * 16 + hc16
                pm = ps_mm[hc16 % 2]
                if hc16 >= 2 or hh > 0 or t > 0:
                    k.wait("tensor", "p4he", k.val("p4he") - 1)
                for ch in range(DCH):
                    k.mm(pm[:], w1q[:, ch * DH + hc * 128: ch * DH + (hc + 1) * 128],
                         xq[:, ch * TW:(ch + 1) * TW], start=(ch == 0), stop=(ch == DCH - 1),
                         inc=("p4hm" if ch == DCH - 1 else None))
                k.wait("vector", "p4hm", k.val("p4hm"))
                if hc16 < 2:
                    k.wait("vector", "p4cv", k.val("p4cv"))
                k.tt("vector", hbuf[:, hc16 * TW:(hc16 + 1) * TW], pm[:],
                     rb1[:, TW:2 * TW], ALU.mult, inc="p4he")
            vhe = k.val("p4he")
            if t == 0 and hh == 0 and "h" in P4DBG:
                k.wait("sync", "p4he", vhe)
                k.dma("sync", io["dbg_h"], hbuf[:], "p4dbgc")
                k.wait("gpsimd", "p4dbgc", k.val("p4dbgc"))
                k.wait("vector", "p4dbgc", k.val("p4dbgc"))
            for qq in range(2):
                q0 = qq * 8
                h3 = hbuf[:].rearrange("p (c w) -> p c w", c=16)[:, q0:q0 + 8, :]
                cwr = cw[:].rearrange("p (c t) -> p c t", c=32)
                cw3 = lambda tap: cwr[:, hh * 16 + q0:hh * 16 + q0 + 8, tap:tap + 1] \
                    .broadcast_to([128, 8, OW])
                cb3 = cb4[:, hh * 16 + q0:hh * 16 + q0 + 8].unsqueeze(2) \
                    .broadcast_to([128, 8, OW])
                tA3 = tA[:].rearrange("p (c w) -> p c w", c=8)
                tB3 = tB[:].rearrange("p (c w) -> p c w", c=8)
                for eng in ("vector", "gpsimd"):
                    k.wait(eng, "p4he", vhe)
                    k.wait(eng, "p4cq", k.val("p4cq"))
                    k.wait(eng, "p4qa0", k.val("p4qa0"))
                    k.wait(eng, "p4qa1", k.val("p4qa1"))
                    k.wait(eng, "p4yb", k.val("p4yb"))
                v1 = k.tt("vector", tA3, h3[:, :, 0:OW], cw3(0), ALU.mult, inc="p4cv")
                v2 = k.tt("gpsimd", tB3, h3[:, :, 2:2 + OW], cw3(1), ALU.mult, inc="p4cv")
                k.wait("vector", "p4cv", v2)
                v3 = k.tt("vector", tA3, tA3, tB3, ALU.add, inc="p4cv")
                k.wait("gpsimd", "p4cv", v3)
                v4 = k.tt("gpsimd", tB3, h3[:, :, 4:4 + OW], cw3(2), ALU.mult, inc="p4cv")
                k.wait("vector", "p4cv", v4)
                v5 = k.tt("vector", tA3, tA3, tB3, ALU.add, inc="p4cv")
                k.wait("gpsimd", "p4cv", v5)
                v6 = k.tt("gpsimd", tA3, tA3, cb3, ALU.add, inc="p4cv")
                k.wait("scalar", "p4cv", v6)
                k.wait("scalar", "p4hqw", k.val("p4hqw"))
                k.act(cbuf[:, (hh * 16 + q0) * OW:(hh * 16 + q0 + 8) * OW], tA[:], AF.Silu,
                      inc="p4cq")
        vcq = k.val("p4cq")
        if t == 0 and "c" in P4DBG:
            k.wait("sync", "p4cq", vcq)
            k.dma("sync", io["dbg_c"], cbuf[:], "p4dbgd")
            k.wait("gpsimd", "p4dbgd", k.val("p4dbgd"))
            k.wait("vector", "p4dbgd", k.val("p4dbgd"))
        k.wait("tensor", "p4cq", vcq)
        for cp2 in range(2):
            k.wait("vector", "p4sc", k.val("p4sc"))
            for qq in range(4):
                k.wait("tensor", "p4red", k.val("p4red"))
                for ci in range(8):
                    ch = 8 * qq + ci
                    k.tr(ps_t[:, ci * 128:(ci + 1) * 128],
                         cbuf[:, ch * OW + cp2 * 128: ch * OW + cp2 * 128 + 128], idenb[:],
                         inc=("p4tr" if ci == 7 else None))
                k.wait("vector", "p4tr", k.val("p4tr"))
                k.red("vector", cmt[:, qq:qq + 1], ps_t[:], ALU.max, absval=True,
                      inc="p4red")
            k.wait("vector", "p4red", k.val("p4red"))
            k.tt("vector", cmt[:, 0:1], cmt[:, 0:1], cmt[:, 1:2], ALU.max, inc="p4sc")
            k.tt("vector", cmt[:, 2:3], cmt[:, 2:3], cmt[:, 3:4], ALU.max, inc="p4sc")
            k.wait("vector", "p4sc", k.val("p4sc"))
            k.tt("vector", cmt[:, 4 + cp2:5 + cp2], cmt[:, 0:1], cmt[:, 2:3], ALU.max,
                 inc="p4sc")
        k.wait("vector", "p4sc", k.val("p4sc"))
        k.tsc("vector", cmt[:, 6:8], cmt[:, 4:6], 1e-5, None, ALU.max, inc="p4sc")
        k.wait("vector", "p4sc", k.val("p4sc"))
        k.recip(cmt[:, 4:6], cmt[:, 6:8], inc="p4sc")
        k.wait("vector", "p4sc", k.val("p4sc"))
        k.tsc("vector", cmt[:, 4:6], cmt[:, 4:6], 127.0, None, ALU.mult, inc="p4sc")
        k.tsc("vector", cmt[:, 6:8], cmt[:, 6:8], kq1[:, 1:2], None, ALU.mult, inc="p4sc")
        k.wait("tensor", "p4sc", k.val("p4sc"))
        k.wait("tensor", "p4rbe2", k.val("p4rbe2"))
        for cp2 in range(2):
            k.tr(ps_bc[0][0:1, cp2 * 128:(cp2 + 1) * 128], cmt[:, 4 + cp2:5 + cp2], iden[:])
            k.tr(ps_bc[1][0:1, cp2 * 128:(cp2 + 1) * 128], cmt[:, 6 + cp2:7 + cp2], iden[:],
                 inc=("p4trr2" if cp2 == 1 else None))
        k.wait("scalar", "p4trr2", k.val("p4trr2"))
        k.wait("scalar", "p4bc2", k.val("p4bc2"))
        k.cp("scalar", rowq2[0:1, 0:OW], ps_bc[0][0:1, 0:OW], inc="p4row2")
        k.cp("scalar", rowq2[0:1, OW:2 * OW], ps_bc[1][0:1, 0:OW], inc="p4row2")
        k.wait("tensor", "p4row2", k.val("p4row2"))
        k.mm(ps_bc[0][:, 0:OW], ones1[:], rowq2[0:1, 0:OW], start=True, stop=True)
        k.mm(ps_bc[1][:, 0:OW], ones1[:], rowq2[0:1, OW:2 * OW], start=True, stop=True,
             inc="p4bc2")
        k.wait("scalar", "p4bc2", k.val("p4bc2"))
        k.wait("scalar", "p4qm2", k.val("p4qm2"))
        k.wait("scalar", "p4oe", k.val("p4oe"))
        k.cp("scalar", rb2[:, 0:OW], ps_bc[0][:, 0:OW], inc="p4rbe2")
        k.cp("scalar", rb2[:, OW:2 * OW], ps_bc[1][:, 0:OW], inc="p4rbe2")
        vrbe2 = k.val("p4rbe2")
        k.wait("gpsimd", "p4rbe2", vrbe2)
        k.wait("vector", "p4rbe2", vrbe2)
        for qq in range(4):
            q0 = qq * 8
            r4b = rb2[:, 0:OW].unsqueeze(1).broadcast_to([128, 8, OW])
            c3 = cbuf[:].rearrange("p (c w) -> p c w", c=32)[:, q0:q0 + 8, :]
            tA3 = tA[:].rearrange("p (c w) -> p c w", c=8)
            k.wait("gpsimd", "p4cq", vcq)
            k.wait("gpsimd", "p4hqw", k.val("p4hqw"))
            vq1 = k.tt("gpsimd", tA3, c3, r4b, ALU.mult, inc="p4qm2")
            k.wait("vector", "p4qm2", vq1)
            vq2 = k.tsc("vector", tA[:], tA[:], MAGIC, None, ALU.add, inc="p4hqw")
            k.wait("vector", "p4hqw", vq2)
            k.tsc("vector", c3, tA3, MAGIC, None, ALU.subtract, inc="p4hqw")
        k.wait("tensor", "p4hqw", k.val("p4hqw"))
        if t == 0 and "hq" in P4DBG:
            k.wait("sync", "p4hqw", k.val("p4hqw"))
            k.dma("sync", io["dbg_hq"], cbuf[:], "p4dbge")
            k.wait("gpsimd", "p4dbge", k.val("p4dbge"))
            k.wait("tensor", "p4dbge", k.val("p4dbge"))
        for oc in range(DCH):
            pm = ps_mm[oc % 2]
            if oc >= 2:
                k.wait("tensor", "p4oe", k.val("p4oe") - 1)
            for hc in range(32):
                k.mm(pm[:, 0:OW], w2q[:, hc * 1024 + oc * 128: hc * 1024 + (oc + 1) * 128],
                     cbuf[:, hc * OW:(hc + 1) * OW], start=(hc == 0), stop=(hc == 31),
                     inc=("p4om" if hc == 31 else None))
            k.wait("vector", "p4om", k.val("p4om"))
            k.wait("vector", "p4yb", k.val("p4yb"))
            v1 = k.tt("vector", tB[:, 0:OW], pm[:, 0:OW], rb2[:, OW:2 * OW], ALU.mult,
                      inc="p4oe")
            k.wait("gpsimd", "p4oe", v1)
            k.wait("gpsimd", "p4fa", k.val("p4fa"))
            k.wait("gpsimd", "p4out", k.val("p4out"))
            k.tt("gpsimd",
                 y4n[:].rearrange("p (c n w) -> p c n w", c=DCH, n=2)[:, oc],
                 tB[:, 0:OW].rearrange("p (w n) -> p n w", n=2),
                 xc[:, oc * TW + 2: oc * TW + 2 + OW].rearrange("p (w n) -> p n w", n=2),
                 ALU.add, inc="p4y")
            k.wait("vector", "p4y", k.val("p4y"))
            k.nopi("vector", "p4yb")
        vy = k.val("p4y")
        if t == 0 and "y" in P4DBG:
            k.wait("sync", "p4y", vy)
            k.dma("sync", io["dbg_y"], y4n[:], "p4dbgf")
            k.wait("gpsimd", "p4dbgf", k.val("p4dbgf"))
            k.wait("scalar", "p4dbgf", k.val("p4dbgf"))
            k.wait("vector", "p4dbgf", k.val("p4dbgf"))
        k.wait("tensor", "p4y", vy)
        engs = ("scalar", "vector", "gpsimd")
        for eng in engs:
            k.wait(eng, "p4y", vy)
        for ch in range(DCH):
            k.cp(engs[ch % 3], xq[:, ch * OW:(ch + 1) * OW], y4n[:, ch * OW:(ch + 1) * OW],
                 inc="p4yc")
        vyc = k.val("p4yc")
        k.wait("scalar", "p4yc", vyc)
        k.wait("tensor", "p4yc", vyc)
        k.push_scope()
        arow, brow, ssem, sval = emit_norm_stats(
            k, f"p4s{t}", lambda ch, aa, bb: xq[:, ch * OW + aa: ch * OW + bb], OW, 1, OW)
        k.wait("tensor", ssem, sval)
        k.pop_scope()
        k.wait("tensor", "p4rbe3", k.val("p4rbe3"))
        k.mm(ps_bc[0][:, 0:OW], ones1[:], arow[0:1, :], start=True, stop=True)
        k.mm(ps_bc[1][:, 0:OW], ones1[:], brow[0:1, :], start=True, stop=True, inc="p4bc3")
        k.wait("scalar", "p4bc3", k.val("p4bc3"))
        k.wait("scalar", "p4fm", k.val("p4fm"))
        k.wait("scalar", "p4fa", k.val("p4fa"))
        k.cp("scalar", rb3[:, 0:OW], ps_bc[0][:, 0:OW], inc="p4rbe3")
        k.cp("scalar", rb3[:, OW:2 * OW], ps_bc[1][:, 0:OW], inc="p4rbe3")
        vab = k.val("p4rbe3")
        for eng in ("vector", "gpsimd"):
            k.wait(eng, "p4rbe3", vab)
        for ch in range(DCH):
            ya = y4n[:, ch * OW:(ch + 1) * OW]
            vm = k.tt("gpsimd", ya, ya, rb3[:, 0:OW], ALU.mult, inc="p4fm")
            k.wait("vector", "p4fm", vm)
            k.tt("vector", ya, ya, rb3[:, OW:2 * OW], ALU.add, inc="p4fa")
        vfn = k.val("p4fa")
        k.wait("sync", "p4fa", vfn)
        k.dma("sync",
              io["yout"].rearrange("p (c n j w) -> p (c n) j w", c=DCH, n=2, j=JB, w=128)[:, :, t, :],
              y4n[:].rearrange("p (cn w) -> p cn w", w=128), "p4out")
    k.wait("sync", "p4out", k.val("p4out"))
    k.pop_scope()
    return "p4out", k.val("p4out")


def declare_io_phase4(nc, standalone):
    io = {}
    add = lambda n, s, d, kind: io.__setitem__(n, nc.dram_tensor(n + _u(), s, d, kind=kind).ap())
    for nm, shp, dt in (("dbg_w1q", [128, DCH * DH], BF16), ("dbg_xq", [128, DCH * BLK], BF16),
                        ("dbg_rb1", [128, 2 * BLK], F32),
                        ("dbg_h", [128, 16 * BLK], BF16), ("dbg_c", [128, 32 * 256], BF16),
                        ("dbg_hq", [128, 32 * 256], BF16), ("dbg_y", [128, DCH * 256], F32)):
        if nm[4:] in P4DBG:
            add(nm, shp, dt, "ExternalOutput")
    add("w1T", [1024, DH], F32, "ExternalInput")
    add("w2T", [DH, 1024], F32, "ExternalInput")
    add("cw", [128, 32, 3], F32, "ExternalInput")
    add("cb", [128, 32], F32, "ExternalInput")
    add("cmrow", [1, NCOL], F32, "ExternalInput")
    add("ones1", [1, 128], F32, "ExternalInput")
    add("onesn", [128, 1], F32, "ExternalInput")
    add("iden", [128, 128], F32, "ExternalInput")
    add("yout", [128, DCH * NB * JB * 128], F32, "ExternalOutput")
    if standalone:
        add("xstate", [128, DCH * NCOL], F32, "ExternalInput")
    return io


def build_phase4_standalone():
    stack = ExitStack()
    nc = bass.Bass("TRN2", target_bir_lowering=False)
    io = declare_io_phase4(nc, True)
    io["w1qd"] = nc.dram_tensor("w1qd" + _u(), [128, DCH * DH], BF16).ap()
    io["w2qd"] = nc.dram_tensor("w2qd" + _u(), [128, 32 * 1024], BF16).ap()
    with stack:
        k = K(nc, stack)
        alloc_common(k, io, stream=False)
        sem, val = emit_phase4(k, io)
        k.wait("sync", sem, val)
        k.finalize()
    return nc, io_names(io)


def phase4_in_map(P, Q, xstate):
    return {
        "w1T": P["w1T"], "w2T": P["w2T"], "cw": P["cw"], "cb": P["cb"], "cmrow": Q["cmrow"],
        "ones1": P["ones1"], "onesn": P["onesn"], "iden": P["iden"], "xstate": xstate,
    }



_BUILD_CACHE = {}


def _built(name, fn):
    if name not in _BUILD_CACHE:
        _BUILD_CACHE[name] = fn()
    return _BUILD_CACHE[name]


def host_assemble_yout(outs):
    y = np.empty((1024, 2, TSEQ), dtype=np.float32)
    for c in range(NCORES):
        o = np.asarray(outs[c], np.float32).reshape(128, DCH, 2, JB, 128)
        o = o.transpose(1, 0, 2, 3, 4).reshape(1024, 2, JB, 128)
        for j in range(JB):
            y[:, :, 1024 * j + 128 * c:1024 * j + 128 * c + 128] = o[:, :, j, :]
    return np.ascontiguousarray(y.transpose(1, 2, 0))


def kernel_unfused(**inputs):
    P = host_prep(inputs)
    Qs = [host_core_inputs(P, c) for c in range(NCORES)]

    nc1, n1 = _built("p1", build_phase1_standalone)
    maps = [{n1[kk]: v for kk, v in phase1_in_map(P, Qs[c]).items()} for c in range(NCORES)]
    r1 = run_bass_kernel_spmd(nc1, maps, core_ids=list(range(NCORES)))
    xs = [np.asarray(r1.results[c][n1["xout"]]) for c in range(NCORES)]

    nc2, n2 = _built("p2", build_phase2_standalone)
    maps = [{n2[kk]: v for kk, v in phase2_in_map(P, xs[c]).items()} for c in range(NCORES)]
    r2 = run_bass_kernel_spmd(nc2, maps, core_ids=list(range(NCORES)))
    xs = [np.asarray(r2.results[c][n2["xout"]]) for c in range(NCORES)]

    nc3, n3 = _built("p3", build_phase3_standalone)
    maps = [{n3[kk]: v for kk, v in phase3_in_map(P, xs[c]).items()} for c in range(NCORES)]
    r3 = run_bass_kernel_spmd(nc3, maps, core_ids=list(range(NCORES)))
    xs = [np.asarray(r3.results[c][n3["xout"]]) for c in range(NCORES)]

    nc4, n4 = _built("p4", build_phase4_standalone)
    maps = [{n4[kk]: v for kk, v in phase4_in_map(P, Qs[c], xs[c]).items()} for c in range(NCORES)]
    r4 = run_bass_kernel_spmd(nc4, maps, core_ids=list(range(NCORES)))
    return host_assemble_yout([r4.results[c][n4["yout"]] for c in range(NCORES)])



def declare_io_fused(nc):
    io = {}
    add = lambda n, s, d, kind: io.__setitem__(n, nc.dram_tensor(n + _u(), s, d, kind=kind).ap())
    add("xin", [1024, NCOL], F32, "ExternalInput")
    add("xT", [1024, NB, TSEQ], F32, "ExternalInput")
    add("qkvT", [1024, 192], F32, "ExternalInput")
    add("bq", [64, 1], F32, "ExternalInput")
    add("bkv", [128, 1], F32, "ExternalInput")
    add("projT", [1024, 1024], F32, "ExternalInput")
    add("proj_b", [128, 8], F32, "ExternalInput")
    add("efold", [1024, 64], F32, "ExternalInput")
    add("mask", [128, 8 * CW], F32, "ExternalInput")
    add("ones1", [1, 128], F32, "ExternalInput")
    add("onesn", [128, 1], F32, "ExternalInput")
    add("iden", [128, 128], F32, "ExternalInput")
    add("qTw", [6, 1024, 1024], I32, "ExternalInput")
    add("sTe", [6, 1024, 1024], F32, "ExternalInput")
    add("biasA", [128, 48], F32, "ExternalInput")
    add("laT", [6, 1024, 32], F32, "ExternalInput")
    add("lbT", [32, 6 * 1024], F32, "ExternalInput")
    add("gamma", [128, 24], F32, "ExternalInput")
    add("beta", [128, 24], F32, "ExternalInput")
    add("WuT", [1024, 2048], F32, "ExternalInput")
    add("WvT", [1024, 2048], F32, "ExternalInput")
    add("WoT", [2048, 1024], F32, "ExternalInput")
    add("bu", [128, 16], F32, "ExternalInput")
    add("bv", [128, 16], F32, "ExternalInput")
    add("bo", [128, 8], F32, "ExternalInput")
    add("ze", [128, 16], F32, "ExternalInput")
    add("pe", [128, 16], F32, "ExternalInput")
    add("w1T", [1024, DH], F32, "ExternalInput")
    add("w2T", [DH, 1024], F32, "ExternalInput")
    add("cw", [128, 32, 3], F32, "ExternalInput")
    add("cb", [128, 32], F32, "ExternalInput")
    add("cmrow", [1, NCOL], F32, "ExternalInput")
    add("yout", [128, DCH * NB * JB * 128], F32, "ExternalOutput")
    io["xspill"] = nc.dram_tensor("xspill" + _u(), [128, DCH * NCOL], F32).ap()
    io["w1qd"] = nc.dram_tensor("w1qd" + _u(), [128, DCH * DH], BF16).ap()
    io["w2qd"] = nc.dram_tensor("w2qd" + _u(), [128, 32 * 1024], BF16).ap()
    return io


def _phase_gate(k, sem, val):
    for eng in K.ENGINES:
        k.wait(eng, sem, val)


def build_fused():
    stack = ExitStack()
    nc = bass.Bass("TRN2", target_bir_lowering=False)
    io = declare_io_fused(nc)
    with stack:
        k = K(nc, stack)
        alloc_common(k, io, stream=False)
        k.push_scope()
        k.t["X"] = k.sb("X", [128, DCH * NCOL], F32)
        k.t["XB"] = k.sb("XB", [128, DCH * NCOL], BF16)
        emit_load_stream(k, io)
        s1, v1 = emit_phase1(k, io)
        _phase_gate(k, s1, v1)
        k.release_sems(("p1", "ldx"))
        k.dma("sync", io["xspill"], k.t["X"][:], "fsp1")
        k.wait("sync", "fsp1", k.val("fsp1"))
        io["x1ref"] = io["xspill"]
        s2, v2 = emit_phase2(k, io)
        for s in ("p2ra0", "p2ra1", "pl_nc", "pl_nm", "pl_na", "pl_abe", "ns_qs",
                  "ns_rbe", "ns_rb", "ns_tm", "ns_rows"):
            k.wait("sync", s, k.val(s))
            k.wait("vector", s, k.val(s))
            k.wait("gpsimd", s, k.val(s))
        emit_wquant(k, io, "w1T", io["w1qd"], DCH, DH, q="sync")
        emit_wquant(k, io, "w2T", io["w2qd"], 32, 1024, q="sync")
        io["wquant_done"] = True
        _phase_gate(k, s2, v2)
        k.release_sems(("p2", "dq_", "fsp1"))
        s3, v3 = emit_phase3(k, io)
        _phase_gate(k, s3, v3)
        k.release_sems(("p3",))
        k.dma("sync", io["xspill"], k.t["X"][:], "fsp2")
        k.wait("sync", "fsp2", k.val("fsp2"))
        _phase_gate(k, "fsp2", k.val("fsp2"))
        k.pop_scope()
        io["xstate"] = io["xspill"]
        s4, v4 = emit_phase4(k, io)
        k.wait("sync", s4, v4)
        k.finalize()
    return nc, io_names(io)


def fused_in_map(P, Q):
    return {
        "xin": Q["xin"], "xT": P["xT"], "qkvT": P["qkvT"], "bq": P["bq"], "bkv": P["bkv"],
        "projT": P["projT"], "proj_b": P["proj_b"], "efold": P["efold"], "mask": Q["mask"],
        "ones1": P["ones1"], "onesn": P["onesn"], "iden": P["iden"],
        "qTw": P["qTw"], "sTe": P["sTe"], "biasA": P["biasA"], "laT": P["laT"],
        "lbT": P["lbT2"], "gamma": P["gamma"], "beta": P["beta"],
        "WuT": P["WuT"], "WvT": P["WvT"], "WoT": P["WoT"], "bu": P["bu"], "bv": P["bv"],
        "bo": P["bo"], "ze": P["ze"], "pe": P["pe"],
        "w1T": P["w1T"], "w2T": P["w2T"], "cw": P["cw"], "cb": P["cb"], "cmrow": Q["cmrow"],
    }


FUSED = True


def kernel_fused(**inputs):
    P = host_prep(inputs)
    Qs = [host_core_inputs(P, c) for c in range(NCORES)]
    nc, nm = _built("fused", build_fused)
    maps = [{nm[kk]: v for kk, v in fused_in_map(P, Qs[c]).items()} for c in range(NCORES)]
    r = run_bass_kernel_spmd(nc, maps, core_ids=list(range(NCORES)))
    return host_assemble_yout([r.results[c][nm["yout"]] for c in range(NCORES)])


def kernel(**inputs):
    if FUSED:
        return kernel_fused(**inputs)
    return kernel_unfused(**inputs)
